# version 02
# baseline (speedup 1.0000x reference)
import numpy as np
import ml_dtypes
from contextlib import ExitStack
import concourse.bass as bass
import concourse.mybir as mybir
from concourse.alu_op_type import AluOpType as A
from concourse.bass_utils import run_bass_kernel_spmd

F32 = mybir.dt.float32
BF16 = mybir.dt.bfloat16
I32 = mybir.dt.int32
AF = mybir.ActivationFunctionType
AX = mybir.AxisListType

T = 2048
D = 1024
NC = 16
FC = 8
DH = 4096
BF = ml_dtypes.bfloat16


class Sem:
    def __init__(self, nc, name):
        self.h = nc.alloc_semaphore(name)
        self.n = 0

    def inc(self, instr, k=1):
        instr.then_inc(self.h, k)
        self.n += k
        return instr

    def wait(self, eng, n=None):
        eng.wait_ge(self.h, self.n if n is None else n)


class K:
    def __init__(self, nc):
        self.nc = nc
        self.sems = {}
        self.inp = {}

    def sem(self, name):
        if name not in self.sems:
            self.sems[name] = Sem(self.nc, name)
        return self.sems[name]

    def chainer(self, name):
        s = self.sem(name)
        def ch(eng, instr):
            s.inc(instr)
            eng.wait_ge(s.h, s.n)
            return instr
        return ch


def make_consts(k):
    nc = k.nc
    k.ident = nc.alloc_sbuf_tensor("c_ident", [128, 128], BF16)
    k.identf = nc.alloc_sbuf_tensor("c_identf", [128, 128], F32)
    k.ones_col = nc.alloc_sbuf_tensor("c_ones_col", [128, 1], BF16)
    k.ones_colf = nc.alloc_sbuf_tensor("c_ones_colf", [128, 1], F32)
    k.ones_row = nc.alloc_sbuf_tensor("c_ones_row", [1, 128], BF16)
    k.ones_rowf = nc.alloc_sbuf_tensor("c_ones_rowf", [1, 128], F32)
    k.eps_col = nc.alloc_sbuf_tensor("c_eps_col", [128, 1], F32)
    s = k.sem("consts")
    with nc.Block() as blk:
        @blk.gpsimd
        def _(g):
            g.iota(k.identf[:], pattern=[[1, 128]], base=0, channel_multiplier=-1,
                   allow_small_or_imprecise_dtypes=True)
            g.memset(k.ones_col[:], 1.0)
            g.memset(k.ones_row[:], 1.0)
            g.memset(k.ones_rowf[:], 1.0)
            g.memset(k.eps_col[:], 1e-5)
            s.inc(g.memset(k.ones_colf[:], 1.0))
        @blk.vector
        def _(v):
            s.wait(v)
            s.inc(v.tensor_scalar(k.ident[:], k.identf[:], 0.0, None, A.is_equal))
            s.inc(v.tensor_scalar(k.identf[:], k.identf[:], 0.0, None, A.is_equal))


def emit_xf(k, xb, xf, tag):
    nc = k.nc
    s_mm = k.sem(tag + "_mm")
    s_ev = k.sem(tag + "_ev")
    b_mm, b_ev = s_mm.n, s_ev.n
    with ExitStack() as es:
        ps = [es.enter_context(nc.psum_tensor(f"{tag}_ps{i}", [128, 512], F32)) for i in range(2)]
        with nc.Block() as blk:
            @blk.tensor
            def _(te):
                grp = 0
                for fc in range(FC):
                    for cb in range(4):
                        buf = ps[grp % 2]
                        if grp >= 2:
                            te.wait_ge(s_ev.h, b_ev + grp - 1)
                        for j in range(4):
                            c = cb * 4 + j
                            mm = te.matmul(buf[:, j * 128:(j + 1) * 128],
                                           xb[:, c, fc * 128:(fc + 1) * 128],
                                           k.ident[:], start=True, stop=True)
                            if j == 3:
                                s_mm.inc(mm)
                        grp += 1
            @blk.vector
            def _(v):
                grp = 0
                for fc in range(FC):
                    for cb in range(4):
                        buf = ps[grp % 2]
                        v.wait_ge(s_mm.h, b_mm + grp + 1)
                        s_ev.inc(v.tensor_copy(xf[:, fc, cb * 512:(cb + 1) * 512], buf[:]))
                        grp += 1


def emit_ln(k, xr, tag, y_dram=None, xb=None):
    nc = k.nc
    s_sum = k.sem(tag + "_sum")
    s_sq = k.sem(tag + "_sq")
    s_stat = k.sem(tag + "_stat")
    s_app = k.sem(tag + "_app")
    s_bf = k.sem(tag + "_bf")
    s_st = k.sem(tag + "_st")
    with ExitStack() as es:
        junk = es.enter_context(nc.sbuf_tensor(tag + "_junk", [128, 1024], BF16))
        ssum = es.enter_context(nc.sbuf_tensor(tag + "_ssum", [128, 16], F32))
        ssq = es.enter_context(nc.sbuf_tensor(tag + "_ssq", [128, 16], F32))
        mean = es.enter_context(nc.sbuf_tensor(tag + "_mean", [128, 16], F32))
        var = es.enter_context(nc.sbuf_tensor(tag + "_var", [128, 16], F32))
        rstd = es.enter_context(nc.sbuf_tensor(tag + "_rstd", [128, 16], F32))
        nm = es.enter_context(nc.sbuf_tensor(tag + "_nm", [128, 16], F32))
        with nc.Block() as blk:
            @blk.vector
            def _(v):
                for c in range(NC):
                    s_sum.inc(v.tensor_reduce(
                        ssum[:, c:c + 1].unsqueeze(2),
                        xr[:, c:c + 1, :], AX.X, A.add))
            @blk.scalar
            def _(sc):
                for c in range(NC):
                    s_sq.inc(sc.activation(junk[:], xr[:, c, :], AF.Square,
                                           accum_out=ssq[:, c:c + 1]))
            @blk.vector
            def _(v):
                s_sum.wait(v)
                s_sq.wait(v)
                v.tensor_scalar(mean[:], ssum[:], 1.0 / D, None, A.mult)
                v.tensor_scalar(var[:], ssq[:], 1.0 / D, None, A.mult)
                v.tensor_tensor(nm[:], mean[:], mean[:], A.mult)
                s_stat.inc(v.tensor_tensor(var[:], var[:], nm[:], A.subtract))
            @blk.scalar
            def _(sc):
                s_stat.wait(sc)
                s_stat.inc(sc.activation(var[:], var[:], AF.Sqrt, bias=k.eps_col[:]))
            @blk.vector
            def _(v):
                s_stat.wait(v)
                v.reciprocal(rstd[:], var[:])
                v.tensor_tensor(nm[:], mean[:], rstd[:], A.mult)
                s_stat.inc(v.tensor_scalar(nm[:], nm[:], -1.0, None, A.mult))
            @blk.scalar
            def _(sc):
                s_stat.wait(sc)
                for c in range(NC):
                    s_app.inc(sc.activation(xr[:, c, :], xr[:, c, :], AF.Identity,
                                            bias=nm[:, c:c + 1], scale=rstd[:, c:c + 1]))
            if xb is not None:
                @blk.vector
                def _(v):
                    s_app.wait(v)
                    for c in range(NC):
                        s_bf.inc(v.tensor_copy(xb[:, c, :], xr[:, c, :]))
            if y_dram is not None:
                @blk.sync
                def _(sy):
                    s_app.wait(sy)
                    yv = y_dram.rearrange("(c p) d -> p c d", p=128)
                    s_st.inc(sy.dma_start(yv, xr[:]), 16)
                    s_st.wait(sy)


def emit_l1ffn(k, xr_dram, xf, y_dram, dbg_upto=None, dbg_out=None):
    nc = k.nc
    w1t, w2t = k.inp["l1_w1t"], k.inp["l1_w2t"]
    h1t, h2t = k.inp["l1_h1t"], k.inp["l1_h2t"]
    w3t = k.inp["l1_w3t"]
    HT = T // 2
    NOC = DH // 128

    xrv = xr_dram.rearrange("(c p) d -> p c d", p=128)
    yv = y_dram.rearrange("(c p) d -> p c d", p=128)

    s_sc = k.sem("ff_sc")
    s_s3 = k.sem("ff_s3")

    with ExitStack() as es:
        Aact = es.enter_context(nc.sbuf_tensor("ff_Aact", [128, NOC, HT], BF16))
        Bact = es.enter_context(nc.sbuf_tensor("ff_Bact", [128, NOC, HT], BF16))
        wbig = es.enter_context(nc.sbuf_tensor("ff_wbig", [128, 8192], BF16))
        wbA = [wbig[:, s * 1024:(s + 1) * 1024].rearrange("p (ic o) -> p ic o", o=128)
               for s in range(4)]
        wbH = [wbig[:, par * 4096:(par + 1) * 4096].rearrange("p (ic o) -> p ic o", o=128)
               for par in range(2)]
        wb3 = [wbig[:, par * 1024:(par + 1) * 1024] for par in range(2)]
        scols = es.enter_context(nc.sbuf_tensor("ff_scols", [128, 4, 32], F32))
        s3rep = es.enter_context(nc.sbuf_tensor("ff_s3rep", [128, 1024], F32))
        silu1 = es.enter_context(nc.sbuf_tensor("ff_silu1", [128, 512, 4], BF16))
        xrsl = es.enter_context(nc.sbuf_tensor("ff_xrsl", [128, 2, 1024], F32))
        vch = es.enter_context(nc.sbuf_tensor("ff_vch", [128, 2, 1024], F32))
        junkv = es.enter_context(nc.sbuf_tensor("ff_junkv", [128, 1024], F32))
        junkc = es.enter_context(nc.sbuf_tensor("ff_junkc", [128, 1], F32))
        statarch = es.enter_context(nc.sbuf_tensor("ff_statarch", [128, 16, 8], F32))
        lst = es.enter_context(nc.sbuf_tensor("ff_lst", [128, 2, 8], F32))
        psA = [es.enter_context(nc.psum_tensor(f"ff_ps{i}", [128, 512], F32)) for i in range(8)]

        with nc.Block() as blk0:
            @blk0.sync
            def _(sy):
                s_sc.inc(sy.dma_start(scols[:, 0, :], k.inp["l1_s1pc"]), 16)
                s_sc.inc(sy.dma_start(scols[:, 1, :], k.inp["l1_s2pc"]), 16)
                s_sc.inc(sy.dma_start(scols[:, 2, :], k.inp["l1_sh1pc"]), 16)
                s_sc.inc(sy.dma_start(scols[:, 3, :], k.inp["l1_sh2pc"]), 16)
                s_sc.inc(sy.dma_start(s3rep[0:1, :], k.inp["l1_s3row"]), 16)
            @blk0.tensor
            def _(te):
                s_sc.wait(te)
                s_s3.inc(te.matmul(psA[0][:], k.ones_rowf[:], s3rep[0:1, 0:512],
                                   start=True, stop=True))
                s_s3.inc(te.matmul(psA[1][:], k.ones_rowf[:], s3rep[0:1, 512:1024],
                                   start=True, stop=True))
            @blk0.vector
            def _(v):
                s_s3.wait(v)
                v.tensor_copy(s3rep[:, 0:512], psA[0][:])
                s_s3.inc(v.tensor_copy(s3rep[:, 512:1024], psA[1][:]))

        halves = [0] if dbg_upto is not None and dbg_upto < 4 else [0, 1]
        for H in halves:
            nbase = H * HT
            s_wl = k.sem(f"ffA{H}_wl")
            s_mmA = k.sem(f"ffA{H}_mm")
            s_sil = k.sem(f"ffA{H}_sil")
            s_evA = k.sem(f"ffA{H}_ev")

            with nc.Block() as blk:
                @blk.gpsimd
                def _(g):
                    for oc in range(NOC):
                        par = oc % 2
                        for wi, wt in ((0, w1t), (1, w2t)):
                            if oc >= 2:
                                g.wait_ge(s_mmA.h, 4 * (oc - 2) + 2 * (wi + 1))
                            wv = wt.rearrange("(ic p) o -> p ic o", p=128)[:, :, oc * 128:(oc + 1) * 128]
                            s_wl.inc(g.dma_start(wbA[2 * par + wi], wv), 16)
                @blk.tensor
                def _(te):
                    for oc in range(NOC):
                        par = oc % 2
                        te.wait_ge(s_wl.h, (2 * oc + 2) * 16)
                        if oc >= 2:
                            te.wait_ge(s_sil.h, 2 * (oc - 2) + 2)
                            te.wait_ge(s_evA.h, 2 * (oc - 2) + 2)
                        for wi in range(2):
                            for nt in range(2):
                                bank = psA[4 * par + 2 * wi + nt]
                                for ic in range(FC):
                                    mm = te.matmul(bank[:], wbA[2 * par + wi][:, ic, :],
                                                   xf[:, ic, nbase + nt * 512: nbase + (nt + 1) * 512],
                                                   start=(ic == 0), stop=(ic == FC - 1))
                                    if ic == FC - 1:
                                        s_mmA.inc(mm)
                @blk.scalar
                def _(sc):
                    for oc in range(NOC):
                        par = oc % 2
                        for nt in range(2):
                            sc.wait_ge(s_mmA.h, 4 * oc + nt + 1)
                            if oc >= 2:
                                sc.wait_ge(s_evA.h, 2 * (oc - 2) + nt + 1)
                            s_sil.inc(sc.activation(silu1[:, :, 2 * par + nt],
                                                    psA[4 * par + nt][:], AF.Silu,
                                                    scale=scols[:, 0, oc:oc + 1]))
                @blk.vector
                def _(v):
                    for oc in range(NOC):
                        par = oc % 2
                        for nt in range(2):
                            v.wait_ge(s_mmA.h, 4 * oc + 2 + nt + 1)
                            v.wait_ge(s_sil.h, 2 * oc + nt + 1)
                            s_evA.inc(v.scalar_tensor_tensor(
                                Aact[:, oc, nt * 512:(nt + 1) * 512],
                                psA[4 * par + 2 + nt][:], scols[:, 1, oc:oc + 1],
                                silu1[:, :, 2 * par + nt], A.mult, A.mult))

            if dbg_upto == 0:
                sdbg = k.sem("dbg_st")
                with nc.Block() as blkd:
                    @blkd.sync
                    def _(sy):
                        sdbg.inc(sy.dma_start(dbg_out, Aact[:]), 16)
                        sdbg.wait(sy)
                return
            for stage in range(2):
                ht = h1t if stage == 0 else h2t
                src = Aact if stage == 0 else Bact
                dst = Bact if stage == 0 else Aact
                s_wlH = k.sem(f"ffH{H}{stage}_wl")
                s_mmH = k.sem(f"ffH{H}{stage}_mm")
                s_evH = k.sem(f"ffH{H}{stage}_ev")
                with nc.Block() as blk:
                    @blk.gpsimd
                    def _(g):
                        for oc in range(NOC):
                            par = oc % 2
                            if oc >= 2:
                                g.wait_ge(s_mmH.h, 2 * (oc - 2) + 2)
                            wv = ht.rearrange("(ic p) o -> p ic o", p=128)[:, :, oc * 128:(oc + 1) * 128]
                            for q in range(4):
                                s_wlH.inc(g.dma_start(wbH[par][:, q * 8:(q + 1) * 8, :],
                                                      wv[:, q * 8:(q + 1) * 8, :]), 16)
                    @blk.tensor
                    def _(te):
                        for oc in range(NOC):
                            par = oc % 2
                            te.wait_ge(s_wlH.h, (4 * oc + 4) * 16)
                            if oc >= 2:
                                te.wait_ge(s_evH.h, 2 * (oc - 2) + 2)
                            for nt in range(2):
                                bank = psA[4 * par + nt]
                                for ic in range(32):
                                    mm = te.matmul(bank[:], wbH[par][:, ic, :],
                                                   src[:, ic, nt * 512:(nt + 1) * 512],
                                                   start=(ic == 0), stop=(ic == 31))
                                    if ic == 31:
                                        s_mmH.inc(mm)
                    @blk.scalar
                    def _(sc):
                        for oc in range(NOC):
                            par = oc % 2
                            for nt in range(2):
                                sc.wait_ge(s_mmH.h, 2 * oc + nt + 1)
                                s_evH.inc(sc.activation(dst[:, oc, nt * 512:(nt + 1) * 512],
                                                        psA[4 * par + nt][:], AF.Gelu,
                                                        scale=scols[:, 2 + stage, oc:oc + 1]))

                if dbg_upto == 1 + stage:
                    sdbg = k.sem("dbg_st")
                    with nc.Block() as blkd:
                        @blkd.sync
                        def _(sy):
                            sdbg.inc(sy.dma_start(dbg_out, dst[:]), 16)
                            sdbg.wait(sy)
                    return
            Cact = Aact
            s_wl3 = k.sem(f"ff3{H}_wl")
            s_mm3 = k.sem(f"ff3{H}_mm")
            s_xr = k.sem(f"ff3{H}_xr")
            s_v = k.sem(f"ff3{H}_v")
            s_l1 = k.sem(f"ff3{H}_l1")
            s_l2 = k.sem(f"ff3{H}_l2")
            s_l3 = k.sem(f"ff3{H}_l3")
            s_l4 = k.sem(f"ff3{H}_l4")
            s_l5 = k.sem(f"ff3{H}_l5")
            s_l6 = k.sem(f"ff3{H}_l6")
            s_st = k.sem(f"ff3{H}_st")
            with nc.Block() as blk:
                @blk.gpsimd
                def _(g):
                    for tsb in range(2):
                        for ic in range(32):
                            gi = tsb * 32 + ic
                            if gi >= 2:
                                g.wait_ge(s_mm3.h, gi - 1)
                            s_wl3.inc(g.dma_start(
                                wb3[gi % 2],
                                w3t.rearrange("(ic p) e -> p ic e", p=128)[:, ic, :]), 16)
                @blk.sync
                def _(sy):
                    for li in range(8):
                        c = H * 8 + li
                        if li >= 2:
                            sy.wait_ge(s_v.h, 2 * (li - 1))
                        s_xr.inc(sy.dma_start(xrsl[:, li % 2, :], xrv[:, c, :]), 16)
                @blk.tensor
                def _(te):
                    for tsb in range(2):
                        if tsb == 1:
                            te.wait_ge(s_v.h, 8)
                        for ic in range(32):
                            gi = tsb * 32 + ic
                            te.wait_ge(s_wl3.h, (gi + 1) * 16)
                            for ts in range(4):
                                for et in range(2):
                                    bank = psA[2 * ts + et]
                                    mm = te.matmul(
                                        bank[:],
                                        Cact[:, ic, (tsb * 4 + ts) * 128:(tsb * 4 + ts + 1) * 128],
                                        wb3[gi % 2][:, et * 512:(et + 1) * 512],
                                        start=(ic == 0), stop=(ic == 31),
                                        skip_group_check=True)
                                    if ts == 3 and et == 1:
                                        s_mm3.inc(mm)
                @blk.vector
                def _(v):
                    for li in range(8):
                        tsb, ts = li // 4, li % 4
                        pa = li % 2
                        v.wait_ge(s_xr.h, (li + 1) * 16)
                        if li >= 2:
                            v.wait_ge(s_st.h, (li - 1) * 16)
                        v.wait_ge(s_mm3.h, 32 * (tsb + 1))
                        for et in range(2):
                            bank = psA[2 * ts + et]
                            v.tensor_tensor(vch[:, pa, et * 512:(et + 1) * 512],
                                            bank[:], s3rep[:, et * 512:(et + 1) * 512],
                                            A.mult)
                            s_v.inc(v.tensor_tensor(
                                vch[:, pa, et * 512:(et + 1) * 512],
                                vch[:, pa, et * 512:(et + 1) * 512],
                                xrsl[:, pa, et * 512:(et + 1) * 512], A.add))
                        if dbg_upto == 3:
                            continue
                        v.tensor_reduce(lst[:, pa:pa + 1, 0:1],
                                        vch[:, pa:pa + 1, :], AX.X, A.add)
                        v.tensor_tensor(junkv[:], vch[:, pa, :], vch[:, pa, :], A.mult)
                        v.tensor_reduce(lst[:, pa:pa + 1, 1:2],
                                        junkv[:].unsqueeze(1), AX.X, A.add)
                        v.tensor_scalar(lst[:, pa, 2:3], lst[:, pa, 0:1], 1.0 / D, None, A.mult)
                        v.tensor_scalar(lst[:, pa, 3:4], lst[:, pa, 1:2], 1.0 / D, None, A.mult)
                        v.tensor_tensor(lst[:, pa, 4:5], lst[:, pa, 2:3], lst[:, pa, 2:3], A.mult)
                        v.tensor_tensor(lst[:, pa, 3:4], lst[:, pa, 3:4],
                                        lst[:, pa, 4:5], A.subtract)
                        s_l3.inc(v.tensor_copy(lst[:, pa, 7:8], lst[:, pa, 3:4]))
                        v.wait_ge(s_l4.h, 2 * (li + 1))
                        v.reciprocal(lst[:, pa, 5:6], lst[:, pa, 4:5])
                        s_l6.inc(v.tensor_scalar(vch[:, pa, :], vch[:, pa, :],
                                                 lst[:, pa, 2:3], lst[:, pa, 5:6],
                                                 A.subtract, A.mult))
                        if dbg_upto == 4:
                            v.tensor_copy(statarch[:, H * 8 + li, :], lst[:, pa, :])
                @blk.scalar
                def _(sc):
                    for li in range(8):
                        pa = li % 2
                        if dbg_upto == 3:
                            sc.wait_ge(s_v.h, 2 * (li + 1))
                            s_st.inc(sc.dma_start(yv[:, H * 8 + li, :], vch[:, pa, :]), 16)
                            continue
                        sc.wait_ge(s_l3.h, li + 1)
                        sc.activation(lst[:, pa, 4:5], lst[:, pa, 7:8],
                                      AF.Sqrt, bias=k.eps_col[:])
                        s_l4.inc(sc.activation(junkc[:], lst[:, pa, 7:8],
                                               AF.Sqrt, bias=k.eps_col[:]), 2)
                        sc.wait_ge(s_l6.h, li + 1)
                        s_st.inc(sc.dma_start(yv[:, H * 8 + li, :], vch[:, pa, :]), 16)
                    sc.wait_ge(s_st.h, 8 * 16)
            if dbg_upto == 4 and H == 1:
                with nc.Block() as blkd2:
                    @blkd2.sync
                    def _(sy):
                        sdbg2 = k.sem("dbg_st2")
                        sdbg2.inc(sy.dma_start(dbg_out[:, 0:1, 0:256].bitcast(F32).rearrange("p a (b c) -> p (a b) c", c=8), statarch[:]), 16)
                        sdbg2.wait(sy)


def host_prep_l0ffn(inputs):
    return {
        "f0_win": inputs["L0_ffn_Win"],
        "f0_binc": inputs["L0_ffn_bin_"].reshape(64, 1),
        "f0_w1": inputs["L0_ffn_W1"],
        "f0_b1": inputs["L0_ffn_b1"],
        "f0_w2": inputs["L0_ffn_W2"],
        "f0_b2c": inputs["L0_ffn_b2"].reshape(64, 1),
        "f0_wout": inputs["L0_ffn_Wout"],
        "f0_bout": inputs["L0_ffn_bout"].reshape(1, 1024),
    }


def host_prep_l1attn(inputs):
    return {"l1_wq": inputs["L1_attn_Wq"], "l1_wk": inputs["L1_attn_Wk"],
            "l1_wv": inputs["L1_attn_Wv"], "l1_wo": inputs["L1_attn_Wo"]}


def host_prep_l1ffn(inputs):
    p = {}
    p["l1_w1t"] = np.ascontiguousarray(inputs["L1_ffn_w1_w"].T)
    p["l1_w2t"] = np.ascontiguousarray(inputs["L1_ffn_w2_w"].T)
    p["l1_h1t"] = np.ascontiguousarray(inputs["L1_ffn_h1_w"].T)
    p["l1_h2t"] = np.ascontiguousarray(inputs["L1_ffn_h2_w"].T)
    p["l1_w3t"] = np.ascontiguousarray(inputs["L1_ffn_w3_w"].T)
    for nm, src in (("l1_s1pc", "L1_ffn_w1_s"), ("l1_s2pc", "L1_ffn_w2_s"),
                    ("l1_sh1pc", "L1_ffn_h1_s"), ("l1_sh2pc", "L1_ffn_h2_s")):
        p[nm] = np.ascontiguousarray(inputs[src].reshape(32, 128).T)
    p["l1_s3row"] = np.ascontiguousarray(inputs["L1_ffn_w3_s"].reshape(1, 1024))
    return p


L1FFN_SPECS = [
    ("l1_w1t", [D, DH]), ("l1_w2t", [D, DH]),
    ("l1_h1t", [DH, DH]), ("l1_h2t", [DH, DH]),
    ("l1_w3t", [DH, D]),
    ("l1_s1pc", [128, 32]), ("l1_s2pc", [128, 32]),
    ("l1_sh1pc", [128, 32]), ("l1_sh2pc", [128, 32]),
    ("l1_s3row", [1, 1024]),
]


def declare_inputs(k, specs):
    for nm, shape in specs:
        k.inp[nm] = k.nc.dram_tensor(nm, shape, F32, kind="ExternalInput").ap()


def build_l1ffn_standalone():
    nc = bass.Bass("TRN2", target_bir_lowering=False, debug=False,
                   detect_race_conditions=False)
    k = K(nc)
    x = nc.dram_tensor("x", [T, D], F32, kind="ExternalInput").ap()
    y = nc.dram_tensor("y", [T, D], F32, kind="ExternalOutput").ap()
    declare_inputs(k, L1FFN_SPECS)
    make_consts(k)
    xf = nc.alloc_sbuf_tensor("xf", [128, FC, T], BF16)
    s_ld = k.sem("xld")
    s_cv = k.sem("xcv")
    with ExitStack() as es:
        xr = es.enter_context(nc.sbuf_tensor("xr", [128, NC, D], F32))
        xb = es.enter_context(nc.sbuf_tensor("xb", [128, NC, D], BF16))
        with nc.Block() as blk:
            @blk.sync
            def _(sy):
                s_ld.inc(sy.dma_start(xr[:], x.rearrange("(c p) d -> p c d", p=128)), 16)
            @blk.vector
            def _(v):
                s_ld.wait(v)
                for c in range(NC):
                    s_cv.inc(v.tensor_copy(xb[:, c, :], xr[:, c, :]))
        emit_xf(k, xb, xf, "xf0")
    emit_l1ffn(k, x, xf, y)
    return nc


def build_l1ffn_debug(upto):
    nc = bass.Bass("TRN2", target_bir_lowering=False, debug=False,
                   detect_race_conditions=False)
    k = K(nc)
    x = nc.dram_tensor("x", [T, D], F32, kind="ExternalInput").ap()
    y = nc.dram_tensor("y", [T, D], F32, kind="ExternalOutput").ap()
    dbg = nc.dram_tensor("dbg", [128, 32 * 1024], BF16, kind="ExternalOutput").ap()
    declare_inputs(k, L1FFN_SPECS)
    make_consts(k)
    xf = nc.alloc_sbuf_tensor("xf", [128, FC, T], BF16)
    s_ld = k.sem("xld")
    s_cv = k.sem("xcv")
    with ExitStack() as es:
        xr = es.enter_context(nc.sbuf_tensor("xr", [128, NC, D], F32))
        xb = es.enter_context(nc.sbuf_tensor("xb", [128, NC, D], BF16))
        with nc.Block() as blk:
            @blk.sync
            def _(sy):
                s_ld.inc(sy.dma_start(xr[:], x.rearrange("(c p) d -> p c d", p=128)), 16)
            @blk.vector
            def _(v):
                s_ld.wait(v)
                for c in range(NC):
                    s_cv.inc(v.tensor_copy(xb[:, c, :], xr[:, c, :]))
        emit_xf(k, xb, xf, "xf0")
    emit_l1ffn(k, x, xf, y, dbg_upto=upto, dbg_out=dbg.rearrange("p (a b) -> p a b", b=1024))
    return nc

_PROGS = {}


def _get_prog(name, builder):
    if name not in _PROGS:
        _PROGS[name] = builder()
    return _PROGS[name]


def kernel(**inputs):
    x = np.ascontiguousarray(inputs["L0_attn_x"])
    p_a0 = host_prep_l0attn(inputs)
    p_f0 = host_prep_l0ffn(inputs)
    p_a1 = host_prep_l1attn(inputs)
    p_f1 = host_prep_l1ffn(inputs)
    progs = [
        ("a0", build_l0attn_standalone, p_a0),
        ("f0", build_l0ffn_standalone, p_f0),
        ("a1", build_l1attn_standalone, p_a1),
        ("f1", build_l1ffn_standalone, p_f1),
    ]
    cur = [np.ascontiguousarray(x[c]) for c in range(8)]
    for name, builder, pp in progs:
        nc = _get_prog(name, builder)
        in_maps = []
        for c in range(8):
            m = {"x": cur[c]}
            m.update(pp)
            in_maps.append(m)
        res = run_bass_kernel_spmd(nc, in_maps, core_ids=list(range(8)))
        cur = [np.ascontiguousarray(res.results[c]["y"]) for c in range(8)]
    return np.stack(cur, axis=0)


L1ATTN_SPECS = [
    ("l1_wq", [D, D]), ("l1_wk", [D, D]), ("l1_wv", [D, D]), ("l1_wo", [D, D]),
]




def emit_tail_bodies(k, tag, blk, *, vch, xrsl, junkv, junkc, lst, xb, xrv_in,
                     xrv_out, ps_slice, s_mm, b_mm, mm_target, s_v, s_st_final=True):
    nc = k.nc
    s_l3 = k.sem(tag + "_tl3")
    s_l4 = k.sem(tag + "_tl4")
    s_l6 = k.sem(tag + "_tl6")
    s_st = k.sem(tag + "_tst")
    b_v, b_l3, b_l4, b_l6, b_st = s_v.n, s_l3.n, s_l4.n, s_l6.n, s_st.n
    s_xr = k.sem(tag + "_txr")
    b_xr = s_xr.n

    @blk.sync
    def _(sy):
        for c in range(NC):
            if c >= 2:
                sy.wait_ge(s_v.h, b_v + 2 * (c - 1))
            s_xr.inc(sy.dma_start(xrsl[:, c % 2, :], xrv_in[:, c, :]), 16)

    @blk.vector
    def _(v):
        for c in range(NC):
            pa = c % 2
            v.wait_ge(s_xr.h, b_xr + (c + 1) * 16)
            if c >= 2:
                v.wait_ge(s_st.h, b_st + (c - 1) * 16)
            for et in range(2):
                v.wait_ge(s_mm.h, b_mm + mm_target(c, et))
                s_v.inc(v.tensor_tensor(vch[:, pa, et * 512:(et + 1) * 512],
                                        ps_slice(c, et),
                                        xrsl[:, pa, et * 512:(et + 1) * 512], A.add))
            chv = k.chainer(tag + "_chv")
            ch2 = lambda i: chv(v, i)
            ch2(v.tensor_reduce(lst[:, pa:pa + 1, 0:1], vch[:, pa:pa + 1, :], AX.X, A.add))
            v.tensor_tensor(junkv[:], vch[:, pa, :], vch[:, pa, :], A.mult)
            ch2(v.tensor_reduce(lst[:, pa:pa + 1, 1:2], junkv[:].unsqueeze(1), AX.X, A.add))
            ch2(v.tensor_scalar(lst[:, pa, 2:3], lst[:, pa, 0:1], 1.0 / D, None, A.mult))
            ch2(v.tensor_scalar(lst[:, pa, 3:4], lst[:, pa, 1:2], 1.0 / D, None, A.mult))
            ch2(v.tensor_tensor(lst[:, pa, 4:5], lst[:, pa, 2:3], lst[:, pa, 2:3], A.mult))
            ch2(v.tensor_tensor(lst[:, pa, 3:4], lst[:, pa, 3:4], lst[:, pa, 4:5], A.subtract))
            s_l3.inc(v.tensor_copy(lst[:, pa, 7:8], lst[:, pa, 3:4]))
            v.wait_ge(s_l4.h, b_l4 + 2 * (c + 1))
            ch2(v.reciprocal(lst[:, pa, 5:6], lst[:, pa, 4:5]))
            v.tensor_scalar(vch[:, pa, :], vch[:, pa, :],
                            lst[:, pa, 2:3], lst[:, pa, 5:6], A.subtract, A.mult)
            if xb is not None:
                s_l6.inc(v.tensor_copy(xb[:, c, :], vch[:, pa, :]))
            else:
                s_l6.inc(v.tensor_copy(junkv[:, 0:1], lst[:, pa, 5:6]))

    @blk.scalar
    def _(sc):
        for c in range(NC):
            pa = c % 2
            sc.wait_ge(s_l3.h, b_l3 + c + 1)
            sc.activation(lst[:, pa, 4:5], lst[:, pa, 7:8], AF.Sqrt, bias=k.eps_col[:])
            s_l4.inc(sc.activation(junkc[:], lst[:, pa, 7:8], AF.Sqrt,
                                   bias=k.eps_col[:]), 2)
            sc.wait_ge(s_l6.h, b_l6 + c + 1)
            s_st.inc(sc.dma_start(xrv_out[:, c, :], vch[:, pa, :]), 16)
        if s_st_final:
            sc.wait_ge(s_st.h, b_st + NC * 16)


def emit_l1attn(k, xf, xr_in_dram, xr_out_dram, tag="a1"):
    nc = k.nc
    wq, wk = k.inp["l1_wq"], k.inp["l1_wk"]
    wv, wo = k.inp["l1_wv"], k.inp["l1_wo"]
    xrv_in = xr_in_dram.rearrange("(c p) d -> p c d", p=128)
    xrv_out = xr_out_dram.rearrange("(c p) d -> p c d", p=128)

    with ExitStack() as es:
        xb = es.enter_context(nc.sbuf_tensor(tag + "_xb", [128, NC, D], BF16))
        outT = es.enter_context(nc.sbuf_tensor(tag + "_outT", [128, FC, T], BF16))
        qf = es.enter_context(nc.sbuf_tensor(tag + "_qf", [128, 4, T], BF16))
        kf = es.enter_context(nc.sbuf_tensor(tag + "_kf", [128, 4, T], BF16))
        vg = es.enter_context(nc.sbuf_tensor(tag + "_vg", [128, NC, 8 * 65], BF16))
        wb = es.enter_context(nc.sbuf_tensor(tag + "_wb", [128, 2, 8, 512], BF16))
        e2 = es.enter_context(nc.sbuf_tensor(tag + "_e2", [128, 2, T], BF16))
        stg = es.enter_context(nc.sbuf_tensor(tag + "_stg", [64, T], BF16))
        shid = es.enter_context(nc.sbuf_tensor(tag + "_shid", [64, 128], BF16))
        cmask = es.enter_context(nc.sbuf_tensor(tag + "_cmask", [128, 4, 512], BF16))
        rsrow = es.enter_context(nc.sbuf_tensor(tag + "_rsrow", [1, T], F32))
        vch = es.enter_context(nc.sbuf_tensor(tag + "_vch", [128, 2, 1024], F32))
        xrsl = es.enter_context(nc.sbuf_tensor(tag + "_xrsl", [128, 2, 1024], F32))
        junkv = es.enter_context(nc.sbuf_tensor(tag + "_junkv", [128, 1024], F32))
        junkc = es.enter_context(nc.sbuf_tensor(tag + "_junkc", [128, 1], F32))
        lst = es.enter_context(nc.sbuf_tensor(tag + "_lst", [128, 2, 8], F32))
        pes = es.enter_context(ExitStack())
        ps_s = pes.enter_context(nc.psum_tensor(tag + "_pss", [128, T], F32))
        po = pes.enter_context(nc.psum_tensor(tag + "_po", [65, T], F32))

        s_ck = k.sem(tag + "_ck")
        for m in range(4):
            with nc.Block() as blk:
                @blk.gpsimd
                def _(g):
                    s_ck.inc(g.iota(junkv[:, 0:512], pattern=[[1, 512]], base=-128 * m,
                                    channel_multiplier=-1, allow_small_or_imprecise_dtypes=True))
                @blk.vector
                def _(v):
                    s_ck.wait(v)
                    s_ck.inc(v.tensor_scalar(cmask[:, m, :], junkv[:, 0:512], 0.0, None, A.is_ge))
        with nc.Block() as blk:
            @blk.gpsimd
            def _(g):
                s_ck.inc(g.iota(junkv[0:64, 0:128], pattern=[[1, 128]], base=-64,
                                channel_multiplier=-1, allow_small_or_imprecise_dtypes=True))
            @blk.vector
            def _(v):
                s_ck.wait(v)
                s_ck.inc(v.tensor_scalar(shid[:], junkv[0:64, 0:128], 0.0, None, A.is_equal))

        for g_i in range(2):
            s_wl = k.sem(tag + "_wlg")
            s_pj = k.sem(tag + "_pjg")
            s_pe = k.sem(tag + "_peg")
            b_wl, b_pj, b_pe = s_wl.n, s_pj.n, s_pe.n
            with nc.Block() as blk:
                @blk.gpsimd
                def _(g):
                    for wi, wt, slot in ((0, wq, 0), (1, wk, 1), (2, wv, 0)):
                        if wi == 2:
                            g.wait_ge(s_pj.h, b_pj + 16)
                        wvw = wt.rearrange("(ic p) o -> p ic o", p=128)[:, :, g_i * 512:(g_i + 1) * 512]
                        s_wl.inc(g.dma_start(wb[:, slot, :, :], wvw), 16)
                @blk.tensor
                def _(te):
                    seq = 0
                    for wi in range(2):
                        te.wait_ge(s_wl.h, b_wl + (wi + 1) * 16)
                        for oc in range(4):
                            for nt in range(4):
                                if seq >= 4:
                                    te.wait_ge(s_pe.h, b_pe + seq - 3)
                                for ic in range(FC):
                                    mm = te.matmul(ps_s[:, nt * 512:(nt + 1) * 512],
                                                   wb[:, wi % 2, ic, oc * 128:(oc + 1) * 128],
                                                   xf[:, ic, nt * 512:(nt + 1) * 512],
                                                   start=(ic == 0), stop=(ic == FC - 1),
                                                   skip_group_check=True)
                                    if ic == FC - 1:
                                        s_pj.inc(mm)
                                seq += 1
                    te.wait_ge(s_wl.h, b_wl + 48)
                    for tc in range(NC):
                        if seq >= 4:
                            te.wait_ge(s_pe.h, b_pe + seq - 3)
                        for ic in range(FC):
                            mm = te.matmul(ps_s[:, (tc % 4) * 512:(tc % 4 + 1) * 512],
                                           xf[:, ic, tc * 128:(tc + 1) * 128],
                                           wb[:, 0, ic, :],
                                           start=(ic == 0), stop=(ic == FC - 1),
                                           skip_group_check=True)
                            if ic == FC - 1:
                                s_pj.inc(mm)
                        seq += 1
                @blk.vector
                def _(v):
                    cnt = 0
                    for wi in range(2):
                        dst = qf if wi == 0 else kf
                        for oc in range(4):
                            for nt in range(4):
                                cnt += 1
                                v.wait_ge(s_pj.h, b_pj + cnt)
                                s_pe.inc(v.tensor_copy(dst[:, oc, nt * 512:(nt + 1) * 512],
                                                       ps_s[:, nt * 512:(nt + 1) * 512]))
                    for tc in range(NC):
                        cnt += 1
                        v.wait_ge(s_pj.h, b_pj + cnt)
                        s_pe.inc(v.tensor_copy(
                            vg[:, tc, 0:520].rearrange("p (h e) -> p h e", e=65)[:, :, 0:64],
                            ps_s[:, (tc % 4) * 512:(tc % 4 + 1) * 512]
                            .rearrange("p (h e) -> p h e", e=64)))
                    s_pe.inc(v.memset(
                        vg[:, :, 0:520].rearrange("p c (h e) -> p c h e", e=65)[:, :, :, 64:65], 1.0))

            for hh in range(8):
                h = g_i * 8 + hh
                hb = (h % 2) * 64
                hc = h // 2
                s_sc = k.sem(tag + "_sc")
                s_ex = k.sem(tag + "_ex")
                s_mk = k.sem(tag + "_mk")
                s_av = k.sem(tag + "_av")
                s_ev = k.sem(tag + "_ev")
                b_sc, b_ex, b_mk, b_av, b_ev = s_sc.n, s_ex.n, s_mk.n, s_av.n, s_ev.n
                with nc.Block() as blk:
                    @blk.tensor
                    def _(te):
                        for jc in range(NC):
                            nt0 = jc // 4
                            if jc > 0:
                                te.wait_ge(s_ex.h, b_ex + jc)
                            for nt in range(nt0, 4):
                                mm = te.matmul(ps_s[:, nt * 512:(nt + 1) * 512],
                                               kf[hb:hb + 64, hc % 4, jc * 128:(jc + 1) * 128],
                                               qf[hb:hb + 64, hc % 4, nt * 512:(nt + 1) * 512],
                                               start=True, stop=True,
                                               skip_group_check=True)
                                if nt == 3:
                                    s_sc.inc(mm)
                            te.wait_ge(s_mk.h, b_mk + jc + 1)
                            for nt in range(nt0, 4):
                                mm = te.matmul(po[:, nt * 512:(nt + 1) * 512],
                                               vg[:, jc, hh * 65:(hh + 1) * 65],
                                               e2[:, jc % 2, nt * 512:(nt + 1) * 512],
                                               start=(jc == 0), stop=(jc == 4 * nt + 3),
                                               skip_group_check=True)
                                if nt == 3:
                                    s_av.inc(mm)
                        te.wait_ge(s_ev.h, b_ev + 1)
                        for nt in range(4):
                            mm = te.matmul(ps_s[:, nt * 512:(nt + 1) * 512],
                                           k.ones_rowf[:, 0:128],
                                           rsrow[:, nt * 512:(nt + 1) * 512],
                                           start=True, stop=True, skip_group_check=True)
                            if nt == 3:
                                s_ev.inc(mm)
                        if hb == 64:
                            te.wait_ge(s_ev.h, b_ev + 3)
                            for nt in range(4):
                                mm = te.matmul(ps_s[:, nt * 512:(nt + 1) * 512],
                                               shid[:], stg[:, nt * 512:(nt + 1) * 512],
                                               start=True, stop=True, skip_group_check=True)
                                if nt == 3:
                                    s_ev.inc(mm)
                    @blk.scalar
                    def _(sc):
                        for jc in range(NC):
                            nt0 = jc // 4
                            sc.wait_ge(s_sc.h, b_sc + jc + 1)
                            if jc >= 2:
                                sc.wait_ge(s_av.h, b_av + jc - 1)
                            s_ex.inc(sc.activation(e2[:, jc % 2, nt0 * 512:T],
                                                   ps_s[:, nt0 * 512:T], AF.Exp, scale=0.125))
                    @blk.vector
                    def _(v):
                        for jc in range(NC):
                            nt0 = jc // 4
                            v.wait_ge(s_ex.h, b_ex + jc + 1)
                            s_mk.inc(v.tensor_tensor(
                                e2[:, jc % 2, nt0 * 512:(nt0 + 1) * 512],
                                e2[:, jc % 2, nt0 * 512:(nt0 + 1) * 512],
                                cmask[:, jc % 4, :], A.mult))
                        v.wait_ge(s_av.h, b_av + NC)
                        v.tensor_copy(rsrow[:, :], po[64:65, :])
                        v.reciprocal(rsrow[:, :], rsrow[:, :])
                        s_ev.inc(v.tensor_copy(stg[:], po[0:64, :]))
                        v.wait_ge(s_ev.h, b_ev + 2)
                        s_ev.inc(v.tensor_tensor(stg[:], stg[:], ps_s[0:64, :], A.mult))
                        if hb == 0:
                            s_ev.inc(v.tensor_copy(outT[0:64, hc, :], stg[:]))
                        else:
                            v.wait_ge(s_ev.h, b_ev + 4)
                            s_ev.inc(v.tensor_copy(outT[64:128, hc, :], ps_s[64:128, :]))

        s_wl = k.sem(tag + "_wlo")
        s_mm = k.sem(tag + "_mmo")
        s_xr = k.sem(tag + "_xro")
        s_v = k.sem(tag + "_vo")
        s_l3 = k.sem(tag + "_l3o")
        s_l4 = k.sem(tag + "_l4o")
        s_l6 = k.sem(tag + "_l6o")
        s_st = k.sem(tag + "_sto")
        with nc.Block() as blk:
            @blk.gpsimd
            def _(g):
                wvw = wo.rearrange("(ic p) o -> p ic o", p=128)
                for half in range(2):
                    s_wl.inc(g.dma_start(wb[:, half, :, :], wvw[:, :, half * 512:(half + 1) * 512]), 16)
            @blk.tensor
            def _(te):
                te.wait_ge(s_wl.h, 32)
                for c in range(NC):
                    if c >= 2:
                        te.wait_ge(s_v.h, 2 * (c - 1))
                    for et in range(2):
                        for fc in range(FC):
                            mm = te.matmul(ps_s[:, (2 * (c % 2) + et) * 512:(2 * (c % 2) + et + 1) * 512],
                                           outT[:, fc, c * 128:(c + 1) * 128],
                                           wb[:, et, fc, :],
                                           start=(fc == 0), stop=(fc == FC - 1),
                                           skip_group_check=True)
                            if fc == FC - 1:
                                s_mm.inc(mm)
            emit_tail_bodies(k, tag + "_t", blk, vch=vch, xrsl=xrsl, junkv=junkv,
                             junkc=junkc, lst=lst, xb=xb, xrv_in=xrv_in,
                             xrv_out=xrv_out,
                             ps_slice=lambda c, et: ps_s[:, (2 * (c % 2) + et) * 512:(2 * (c % 2) + et + 1) * 512],
                             s_mm=s_mm, b_mm=0, mm_target=lambda c, et: 2 * c + et + 1,
                             s_v=s_v)
        pes.close()
        emit_xf(k, xb, xf, tag + "_xf")


def build_l1attn_standalone():
    nc = bass.Bass("TRN2", target_bir_lowering=False, debug=False,
                   detect_race_conditions=False)
    k = K(nc)
    x = nc.dram_tensor("x", [T, D], F32, kind="ExternalInput").ap()
    y = nc.dram_tensor("y", [T, D], F32, kind="ExternalOutput").ap()
    declare_inputs(k, L1ATTN_SPECS)
    make_consts(k)
    xf = nc.alloc_sbuf_tensor("xf", [128, FC, T], BF16)
    s_ld = k.sem("xld")
    s_cv = k.sem("xcv")
    with ExitStack() as es:
        xr = es.enter_context(nc.sbuf_tensor("xr", [128, NC, D], F32))
        xb = es.enter_context(nc.sbuf_tensor("xb", [128, NC, D], BF16))
        with nc.Block() as blk:
            @blk.sync
            def _(sy):
                s_ld.inc(sy.dma_start(xr[:], x.rearrange("(c p) d -> p c d", p=128)), 16)
            @blk.vector
            def _(v):
                s_ld.wait(v)
                for c in range(NC):
                    s_cv.inc(v.tensor_copy(xb[:, c, :], xr[:, c, :]))
        emit_xf(k, xb, xf, "xf0")
    emit_l1attn(k, xf, x, y, tag="a1")
    return nc


_PROGS = {}


def _get_prog(name, builder):
    if name not in _PROGS:
        _PROGS[name] = builder()
    return _PROGS[name]


def kernel(**inputs):
    x = np.ascontiguousarray(inputs["L0_attn_x"])
    p_a0 = host_prep_l0attn(inputs)
    p_f0 = host_prep_l0ffn(inputs)
    p_a1 = host_prep_l1attn(inputs)
    p_f1 = host_prep_l1ffn(inputs)
    progs = [
        ("a0", build_l0attn_standalone, p_a0),
        ("f0", build_l0ffn_standalone, p_f0),
        ("a1", build_l1attn_standalone, p_a1),
        ("f1", build_l1ffn_standalone, p_f1),
    ]
    cur = [np.ascontiguousarray(x[c]) for c in range(8)]
    for name, builder, pp in progs:
        nc = _get_prog(name, builder)
        in_maps = []
        for c in range(8):
            m = {"x": cur[c]}
            m.update(pp)
            in_maps.append(m)
        res = run_bass_kernel_spmd(nc, in_maps, core_ids=list(range(8)))
        cur = [np.ascontiguousarray(res.results[c]["y"]) for c in range(8)]
    return np.stack(cur, axis=0)


L0FFN_SPECS = [
    ("f0_win", [D, 64]), ("f0_binc", [64, 1]), ("f0_w1", [64, 16]),
    ("f0_b1", [64, 16]), ("f0_w2", [64, 16]), ("f0_b2c", [64, 1]),
    ("f0_wout", [64, D]), ("f0_bout", [1, D]),
]


def emit_l0ffn(k, xf, xr_in_dram, xr_out_dram, tag="f0"):
    nc = k.nc
    xrv_in = xr_in_dram.rearrange("(c p) d -> p c d", p=128)
    xrv_out = xr_out_dram.rearrange("(c p) d -> p c d", p=128)
    with ExitStack() as es:
        xb = es.enter_context(nc.sbuf_tensor(tag + "_xb", [128, NC, D], BF16))
        winb = es.enter_context(nc.sbuf_tensor(tag + "_winb", [128, 8, 64], BF16))
        woutb = es.enter_context(nc.sbuf_tensor(tag + "_woutb", [64, D], BF16))
        w1 = es.enter_context(nc.sbuf_tensor(tag + "_w1sb", [64, 16], F32))
        bias2 = es.enter_context(nc.sbuf_tensor(tag + "_bias2", [64, 16], F32))
        b1 = es.enter_context(nc.sbuf_tensor(tag + "_b1sb", [64, 16], F32))
        w2 = es.enter_context(nc.sbuf_tensor(tag + "_w2sb", [64, 16], F32))
        b2c = es.enter_context(nc.sbuf_tensor(tag + "_b2csb", [64, 1], F32))
        binc = es.enter_context(nc.sbuf_tensor(tag + "_bincsb", [64, 1], F32))
        boutr = es.enter_context(nc.sbuf_tensor(tag + "_boutr", [1, D], F32))
        hbuf = es.enter_context(nc.sbuf_tensor(tag + "_hbuf", [64, 16, 512], BF16))
        gbuf = es.enter_context(nc.sbuf_tensor(tag + "_gbuf", [64, T], F32))
        gb = es.enter_context(nc.sbuf_tensor(tag + "_gb", [64, T], BF16))
        vch = es.enter_context(nc.sbuf_tensor(tag + "_vch", [128, 2, 1024], F32))
        xrsl = es.enter_context(nc.sbuf_tensor(tag + "_xrsl", [128, 2, 1024], F32))
        junkv = es.enter_context(nc.sbuf_tensor(tag + "_junkv", [128, 1024], F32))
        junkc = es.enter_context(nc.sbuf_tensor(tag + "_junkc", [128, 1], F32))
        lst = es.enter_context(nc.sbuf_tensor(tag + "_lst", [128, 2, 8], F32))
        pes = es.enter_context(ExitStack())
        ps_s = pes.enter_context(nc.psum_tensor(tag + "_pss", [64, T], F32))
        ps_o = pes.enter_context(nc.psum_tensor(tag + "_pso", [128, 4, 512], F32))

        s_wl = k.sem(tag + "_wl")
        s_fl = k.sem(tag + "_fl")
        s_pp = k.sem(tag + "_pp")
        s_sm = k.sem(tag + "_sm")
        s_ha = k.sem(tag + "_ha")
        s_gg = k.sem(tag + "_gg")
        s_gb = k.sem(tag + "_gb")
        s_mm = k.sem(tag + "_mm")
        s_v = k.sem(tag + "_v")
        with nc.Block() as blk:
            @blk.gpsimd
            def _(g):
                s_wl.inc(g.dma_start(winb[:],
                                     k.inp["f0_win"].rearrange("(ic p) m -> p ic m", p=128)), 16)
                s_wl.inc(g.dma_start(woutb[:], k.inp["f0_wout"]), 16)
            @blk.sync
            def _(sy):
                for nm, dst in (("f0_w1", w1), ("f0_b1", b1), ("f0_w2", w2),
                                ("f0_b2c", b2c), ("f0_binc", binc), ("f0_bout", boutr)):
                    s_fl.inc(sy.dma_start(dst[:], k.inp[nm]), 16)
            @blk.vector
            def _(v):
                s_fl.wait(v)
                s_pp.inc(v.scalar_tensor_tensor(bias2[:], w1[:], binc[:], b1[:],
                                                A.mult, A.add))
            @blk.tensor
            def _(te):
                te.wait_ge(s_wl.h, 16)
                for nt in range(4):
                    for ic in range(FC):
                        mm = te.matmul(ps_s[:, nt * 512:(nt + 1) * 512],
                                       winb[:, ic, :],
                                       xf[:, ic, nt * 512:(nt + 1) * 512],
                                       start=(ic == 0), stop=(ic == FC - 1),
                                       skip_group_check=True)
                        if ic == FC - 1:
                            s_sm.inc(mm)
                te.wait_ge(s_wl.h, 32)
                for c in range(NC):
                    te.wait_ge(s_gb.h, c // 4 + 1)
                    if c >= 2:
                        te.wait_ge(s_v.h, 2 * (c - 1))
                    for et in range(2):
                        bank = ps_o[:, 2 * (c % 2) + et, :]
                        mm = te.matmul(bank, gb[:, c * 128:(c + 1) * 128],
                                       woutb[:, et * 512:(et + 1) * 512],
                                       start=True, stop=False, skip_group_check=True)
                        mm = te.matmul(bank, k.ones_rowf[:, 0:128],
                                       boutr[:, et * 512:(et + 1) * 512],
                                       start=False, stop=True, skip_group_check=True)
                        s_mm.inc(mm)
            @blk.scalar
            def _(sc):
                sc.wait_ge(s_pp.h, 1)
                for nt in range(4):
                    sc.wait_ge(s_sm.h, nt + 1)
                    if nt >= 1:
                        sc.wait_ge(s_gg.h, nt)
                    for i in range(16):
                        s_ha.inc(sc.activation(hbuf[:, i, :],
                                               ps_s[:, nt * 512:(nt + 1) * 512],
                                               AF.Relu, scale=w1[:, i:i + 1],
                                               bias=bias2[:, i:i + 1]))
            @blk.vector
            def _(v):
                for nt in range(4):
                    for i in range(16):
                        v.wait_ge(s_ha.h, 16 * nt + i + 1)
                        if i == 0:
                            v.tensor_scalar(gbuf[:, nt * 512:(nt + 1) * 512],
                                            hbuf[:, 0, :], w2[:, 0:1], None, A.mult)
                        else:
                            v.scalar_tensor_tensor(gbuf[:, nt * 512:(nt + 1) * 512],
                                                   hbuf[:, i, :], w2[:, i:i + 1],
                                                   gbuf[:, nt * 512:(nt + 1) * 512],
                                                   A.mult, A.add)
                    v.tensor_scalar(gbuf[:, nt * 512:(nt + 1) * 512],
                                    gbuf[:, nt * 512:(nt + 1) * 512],
                                    b2c[:], None, A.add)
                    s_gg.inc(v.tensor_copy(gb[:, nt * 512:(nt + 1) * 512],
                                           gbuf[:, nt * 512:(nt + 1) * 512]))
                    s_gb.inc(v.tensor_copy(junkc[0:64, :], gbuf[:, nt * 512:nt * 512 + 1]))
            emit_tail_bodies(k, tag + "_t", blk, vch=vch, xrsl=xrsl, junkv=junkv,
                             junkc=junkc, lst=lst, xb=xb, xrv_in=xrv_in,
                             xrv_out=xrv_out,
                             ps_slice=lambda c, et: ps_o[:, 2 * (c % 2) + et, :],
                             s_mm=s_mm, b_mm=0, mm_target=lambda c, et: 2 * c + et + 1,
                             s_v=s_v)
        pes.close()
        emit_xf(k, xb, xf, tag + "_xf")


def build_l0ffn_standalone():
    nc = bass.Bass("TRN2", target_bir_lowering=False, debug=False,
                   detect_race_conditions=False)
    k = K(nc)
    x = nc.dram_tensor("x", [T, D], F32, kind="ExternalInput").ap()
    y = nc.dram_tensor("y", [T, D], F32, kind="ExternalOutput").ap()
    declare_inputs(k, L0FFN_SPECS)
    make_consts(k)
    xf = nc.alloc_sbuf_tensor("xf", [128, FC, T], BF16)
    s_ld = k.sem("xld")
    s_cv = k.sem("xcv")
    with ExitStack() as es:
        xr = es.enter_context(nc.sbuf_tensor("xr", [128, NC, D], F32))
        xb = es.enter_context(nc.sbuf_tensor("xb", [128, NC, D], BF16))
        with nc.Block() as blk:
            @blk.sync
            def _(sy):
                s_ld.inc(sy.dma_start(xr[:], x.rearrange("(c p) d -> p c d", p=128)), 16)
            @blk.vector
            def _(v):
                s_ld.wait(v)
                for c in range(NC):
                    s_cv.inc(v.tensor_copy(xb[:, c, :], xr[:, c, :]))
        emit_xf(k, xb, xf, "xf0")
    emit_l0ffn(k, xf, x, y, tag="f0")
    return nc


L0ATTN_SPECS = [
    ("a0_wq", [D, D]), ("a0_wk", [D, D]), ("a0_wv", [D, D]), ("a0_wo", [D, D]),
    ("a0_bqc", [128, 8]), ("a0_bkc", [128, 8]), ("a0_bvr", [1, D]), ("a0_bor", [1, D]),
    ("a0_probes", [128, 3 * 16 * 64]),
]
NBL = [128, 8, 32]
CSL = [16, 256, 64]
SHL = [4, 8, 6]
TOFF = [0, 128, 136]
REFL = [2, 0, 1]
NT168 = 168


def emit_l0attn(k, xf, x_in_dram, xr_out_dram, tag="a0", dbg=None):
    nc = k.nc
    xrv_in = x_in_dram.rearrange("(c p) d -> p c d", p=128)
    xrv_out = xr_out_dram.rearrange("(c p) d -> p c d", p=128)
    scr = nc.dram_tensor(tag + "_scr", [1, 2048], BF16, kind="Internal").ap()

    with ExitStack() as es:
        outT = es.enter_context(nc.sbuf_tensor(tag + "_outT", [128, FC, T], BF16))
        probesb = es.enter_context(nc.sbuf_tensor(tag + "_probesb", [128, 3072], BF16))
        L128 = es.enter_context(nc.sbuf_tensor(tag + "_L128", [128, 128], BF16))
        L16 = es.enter_context(nc.sbuf_tensor(tag + "_L16", [16, 16], BF16))
        SELf = es.enter_context(nc.sbuf_tensor(tag + "_SELf", [16, T], F32))
        Unb = [es.enter_context(nc.sbuf_tensor(f"{tag}_Unb{l}", [NBL[l], NBL[l]], F32))
               for l in range(3)]
        iotan = es.enter_context(nc.sbuf_tensor(tag + "_iotan", [128, 128], F32))
        iota64r = es.enter_context(nc.sbuf_tensor(tag + "_iota64r", [128, 64], F32))
        tb = [es.enter_context(nc.sbuf_tensor(f"{tag}_tb{l}", [128, 64], F32))
              for l in range(3)]
        tbi = es.enter_context(nc.sbuf_tensor(tag + "_tbi", [128, 64], I32))
        tbi2 = es.enter_context(nc.sbuf_tensor(tag + "_tbi2", [128, 64], I32))
        shid = es.enter_context(nc.sbuf_tensor(tag + "_shid", [64, 128], BF16))
        bqc = es.enter_context(nc.sbuf_tensor(tag + "_bqcs", [128, 8], F32))
        bkc = es.enter_context(nc.sbuf_tensor(tag + "_bkcs", [128, 8], F32))
        bvr = es.enter_context(nc.sbuf_tensor(tag + "_bvrs", [1, D], F32))

        s_ct = k.sem(tag + "_ct")
        with nc.Block() as blk:
            @blk.sync
            def _(sy):
                s_ct.inc(sy.dma_start(bqc[:], k.inp["a0_bqc"]), 16)
                s_ct.inc(sy.dma_start(bkc[:], k.inp["a0_bkc"]), 16)
                s_ct.inc(sy.dma_start(bvr[:], k.inp["a0_bvr"]), 16)
            @blk.gpsimd
            def _(g):
                s_ctg = k.sem(tag + "_ctg")
                s_ctg.inc(g.dma_start(probesb[:], k.inp["a0_probes"]), 16)
                g.iota(iotan[:], pattern=[[1, 128]], base=0, channel_multiplier=0,
                       allow_small_or_imprecise_dtypes=True)
                g.iota(iota64r[:], pattern=[[1, 64]], base=0, channel_multiplier=0,
                       allow_small_or_imprecise_dtypes=True)
                s_ct.inc(g.iota(SELf[:], pattern=[[1, 16], [0, 128]], base=0,
                                channel_multiplier=-1, allow_small_or_imprecise_dtypes=True))
            @blk.vector
            def _(v):
                s_ct.wait(v)
                s_ct.inc(v.tensor_scalar(SELf[:], SELf[:], 0.0, None, A.is_equal))
                v.tensor_copy(tbi[:], iota64r[:])
                v.tensor_scalar(tbi2[:], tbi[:], 9973, None, A.mult)
                for l in range(3):
                    v.tensor_scalar(tbi[:], tbi2[:], NBL[l] - 1, None, A.bitwise_and)
                    s_ct.inc(v.tensor_copy(tb[l][:], tbi[:]))
        with nc.Block() as blk:
            @blk.gpsimd
            def _(g):
                s_ct.inc(g.iota(SELf[0:16, 0:128].bitcast(F32), pattern=[[1, 128]], base=0,
                                channel_multiplier=-1, allow_small_or_imprecise_dtypes=True)) \
                    if False else None
                s_ct.inc(g.iota(iotan[:], pattern=[[1, 128]], base=0, channel_multiplier=-1,
                                allow_small_or_imprecise_dtypes=True))
            @blk.vector
            def _(v):
                s_ct.wait(v)
                v.tensor_scalar(L128[:], iotan[:], 0.0, None, A.is_gt)
                v.tensor_scalar(L16[:], iotan[0:16, 0:16], 0.0, None, A.is_gt)
                for l in range(3):
                    nb = NBL[l]
                    v.tensor_scalar(Unb[l][:], iotan[0:nb, 0:nb], 0.0, None, A.is_gt)
                v.tensor_scalar(iotan[0:64, :], iotan[0:64, :], 64.0, None, A.subtract)
                s_ct.inc(v.tensor_scalar(shid[:], iotan[0:64, :], 0.0, None, A.is_equal))
        with nc.Block() as blk:
            @blk.gpsimd
            def _(g):
                s_ct.inc(g.iota(iotan[:], pattern=[[1, 128]], base=0, channel_multiplier=0,
                                allow_small_or_imprecise_dtypes=True))

        with ExitStack() as hs:
            E = hs.enter_context(nc.sbuf_tensor(tag + "_E", [128, NC, T], BF16))
            qfh = hs.enter_context(nc.sbuf_tensor(tag + "_qfh", [64, T], BF16))
            kfh = hs.enter_context(nc.sbuf_tensor(tag + "_kfh", [64, T], BF16))
            vh = hs.enter_context(nc.sbuf_tensor(tag + "_vh", [128, NC, 64], BF16))
            wqkv = hs.enter_context(nc.sbuf_tensor(tag + "_wqkv", [128, 3, 8, 64], BF16))
            pmax = hs.enter_context(nc.sbuf_tensor(tag + "_pmax", [128, 16], F32))
            regf = hs.enter_context(nc.sbuf_tensor(tag + "_regf", [128, 16], F32))
            regi = hs.enter_context(nc.sbuf_tensor(tag + "_regi", [128, 16], I32))
            bkt = hs.enter_context(nc.sbuf_tensor(tag + "_bkt", [128, 16], F32))
            Obuf = hs.enter_context(nc.sbuf_tensor(tag + "_Obuf", [128, NC, 128], BF16))
            T16 = hs.enter_context(nc.sbuf_tensor(tag + "_T16b", [16, 128], BF16))
            ccol = hs.enter_context(nc.sbuf_tensor(tag + "_ccol", [128, 1], F32))
            strow = hs.enter_context(nc.sbuf_tensor(tag + "_strow", [1, 128], F32))
            carry = hs.enter_context(nc.sbuf_tensor(tag + "_carry", [16, 128], F32))
            ranks = hs.enter_context(nc.sbuf_tensor(tag + "_ranks", [128, 16], F32))
            ranki = hs.enter_context(nc.sbuf_tensor(tag + "_ranki", [128, 16], I32))
            cvals = hs.enter_context(nc.sbuf_tensor(tag + "_cvals", [128, 3, 16], F32))
            rtmp = hs.enter_context(nc.sbuf_tensor(tag + "_rtmp", [128, 128], F32))
            Ucol = hs.enter_context(nc.sbuf_tensor(tag + "_Ucol", [128, NC, NT168], BF16))
            Scol = hs.enter_context(nc.sbuf_tensor(tag + "_Scol", [128, NC, NT168], BF16))
            rsb = hs.enter_context(nc.sbuf_tensor(tag + "_rsb", [128, 3, 16], F32))
            bkts = hs.enter_context(nc.sbuf_tensor(tag + "_bkts", [128, 3, 16], BF16))
            t16a = hs.enter_context(nc.sbuf_tensor(tag + "_t16a", [16, 3, 128], BF16))
            tsa = hs.enter_context(nc.sbuf_tensor(tag + "_tsa", [1, 1, 512], BF16))
            rstmp = hs.enter_context(nc.sbuf_tensor(tag + "_rstmp", [128, 1024], F32))
            T2sb = hs.enter_context(nc.sbuf_tensor(tag + "_T2sb", [128, T], BF16))
            T01sb = hs.enter_context(nc.sbuf_tensor(tag + "_T01sb", [40, T], BF16))
            TTb = hs.enter_context(nc.sbuf_tensor(tag + "_TTb", [128, NC, NT168], BF16))
            A2T = hs.enter_context(nc.sbuf_tensor(tag + "_A2T", [128, T], BF16))
            PS_A = hs.enter_context(nc.psum_tensor(tag + "_psA", [128, T], F32))
            PS_C = hs.enter_context(nc.psum_tensor(tag + "_psC", [64, T], F32))

            for h in range(16):
                emit_l0attn_head(k, tag, h, locals())
                if dbg is not None and h == 0:
                    sdb = k.sem(tag + "_dbg")
                    with nc.Block() as blkd:
                        @blkd.vector
                        def _(v):
                            v.tensor_copy(rstmp[:, 0:48], cvals[:].rearrange("p a b -> p (a b)"))
                            v.tensor_copy(rstmp[:, 64:65], ccol[:])
                            sdb.inc(v.tensor_copy(rstmp[:, 48:64], ranks[:]))
                        @blkd.sync
                        def _(sy):
                            sdb.wait(sy)
                            sdb.inc(sy.dma_start(dbg[:, 0:65], rstmp[:, 0:65]), 16)
                            sdb.inc(sy.dma_start(dbg[:, 64:112], rsb[:].rearrange("p a b -> p (a b)")), 16)
                            sdb.inc(sy.dma_start(dbg[:, 112:2160].bitcast(BF16).rearrange("p (a b) -> p a b", b=4096)[:, 0, :].rearrange("p (a b) -> p a b", b=2048)[:, 0, :] if False else dbg[:, 112:1136].bitcast(BF16), E[:, 0, :]), 16)
                            sdb.inc(sy.dma_start(dbg[0:64, 1136:3184].bitcast(BF16)[0:64, 0:2048], qfh[:]), 16)
                            sdb.inc(sy.dma_start(dbg[:, 3184:3216].bitcast(BF16)[:, 0:64], vh[:, 0, :]), 16)
                            sdb.inc(sy.dma_start(dbg[:, 3216:3232], bkt[:]), 16)
                            sdb.inc(sy.dma_start(dbg[:, 3232:3248], regf[:]), 16)
                            sdb.inc(sy.dma_start(dbg[0:16, 3248:3376], carry[:]), 16)
                            sdb.inc(sy.dma_start(dbg[0:1, 3376:3504], strow[:]), 16)
                            sdb.inc(sy.dma_start(dbg[0:16, 3508:3572].bitcast(BF16), T16[:]), 16)
                            sdb.inc(sy.dma_start(dbg[:, 3572:3596].bitcast(BF16), bkts[:].rearrange("p a b -> p (a b)")), 16)
                            sdb.inc(sy.dma_start(dbg[0:16, 3620:3812].bitcast(BF16), t16a[:].rearrange("p a b -> p (a b)")), 16)
                            sdb.inc(sy.dma_start(dbg[0:1, 3824:4080].bitcast(BF16)[0:1, 0:512], tsa[0:1, 0, 0:512]), 16)
                            sdb.wait(sy)

        with ExitStack() as ts:
            xb = ts.enter_context(nc.sbuf_tensor(tag + "_xb", [128, NC, D], BF16))
            wob = ts.enter_context(nc.sbuf_tensor(tag + "_wob", [128, 2, 8, 512], BF16))
            bor = ts.enter_context(nc.sbuf_tensor(tag + "_bors", [1, D], F32))
            vch = ts.enter_context(nc.sbuf_tensor(tag + "_vch", [128, 2, 1024], F32))
            xrsl = ts.enter_context(nc.sbuf_tensor(tag + "_xrsl", [128, 2, 1024], F32))
            junkv = ts.enter_context(nc.sbuf_tensor(tag + "_junkv", [128, 1024], F32))
            junkc = ts.enter_context(nc.sbuf_tensor(tag + "_junkc", [128, 1], F32))
            lst = ts.enter_context(nc.sbuf_tensor(tag + "_lst", [128, 2, 8], F32))
            ps_w = ts.enter_context(nc.psum_tensor(tag + "_psw", [128, T], F32))
            s_wl = k.sem(tag + "_wlo")
            s_mm = k.sem(tag + "_mmo")
            s_v = k.sem(tag + "_vo")
            with nc.Block() as blk:
                @blk.gpsimd
                def _(g):
                    wvw = k.inp["a0_wo"].rearrange("(ic p) o -> p ic o", p=128)
                    for half in range(2):
                        s_wl.inc(g.dma_start(wob[:, half, :, :],
                                             wvw[:, :, half * 512:(half + 1) * 512]), 16)
                @blk.sync
                def _(sy):
                    s_bor = k.sem(tag + "_bor")
                    s_bor.inc(sy.dma_start(bor[:], k.inp["a0_bor"]), 16)
                @blk.tensor
                def _(te):
                    te.wait_ge(s_wl.h, 32)
                    te.wait_ge(k.sem(tag + "_bor").h, 16)
                    for c in range(NC):
                        if c >= 2:
                            te.wait_ge(s_v.h, 2 * (c - 1))
                        for et in range(2):
                            bank = ps_w[:, (2 * (c % 2) + et) * 512:(2 * (c % 2) + et + 1) * 512]
                            for fc in range(FC):
                                te.matmul(bank, outT[:, fc, c * 128:(c + 1) * 128],
                                          wob[:, et, fc, :],
                                          start=(fc == 0), stop=False,
                                          skip_group_check=True)
                            mm = te.matmul(bank, k.ones_rowf[:, 0:128],
                                           bor[:, et * 512:(et + 1) * 512],
                                           start=False, stop=True, skip_group_check=True)
                            s_mm.inc(mm)
                emit_tail_bodies(k, tag + "_t", blk, vch=vch, xrsl=xrsl, junkv=junkv,
                                 junkc=junkc, lst=lst, xb=xb, xrv_in=xrv_in,
                                 xrv_out=xrv_out,
                                 ps_slice=lambda c, et: ps_w[:, (2 * (c % 2) + et) * 512:(2 * (c % 2) + et + 1) * 512],
                                 s_mm=s_mm, b_mm=0, mm_target=lambda c, et: 2 * c + et + 1,
                                 s_v=s_v)
            emit_xf(k, xb, xf, tag + "_xf")


def emit_l0attn_head(k, tag, h, B):
    nc = k.nc
    xf = B["xf"]
    E, qfh, kfh, vh = B["E"], B["qfh"], B["kfh"], B["vh"]
    stg = kfh
    wqkv, probesb = B["wqkv"], B["probesb"]
    pmax, regf, regi, bkt = B["pmax"], B["regf"], B["regi"], B["bkt"]
    Obuf, T16, ccol, strow, carry = (B["Obuf"], B["T16"],
                                     B["ccol"], B["strow"], B["carry"])
    ohb = Obuf[:, :, 64:128]
    tstag = B["rstmp"][0:1, 0:1024].bitcast(BF16)
    ranks, ranki, cvals, rtmp = B["ranks"], B["ranki"], B["cvals"], B["rtmp"]
    Ucol, Scol, rsb, rstmp = B["Ucol"], B["Scol"], B["rsb"], B["rstmp"]
    bkts = B["bkts"]
    t16a, tsa = B["t16a"], B["tsa"]
    Acol = Ucol
    T2sb, T01sb, TTb = B["T2sb"], B["T01sb"], B["TTb"]
    A2T = B["A2T"]
    A01T = B["qfh"][0:40, :]
    S2T, S01T = B["T2sb"], B["T01sb"]
    PS_A, PS_C = B["PS_A"], B["PS_C"]
    L128, L16, SELf, Unb, iotan, iota64r, tb = (B["L128"], B["L16"], B["SELf"],
                                                B["Unb"], B["iotan"], B["iota64r"], B["tb"])
    shid, bqc, bkc, bvr, outT, scr = (B["shid"], B["bqc"], B["bkc"], B["bvr"],
                                      B["outT"], B["scr"])
    hb = (h % 2) * 64

    s_wl = k.sem(tag + "_hwl"); b_wl = s_wl.n
    s_mmq = k.sem(tag + "_hmmq"); b_mmq = s_mmq.n
    s_evq = k.sem(tag + "_hevq"); b_evq = s_evq.n
    s_mmv = k.sem(tag + "_hmmv"); b_mmv = s_mmv.n
    s_evv = k.sem(tag + "_hevv"); b_evv = s_evv.n
    s_pp = k.sem(tag + "_hpp"); b_pp = s_pp.n
    s_O = k.sem(tag + "_hO"); b_O = s_O.n
    s_cnt = k.sem(tag + "_hcnt"); b_cnt = s_cnt.n
    s_ts = k.sem(tag + "_hts"); b_ts = s_ts.n
    s_t16 = k.sem(tag + "_ht16"); b_t16 = s_t16.n
    s_pcm = k.sem(tag + "_hpcm"); b_pcm = s_pcm.n
    s_pcv = k.sem(tag + "_hpcv"); b_pcv = s_pcv.n
    s_rp = k.sem(tag + "_hrp"); b_rp = s_rp.n
    s_rk = k.sem(tag + "_hrk"); b_rk = s_rk.n
    s_cv = k.sem(tag + "_hcv"); b_cv = s_cv.n
    s_ust = k.sem(tag + "_hust"); b_ust = s_ust.n
    s_sc = k.sem(tag + "_hsc"); b_sc = s_sc.n
    s_exp = k.sem(tag + "_hexp"); b_exp = s_exp.n
    s_t2 = k.sem(tag + "_ht2"); b_t2 = s_t2.n
    s_tev = k.sem(tag + "_htev"); b_tev = s_tev.n
    s_ttm = k.sem(tag + "_httm"); b_ttm = s_ttm.n
    s_ttv = k.sem(tag + "_httv"); b_ttv = s_ttv.n
    s_rs = k.sem(tag + "_hrs"); b_rs = s_rs.n
    s_f = k.sem(tag + "_hf"); b_f = s_f.n
    s_pm = k.sem(tag + "_hpm"); b_pm = s_pm.n
    s_av = k.sem(tag + "_hav"); b_av = s_av.n
    s_oe = k.sem(tag + "_hoe"); b_oe = s_oe.n

    with nc.Block() as blk:
        @blk.gpsimd
        def _(g):
            for wi, nm in ((0, "a0_wq"), (1, "a0_wk"), (2, "a0_wv")):
                wv = k.inp[nm].rearrange("(ic p) o -> p ic o", p=128)[:, :, h * 64:(h + 1) * 64]
                s_wl.inc(g.dma_start(wqkv[:, wi, :, :], wv), 16)
        @blk.sync
        def _(sy):
            for l in range(3):
                nb = NBL[l]
                sy.wait_ge(s_ts.h, b_ts + l + 1)
                s_t16.inc(sy.dma_start(scr[0:1, 0:16 * nb], tstag[0:1, 0:16 * nb]), 16)
                sy.wait_ge(s_t16.h, b_t16 + 32 * l + 16)
                s_t16.inc(sy.dma_start(
                    T16[0:16, 0:nb],
                    scr[0:1, 0:16 * nb].rearrange("a (c v) -> (a c) v", v=nb)), 16)

        @blk.tensor
        def _(te):
            for wi in range(2):
                te.wait_ge(s_wl.h, b_wl + 16 * (wi + 1))
                if wi == 1:
                    te.wait_ge(s_evq.h, b_evq + 4)
                for nt in range(4):
                    for ic in range(FC):
                        mm = te.matmul(PS_C[:, nt * 512:(nt + 1) * 512],
                                       wqkv[:, wi, ic, :],
                                       xf[:, ic, nt * 512:(nt + 1) * 512],
                                       start=(ic == 0), stop=(ic == FC - 1),
                                       skip_group_check=True)
                        if ic == FC - 1:
                            s_mmq.inc(mm)
            te.wait_ge(s_wl.h, b_wl + 48)
            for tc in range(NC):
                for ic in range(FC):
                    te.matmul(PS_A[:, tc * 64:(tc + 1) * 64],
                              xf[:, ic, tc * 128:(tc + 1) * 128],
                              wqkv[:, 2, ic, :],
                              start=(ic == 0), stop=False, skip_group_check=True)
                mm = te.matmul(PS_A[:, tc * 64:(tc + 1) * 64], k.ones_rowf[:, 0:128],
                               bvr[:, h * 64:(h + 1) * 64],
                               start=False, stop=True, skip_group_check=True)
                s_mmv.inc(mm)
            te.wait_ge(k.sem(tag + "_ctg").h, 16)
            for l in range(3):
                rl = REFL[l]
                nb = NBL[l]
                te.wait_ge(s_evq.h, b_evq + 8)
                te.wait_ge(s_evv.h, b_evv + 1)
                if l > 0:
                    te.wait_ge(s_rk.h, b_rk + 16 * l)
                for c in range(NC):
                    mm = te.matmul(PS_A[:, c * 64:(c + 1) * 64],
                                   qfh[:, c * 128:(c + 1) * 128],
                                   probesb[0:64, (rl * 16 + h) * 64:(rl * 16 + h + 1) * 64],
                                   start=True, stop=True, skip_group_check=True)
                    if c == NC - 1:
                        s_pp.inc(mm)
                te.wait_ge(s_O.h, b_O + l + 1)
                Ofl = Obuf[:].rearrange("p c e -> p (c e)")
                for c in range(NC):
                    mm = te.matmul(PS_A[0:1, c * nb:(c + 1) * nb], k.ones_col[:, :],
                                   Ofl[:, c * nb:(c + 1) * nb],
                                   start=True, stop=True, skip_group_check=True)
                    if c == NC - 1:
                        s_cnt.inc(mm)
                te.wait_ge(s_t16.h, b_t16 + 32 * (l + 1))
                s_pcm.inc(te.matmul(PS_A[0:nb, 1200:1201], T16[0:16, 0:nb],
                                    k.ones_col[0:16, :], start=True, stop=True,
                                    skip_group_check=True))
                s_pcm.inc(te.matmul(PS_A[0:16, 1600:1600 + nb], L16[:],
                                    T16[0:16, 0:nb], start=True, stop=False,
                                    skip_group_check=True))
                te.wait_ge(s_pcv.h, b_pcv + 3 * l + 1)
                s_pcm.inc(te.matmul(PS_A[0:1, 1400:1400 + nb], ccol[0:nb, :],
                                    Unb[l][:], start=True, stop=True,
                                    skip_group_check=True))
                te.wait_ge(s_pcv.h, b_pcv + 3 * l + 2)
                s_pcm.inc(te.matmul(PS_A[0:16, 1600:1600 + nb],
                                    k.ones_rowf[0:1, 0:16],
                                    strow[0:1, 0:nb],
                                    start=False, stop=True, skip_group_check=True))
                te.wait_ge(s_pcv.h, b_pcv + 3 * l + 3)
                for c in range(NC):
                    if c > 0:
                        te.wait_ge(s_rk.h, b_rk + 16 * l + c)
                    te.matmul(PS_A[:, 1024:1024 + nb], L128[:], Ofl[:, c * nb:(c + 1) * nb],
                              start=True, stop=False, skip_group_check=True)
                    mm = te.matmul(PS_A[:, 1024:1024 + nb],
                                   SELf[:, c * 128:(c + 1) * 128],
                                   carry[0:16, 0:nb], start=False, stop=True,
                                   skip_group_check=True)
                    s_rp.inc(mm)
            te.wait_ge(s_cv.h, b_cv + 3)
            for jc in range(NC):
                if jc > 0:
                    te.wait_ge(s_exp.h, b_exp + jc)
                for nt in range(4):
                    mm = te.matmul(PS_A[:, nt * 512:(nt + 1) * 512],
                                   kfh[:, jc * 128:(jc + 1) * 128],
                                   qfh[:, nt * 512:(nt + 1) * 512],
                                   start=True, stop=True, skip_group_check=True)
                    if nt == 3:
                        s_sc.inc(mm)
            te.wait_ge(s_exp.h, b_exp + NC)
            te.wait_ge(s_ust.h, b_ust + 3)
            for nt in range(4):
                for c in range(NC):
                    mm = te.matmul(PS_A[:, nt * 512:(nt + 1) * 512],
                                   Scol[:, c, 0:128], E[:, c, nt * 512:(nt + 1) * 512],
                                   start=(c == 0), stop=(c == NC - 1),
                                   skip_group_check=True)
                    if c == NC - 1:
                        s_t2.inc(mm)
                for c in range(NC):
                    mm = te.matmul(PS_C[0:40, nt * 512:(nt + 1) * 512],
                                   Scol[:, c, 128:168], E[:, c, nt * 512:(nt + 1) * 512],
                                   start=(c == 0), stop=(c == NC - 1),
                                   skip_group_check=True)
                    if c == NC - 1:
                        s_t2.inc(mm)
            te.wait_ge(s_tev.h, b_tev + 8)
            for grp in range(4):
                if grp >= 2:
                    te.wait_ge(s_ttv.h, b_ttv + 2 * (grp - 1))
                for j in range(4):
                    c = grp * 4 + j
                    mm = te.matmul(PS_A[:, (grp % 2) * 512:(grp % 2) * 512 + 512][:, j * 128:(j + 1) * 128],
                                   T2sb[:, c * 128:(c + 1) * 128], k.ident[:],
                                   start=True, stop=True, skip_group_check=True)
                    if j == 3:
                        s_ttm.inc(mm)
                for j in range(4):
                    c = grp * 4 + j
                    mm = te.matmul(PS_A[:, 1024 + (grp % 2) * 512:1024 + (grp % 2) * 512 + 512][:, j * 128:(j + 1) * 128],
                                   T01sb[0:40, c * 128:(c + 1) * 128], k.ident[0:40, :],
                                   start=True, stop=True, skip_group_check=True)
                    if j == 3:
                        s_ttm.inc(mm)
            te.wait_ge(s_rs.h, b_rs + 4)
            te.wait_ge(s_ttv.h, b_ttv + 8)
            for wave in range(2):
                src = Scol if wave == 0 else Acol
                for grp in range(4):
                    gi = wave * 4 + grp
                    if gi >= 2:
                        te.wait_ge(s_ttv.h, b_ttv + 8 + 2 * (gi - 1))
                    for j in range(4):
                        c = grp * 4 + j
                        mm = te.matmul(PS_A[:, (gi % 2) * 512 + j * 128:(gi % 2) * 512 + (j + 1) * 128],
                                       src[:, c, 0:128], k.ident[:],
                                       start=True, stop=True, skip_group_check=True)
                        if j == 3:
                            s_ttm.inc(mm)
                    for j in range(4):
                        c = grp * 4 + j
                        mm = te.matmul(PS_A[0:40, 1024 + (gi % 2) * 512 + j * 128:1024 + (gi % 2) * 512 + (j + 1) * 128],
                                       src[:, c, 128:168], k.ident[:],
                                       start=True, stop=True, skip_group_check=True)
                        if j == 3:
                            s_ttm.inc(mm)
            te.wait_ge(s_ttv.h, b_ttv + 24)
            for jc in range(NC):
                if jc > 0:
                    te.wait_ge(s_pm.h, b_pm + jc)
                for nt in range(4):
                    te.matmul(PS_A[:, nt * 512:(nt + 1) * 512],
                              S2T[:, jc * 128:(jc + 1) * 128],
                              A2T[:, nt * 512:(nt + 1) * 512],
                              start=True, stop=False, skip_group_check=True)
                    mm = te.matmul(PS_A[:, nt * 512:(nt + 1) * 512],
                                   S01T[0:40, jc * 128:(jc + 1) * 128],
                                   A01T[0:40, nt * 512:(nt + 1) * 512],
                                   start=False, stop=True, skip_group_check=True)
                    if nt == 3:
                        s_f.inc(mm)
                te.wait_ge(s_pm.h, b_pm + jc + 1)
                for nt in range(4):
                    mm = te.matmul(PS_C[:, nt * 512:(nt + 1) * 512],
                                   vh[:, jc, :], E[:, jc, nt * 512:(nt + 1) * 512],
                                   start=(jc == 0), stop=(jc == NC - 1),
                                   skip_group_check=True)
                    if nt == 3:
                        s_av.inc(mm)
            if hb == 64:
                te.wait_ge(s_oe.h, b_oe + 1)
                for nt in range(4):
                    mm = te.matmul(PS_A[:, nt * 512:(nt + 1) * 512], shid[:],
                                   stg[:, nt * 512:(nt + 1) * 512],
                                   start=True, stop=True, skip_group_check=True)
                    if nt == 3:
                        s_oe.inc(mm)

        @blk.scalar
        def _(sc):
            for wi in range(2):
                dst = qfh if wi == 0 else kfh
                bcol = bqc if wi == 0 else bkc
                for nt in range(4):
                    sc.wait_ge(s_mmq.h, b_mmq + 4 * wi + nt + 1)
                    s_evq.inc(sc.activation(dst[:, nt * 512:(nt + 1) * 512],
                                            PS_C[:, nt * 512:(nt + 1) * 512],
                                            AF.Identity,
                                            bias=bcol[hb:hb + 64, h // 2:h // 2 + 1]))
            for jc in range(NC):
                sc.wait_ge(s_sc.h, b_sc + jc + 1)
                s_exp.inc(sc.activation(E[:, jc, :], PS_A[:, :], AF.Exp, scale=0.125))

        ch = k.chainer(tag + "_hch")
        @blk.vector
        def _(v):
            v.wait_ge(s_mmv.h, b_mmv + NC)
            s_evv.inc(v.tensor_copy(vh[:].rearrange("p c e -> p (c e)"), PS_A[:, 0:1024]))
            for l in range(3):
                nb = NBL[l]
                toff = TOFF[l]
                if l >= 1:
                    v.wait_ge(s_t16.h, b_t16 + 32 * (l - 1) + 16)
                v.wait_ge(s_pp.h, b_pp + l + 1)
                v.tensor_copy(rstmp[:, 0:1024], PS_A[:, 0:1024])
                pp3 = rstmp[:, 0:1024].rearrange("p (c e) -> p c e", e=64)
                ch(v, v.tensor_reduce(pmax[:].unsqueeze(2), pp3, AX.X, A.max))
                v.tensor_tensor(ohb, pp3, pmax[:].unsqueeze(2).broadcast_to([128, 16, 64]),
                                A.is_equal)
                v.tensor_tensor(rstmp[:, 0:1024].rearrange("p (c e) -> p c e", e=64),
                                ohb,
                                iota64r[:].rearrange("p (o e) -> p o e", o=1)
                                .broadcast_to([128, 16, 64]), A.mult)
                ch(v, v.tensor_reduce(regf[:].unsqueeze(2),
                                      rstmp[:, 0:1024].rearrange("p (c e) -> p c e", e=64),
                                      AX.X, A.add))
                ch(v, v.tensor_copy(regi[:], regf[:]))
                ch(v, v.tensor_scalar(regi[:], regi[:], 9973, None, A.mult))
                ch(v, v.tensor_scalar(regi[:], regi[:], nb - 1, None, A.bitwise_and))
                ch(v, v.tensor_copy(bkt[:], regi[:]))
                v.tensor_copy(bkts[:, l, :], bkt[:])
                s_O.inc(v.tensor_tensor(
                    Obuf[:].rearrange("p c e -> p (c e)")[:, 0:16 * nb]
                    .rearrange("p (c e) -> p c e", e=nb),
                    iotan[:, 0:nb].rearrange("p (o e) -> p o e", o=1)
                    .broadcast_to([128, 16, nb]),
                    bkt[:].unsqueeze(2).broadcast_to([128, 16, nb]), A.is_equal))
                v.wait_ge(s_cnt.h, b_cnt + l + 1)
                if l == 2:
                    v.tensor_copy(tsa[0:1, 0, 0:512], PS_A[0:1, 0:512])
                s_ts.inc(v.tensor_copy(tstag[0:1, 0:16 * nb], PS_A[0:1, 0:16 * nb]))
                v.wait_ge(s_t16.h, b_t16 + 32 * (l + 1))
                v.tensor_copy(t16a[:, l, :], T16[:])
                v.wait_ge(s_pcm.h, b_pcm + 4 * l + 1)
                s_pcv.inc(v.tensor_copy(ccol[0:nb, :], PS_A[0:nb, 1200:1201]))
                v.wait_ge(s_pcm.h, b_pcm + 4 * l + 3)
                s_pcv.inc(v.tensor_copy(strow[0:1, 0:nb], PS_A[0:1, 1400:1400 + nb]))
                v.wait_ge(s_pcm.h, b_pcm + 4 * l + 4)
                s_pcv.inc(v.tensor_copy(carry[0:16, 0:nb], PS_A[0:16, 1600:1600 + nb]))
                for c in range(NC):
                    v.wait_ge(s_rp.h, b_rp + 16 * l + c + 1)
                    ch(v, v.tensor_tensor(rtmp[:, 0:nb],
                                          Obuf[:].rearrange("p c e -> p (c e)")[:, c * nb:(c + 1) * nb],
                                          PS_A[:, 1024:1024 + nb], A.mult))
                    s_rk.inc(v.tensor_reduce(ranks[:, c:c + 1].unsqueeze(2),
                                             rtmp[:, 0:nb].unsqueeze(1), AX.X, A.add))
                v.wait_ge(s_rk.h, b_rk + 16 * (l + 1))
                ch(v, v.tensor_copy(ranki[:], ranks[:]))
                ch(v, v.tensor_scalar(ranki[:], ranki[:], SHL[l], None, A.arith_shift_right))
                s_cv.inc(v.tensor_copy(cvals[:, l, :], ranki[:]))
                v.wait_ge(s_cv.h, b_cv + l + 1)
                v.tensor_tensor(
                    Ucol[:, :, toff:toff + nb],
                    iotan[:, 0:nb].rearrange("p (o e) -> p o e", o=1)
                    .broadcast_to([128, 16, nb]),
                    cvals[:, l, :].unsqueeze(2).broadcast_to([128, 16, nb]), A.is_equal)
                v.tensor_copy(Scol[:, :, toff:toff + nb], Ucol[:, :, toff:toff + nb])
                s_ust.inc(v.tensor_tensor(Scol[:, :, toff + 1:toff + nb],
                                          Scol[:, :, toff + 1:toff + nb],
                                          Ucol[:, :, toff:toff + nb - 1], A.add))
            for nt in range(4):
                v.wait_ge(s_t2.h, b_t2 + 2 * nt + 1)
                s_tev.inc(v.tensor_copy(T2sb[:, nt * 512:(nt + 1) * 512],
                                        PS_A[:, nt * 512:(nt + 1) * 512]))
                v.wait_ge(s_t2.h, b_t2 + 2 * nt + 2)
                s_tev.inc(v.tensor_copy(T01sb[0:40, nt * 512:(nt + 1) * 512],
                                        PS_C[0:40, nt * 512:(nt + 1) * 512]))
            for grp in range(4):
                v.wait_ge(s_ttm.h, b_ttm + 2 * grp + 1)
                s_ttv.inc(v.tensor_copy(
                    TTb[:, grp * 4:(grp + 1) * 4, 0:128],
                    PS_A[:, (grp % 2) * 512:(grp % 2) * 512 + 512]
                    .rearrange("p (j e) -> p j e", e=128)))
                v.wait_ge(s_ttm.h, b_ttm + 2 * grp + 2)
                s_ttv.inc(v.tensor_copy(
                    TTb[:, grp * 4:(grp + 1) * 4, 128:168],
                    PS_A[:, 1024 + (grp % 2) * 512:1024 + (grp % 2) * 512 + 512]
                    .rearrange("p (j e) -> p j e", e=128)[:, :, 0:40]))
            for l in range(3):
                nb = NBL[l]
                toff = TOFF[l]
                nh = 8 if nb == 128 else 16
                for half in range(2 if nb * 16 > 1024 else 1):
                    cs0 = half * 8
                    ncs = 8 if nb * 16 > 1024 else 16
                    dst = rsb[:, l, cs0:cs0 + ncs] if nb * 16 > 1024 else rsb[:, l, :]
                    v.tensor_tensor(rstmp[:, 0:ncs * nb].rearrange("p (c e) -> p c e", e=nb),
                                    Ucol[:, cs0:cs0 + ncs, toff:toff + nb],
                                    TTb[:, cs0:cs0 + ncs, toff:toff + nb], A.mult)
                    ch(v, v.tensor_reduce(dst.unsqueeze(2),
                                          rstmp[:, 0:ncs * nb].rearrange("p (c e) -> p c e", e=nb),
                                          AX.X, A.add))
                    v.tensor_tensor(rstmp[:, 0:ncs * (nb - 1)]
                                    .rearrange("p (c e) -> p c e", e=nb - 1),
                                    Ucol[:, cs0:cs0 + ncs, toff + 1:toff + nb],
                                    TTb[:, cs0:cs0 + ncs, toff:toff + nb - 1], A.mult)
                    ch(v, v.tensor_reduce(regf[:, cs0:cs0 + ncs].unsqueeze(2),
                                          rstmp[:, 0:ncs * (nb - 1)]
                                          .rearrange("p (c e) -> p c e", e=nb - 1),
                                          AX.X, A.add))
                    ch(v, v.tensor_tensor(dst, dst, regf[:, cs0:cs0 + ncs], A.add))
                    if nb * 16 <= 1024:
                        break
                ch(v, v.tensor_scalar(rsb[:, l, :], rsb[:, l, :], 3.0, None, A.mult))
                ch(v, v.reciprocal(rsb[:, l, :], rsb[:, l, :]))
                s_rs.inc(v.tensor_tensor(Acol[:, :, toff:toff + nb],
                                         Ucol[:, :, toff:toff + nb],
                                         rsb[:, l, :].unsqueeze(2)
                                         .broadcast_to([128, 16, nb]), A.mult))
            s_rs.inc(v.tensor_copy(rtmp[:, 0:1], rsb[:, 0, 0:1]))
            for wave in range(2):
                d2 = S2T if wave == 0 else A2T
                d01 = S01T if wave == 0 else A01T
                for grp in range(4):
                    gi = wave * 4 + grp
                    v.wait_ge(s_ttm.h, b_ttm + 8 + 2 * gi + 1)
                    s_ttv.inc(v.tensor_copy(
                        d2[:, grp * 512:(grp + 1) * 512],
                        PS_A[:, (gi % 2) * 512:(gi % 2) * 512 + 512]))
                    v.wait_ge(s_ttm.h, b_ttm + 8 + 2 * gi + 2)
                    s_ttv.inc(v.tensor_copy(
                        d01[0:40, grp * 512:(grp + 1) * 512],
                        PS_A[0:40, 1024 + (gi % 2) * 512:1024 + (gi % 2) * 512 + 512]))
            for jc in range(NC):
                v.wait_ge(s_f.h, b_f + jc + 1)
                s_pm.inc(v.tensor_tensor(E[:, jc, :], E[:, jc, :], PS_A[:, :], A.mult))
            v.wait_ge(s_av.h, b_av + NC)
            s_oe.inc(v.tensor_copy(stg[:], PS_C[0:64, :]))
            if hb == 0:
                s_oe.inc(v.tensor_copy(outT[0:64, h // 2, :], stg[:]))
            else:
                v.wait_ge(s_oe.h, b_oe + 2)
                s_oe.inc(v.tensor_copy(outT[64:128, h // 2, :], PS_A[64:128, :]))


def host_prep_l0attn(inputs):
    pr = np.ascontiguousarray(inputs["L0_attn_probes"].transpose(2, 0, 1, 3).reshape(64, 3 * 16 * 64))
    return {
        "a0_wq": inputs["L0_attn_Wq"], "a0_wk": inputs["L0_attn_Wk"],
        "a0_wv": inputs["L0_attn_Wv"], "a0_wo": inputs["L0_attn_Wo"],
        "a0_bqc": np.ascontiguousarray(inputs["L0_attn_bq"].reshape(8, 128).T),
        "a0_bkc": np.ascontiguousarray(inputs["L0_attn_bk"].reshape(8, 128).T),
        "a0_bvr": inputs["L0_attn_bv"].reshape(1, 1024),
        "a0_bor": inputs["L0_attn_bo"].reshape(1, 1024),
        "a0_probes": np.ascontiguousarray(np.tile(pr, (2, 1))),
    }


def build_l0attn_standalone():
    nc = bass.Bass("TRN2", target_bir_lowering=False, debug=False,
                   detect_race_conditions=False)
    k = K(nc)
    x = nc.dram_tensor("x", [T, D], F32, kind="ExternalInput").ap()
    y = nc.dram_tensor("y", [T, D], F32, kind="ExternalOutput").ap()
    declare_inputs(k, L0ATTN_SPECS)
    make_consts(k)
    xf = nc.alloc_sbuf_tensor("xf", [128, FC, T], BF16)
    s_ld = k.sem("xld")
    s_cv = k.sem("xcv")
    with ExitStack() as es:
        xr = es.enter_context(nc.sbuf_tensor("xr", [128, NC, D], F32))
        xb = es.enter_context(nc.sbuf_tensor("xb", [128, NC, D], BF16))
        with nc.Block() as blk:
            @blk.sync
            def _(sy):
                s_ld.inc(sy.dma_start(xr[:], x.rearrange("(c p) d -> p c d", p=128)), 16)
            @blk.vector
            def _(v):
                s_ld.wait(v)
                for c in range(NC):
                    s_cv.inc(v.tensor_copy(xb[:, c, :], xr[:, c, :]))
        emit_xf(k, xb, xf, "xf0")
    emit_l0attn(k, xf, x, y, tag="a0")
    return nc


def build_l0attn_debug():
    nc = bass.Bass("TRN2", target_bir_lowering=False, debug=False,
                   detect_race_conditions=False)
    k = K(nc)
    x = nc.dram_tensor("x", [T, D], F32, kind="ExternalInput").ap()
    y = nc.dram_tensor("y", [T, D], F32, kind="ExternalOutput").ap()
    dbg = nc.dram_tensor("dbg", [128, 4096], F32, kind="ExternalOutput").ap()
    declare_inputs(k, L0ATTN_SPECS)
    make_consts(k)
    xf = nc.alloc_sbuf_tensor("xf", [128, FC, T], BF16)
    s_ld = k.sem("xld")
    s_cv = k.sem("xcv")
    with ExitStack() as es:
        xr = es.enter_context(nc.sbuf_tensor("xr", [128, NC, D], F32))
        xb = es.enter_context(nc.sbuf_tensor("xb", [128, NC, D], BF16))
        with nc.Block() as blk:
            @blk.sync
            def _(sy):
                s_ld.inc(sy.dma_start(xr[:], x.rearrange("(c p) d -> p c d", p=128)), 16)
            @blk.vector
            def _(v):
                s_ld.wait(v)
                for c in range(NC):
                    s_cv.inc(v.tensor_copy(xb[:, c, :], xr[:, c, :]))
        emit_xf(k, xb, xf, "xf0")
    emit_l0attn(k, xf, x, y, tag="a0", dbg=dbg)
    return nc


# version 03
# speedup vs baseline: 1.0167x; 1.0167x over previous
import numpy as np
import ml_dtypes
from contextlib import ExitStack
import concourse.bass as bass
import concourse.mybir as mybir
from concourse.alu_op_type import AluOpType as A
from concourse.bass_utils import run_bass_kernel_spmd

F32 = mybir.dt.float32
BF16 = mybir.dt.bfloat16
I32 = mybir.dt.int32
AF = mybir.ActivationFunctionType
AX = mybir.AxisListType

T = 2048
D = 1024
NC = 16
FC = 8
DH = 4096
BF = ml_dtypes.bfloat16


class Sem:
    def __init__(self, nc, name):
        self.h = nc.alloc_semaphore(name)
        self.n = 0

    def inc(self, instr, k=1):
        instr.then_inc(self.h, k)
        self.n += k
        return instr

    def wait(self, eng, n=None):
        eng.wait_ge(self.h, self.n if n is None else n)


class K:
    def __init__(self, nc):
        self.nc = nc
        self.sems = {}
        self.inp = {}

    def sem(self, name):
        if name not in self.sems:
            self.sems[name] = Sem(self.nc, name)
        return self.sems[name]

    def chainer(self, name):
        s = self.sem(name)
        def ch(eng, instr):
            s.inc(instr)
            eng.wait_ge(s.h, s.n)
            return instr
        return ch


def make_consts(k):
    nc = k.nc
    k.ident = nc.alloc_sbuf_tensor("c_ident", [128, 128], BF16)
    k.identf = nc.alloc_sbuf_tensor("c_identf", [128, 128], F32)
    k.ones_col = nc.alloc_sbuf_tensor("c_ones_col", [128, 1], BF16)
    k.ones_colf = nc.alloc_sbuf_tensor("c_ones_colf", [128, 1], F32)
    k.ones_row = nc.alloc_sbuf_tensor("c_ones_row", [1, 128], BF16)
    k.ones_rowf = nc.alloc_sbuf_tensor("c_ones_rowf", [1, 128], F32)
    k.eps_col = nc.alloc_sbuf_tensor("c_eps_col", [128, 1], F32)
    s = k.sem("consts")
    with nc.Block() as blk:
        @blk.gpsimd
        def _(g):
            g.iota(k.identf[:], pattern=[[1, 128]], base=0, channel_multiplier=-1,
                   allow_small_or_imprecise_dtypes=True)
            g.memset(k.ones_col[:], 1.0)
            g.memset(k.ones_row[:], 1.0)
            g.memset(k.ones_rowf[:], 1.0)
            g.memset(k.eps_col[:], 1e-5)
            s.inc(g.memset(k.ones_colf[:], 1.0))
        @blk.vector
        def _(v):
            s.wait(v)
            s.inc(v.tensor_scalar(k.ident[:], k.identf[:], 0.0, None, A.is_equal))
            s.inc(v.tensor_scalar(k.identf[:], k.identf[:], 0.0, None, A.is_equal))


_XFC = [0]


def emit_xf(k, xb, xf, tag):
    _XFC[0] += 1
    uniq = f"xfu{_XFC[0]}"
    tag = "xfc"
    """xb [128,16,1024] bf16 token-major -> xf [128,8,2048] bf16 feature-major."""
    nc = k.nc
    s_mm = k.sem(tag + "_mm")
    s_ev = k.sem(tag + "_ev")
    b_mm, b_ev = s_mm.n, s_ev.n
    with ExitStack() as es:
        ps = [es.enter_context(nc.psum_tensor(f"{uniq}_ps{i}", [128, 512], F32)) for i in range(2)]
        with nc.Block() as blk:
            @blk.tensor
            def _(te):
                grp = 0
                for fc in range(FC):
                    for cb in range(4):
                        buf = ps[grp % 2]
                        if grp >= 2:
                            te.wait_ge(s_ev.h, b_ev + grp - 1)
                        for j in range(4):
                            c = cb * 4 + j
                            mm = te.matmul(buf[:, j * 128:(j + 1) * 128],
                                           xb[:, c, fc * 128:(fc + 1) * 128],
                                           k.ident[:], start=True, stop=True)
                            if j == 3:
                                s_mm.inc(mm)
                        grp += 1
            @blk.vector
            def _(v):
                grp = 0
                for fc in range(FC):
                    for cb in range(4):
                        buf = ps[grp % 2]
                        v.wait_ge(s_mm.h, b_mm + grp + 1)
                        s_ev.inc(v.tensor_copy(xf[:, fc, cb * 512:(cb + 1) * 512], buf[:]))
                        grp += 1


def emit_ln(k, xr, tag, y_dram=None, xb=None):
    nc = k.nc
    s_sum = k.sem(tag + "_sum")
    s_sq = k.sem(tag + "_sq")
    s_stat = k.sem(tag + "_stat")
    s_app = k.sem(tag + "_app")
    s_bf = k.sem(tag + "_bf")
    s_st = k.sem(tag + "_st")
    with ExitStack() as es:
        junk = es.enter_context(nc.sbuf_tensor(tag + "_junk", [128, 1024], BF16))
        ssum = es.enter_context(nc.sbuf_tensor(tag + "_ssum", [128, 16], F32))
        ssq = es.enter_context(nc.sbuf_tensor(tag + "_ssq", [128, 16], F32))
        mean = es.enter_context(nc.sbuf_tensor(tag + "_mean", [128, 16], F32))
        var = es.enter_context(nc.sbuf_tensor(tag + "_var", [128, 16], F32))
        rstd = es.enter_context(nc.sbuf_tensor(tag + "_rstd", [128, 16], F32))
        nm = es.enter_context(nc.sbuf_tensor(tag + "_nm", [128, 16], F32))
        with nc.Block() as blk:
            @blk.vector
            def _(v):
                for c in range(NC):
                    s_sum.inc(v.tensor_reduce(
                        ssum[:, c:c + 1].unsqueeze(2),
                        xr[:, c:c + 1, :], AX.X, A.add))
            @blk.scalar
            def _(sc):
                for c in range(NC):
                    s_sq.inc(sc.activation(junk[:], xr[:, c, :], AF.Square,
                                           accum_out=ssq[:, c:c + 1]))
            @blk.vector
            def _(v):
                s_sum.wait(v)
                s_sq.wait(v)
                v.tensor_scalar(mean[:], ssum[:], 1.0 / D, None, A.mult)
                v.tensor_scalar(var[:], ssq[:], 1.0 / D, None, A.mult)
                v.tensor_tensor(nm[:], mean[:], mean[:], A.mult)
                s_stat.inc(v.tensor_tensor(var[:], var[:], nm[:], A.subtract))
            @blk.scalar
            def _(sc):
                s_stat.wait(sc)
                s_stat.inc(sc.activation(var[:], var[:], AF.Sqrt, bias=k.eps_col[:]))
            @blk.vector
            def _(v):
                s_stat.wait(v)
                v.reciprocal(rstd[:], var[:])
                v.tensor_tensor(nm[:], mean[:], rstd[:], A.mult)
                s_stat.inc(v.tensor_scalar(nm[:], nm[:], -1.0, None, A.mult))
            @blk.scalar
            def _(sc):
                s_stat.wait(sc)
                for c in range(NC):
                    s_app.inc(sc.activation(xr[:, c, :], xr[:, c, :], AF.Identity,
                                            bias=nm[:, c:c + 1], scale=rstd[:, c:c + 1]))
            if xb is not None:
                @blk.vector
                def _(v):
                    s_app.wait(v)
                    for c in range(NC):
                        s_bf.inc(v.tensor_copy(xb[:, c, :], xr[:, c, :]))
            if y_dram is not None:
                @blk.sync
                def _(sy):
                    s_app.wait(sy)
                    yv = y_dram.rearrange("(c p) d -> p c d", p=128)
                    s_st.inc(sy.dma_start(yv, xr[:]), 16)
                    s_st.wait(sy)


def emit_l1ffn(k, xr_dram, xf, y_dram, dbg_upto=None, dbg_out=None):
    nc = k.nc
    w1t, w2t = k.inp["l1_w1t"], k.inp["l1_w2t"]
    h1t, h2t = k.inp["l1_h1t"], k.inp["l1_h2t"]
    w3t = k.inp["l1_w3t"]
    HT = T // 2
    NOC = DH // 128

    xrv = xr_dram.rearrange("(c p) d -> p c d", p=128)
    yv = y_dram.rearrange("(c p) d -> p c d", p=128)

    s_sc = k.sem("ff_sc")
    s_s3 = k.sem("ff_s3")

    with ExitStack() as es:
        Aact = es.enter_context(nc.sbuf_tensor("ff_Aact", [128, NOC, HT], BF16))
        Bact = es.enter_context(nc.sbuf_tensor("ff_Bact", [128, NOC, HT], BF16))
        wbig = es.enter_context(nc.sbuf_tensor("ff_wbig", [128, 8192], BF16))
        wbA = [wbig[:, s * 1024:(s + 1) * 1024].rearrange("p (ic o) -> p ic o", o=128)
               for s in range(4)]
        wbH = [wbig[:, par * 4096:(par + 1) * 4096].rearrange("p (ic o) -> p ic o", o=128)
               for par in range(2)]
        wb3 = [wbig[:, par * 1024:(par + 1) * 1024] for par in range(2)]
        scols = es.enter_context(nc.sbuf_tensor("ff_scols", [128, 4, 32], F32))
        s3rep = es.enter_context(nc.sbuf_tensor("ff_s3rep", [128, 1024], F32))
        silu1 = es.enter_context(nc.sbuf_tensor("ff_silu1", [128, 512, 4], BF16))
        xrsl = es.enter_context(nc.sbuf_tensor("ff_xrsl", [128, 2, 1024], F32))
        vch = es.enter_context(nc.sbuf_tensor("ff_vch", [128, 2, 1024], F32))
        junkv = es.enter_context(nc.sbuf_tensor("ff_junkv", [128, 1024], F32))
        junkc = es.enter_context(nc.sbuf_tensor("ff_junkc", [128, 1], F32))
        statarch = es.enter_context(nc.sbuf_tensor("ff_statarch", [128, 16, 8], F32))
        lst = es.enter_context(nc.sbuf_tensor("ff_lst", [128, 2, 8], F32))
        psA = [es.enter_context(nc.psum_tensor(f"ff_ps{i}", [128, 512], F32)) for i in range(8)]

        with nc.Block() as blk0:
            @blk0.sync
            def _(sy):
                s_sc.inc(sy.dma_start(scols[:, 0, :], k.inp["l1_s1pc"]), 16)
                s_sc.inc(sy.dma_start(scols[:, 1, :], k.inp["l1_s2pc"]), 16)
                s_sc.inc(sy.dma_start(scols[:, 2, :], k.inp["l1_sh1pc"]), 16)
                s_sc.inc(sy.dma_start(scols[:, 3, :], k.inp["l1_sh2pc"]), 16)
                s_sc.inc(sy.dma_start(s3rep[0:1, :], k.inp["l1_s3row"]), 16)
            @blk0.tensor
            def _(te):
                s_sc.wait(te)
                s_s3.inc(te.matmul(psA[0][:], k.ones_rowf[:], s3rep[0:1, 0:512],
                                   start=True, stop=True))
                s_s3.inc(te.matmul(psA[1][:], k.ones_rowf[:], s3rep[0:1, 512:1024],
                                   start=True, stop=True))
            @blk0.vector
            def _(v):
                s_s3.wait(v)
                v.tensor_copy(s3rep[:, 0:512], psA[0][:])
                s_s3.inc(v.tensor_copy(s3rep[:, 512:1024], psA[1][:]))

        halves = [0] if dbg_upto is not None and dbg_upto < 4 else [0, 1]
        for H in halves:
            nbase = H * HT
            s_wl = k.sem("ffA_wl"); bwl = s_wl.n
            s_mmA = k.sem("ffA_mm"); bmmA = s_mmA.n
            s_sil = k.sem("ffA_sil"); bsil = s_sil.n
            s_evA = k.sem("ffA_ev"); bevA = s_evA.n

            with nc.Block() as blk:
                @blk.gpsimd
                def _(g):
                    for oc in range(NOC):
                        par = oc % 2
                        for wi, wt in ((0, w1t), (1, w2t)):
                            if oc >= 2:
                                g.wait_ge(s_mmA.h, bmmA + 4 * (oc - 2) + 2 * (wi + 1))
                            wv = wt.rearrange("(ic p) o -> p ic o", p=128)[:, :, oc * 128:(oc + 1) * 128]
                            s_wl.inc(g.dma_start(wbA[2 * par + wi], wv), 16)
                @blk.tensor
                def _(te):
                    for oc in range(NOC):
                        par = oc % 2
                        te.wait_ge(s_wl.h, bwl + (2 * oc + 2) * 16)
                        if oc >= 2:
                            te.wait_ge(s_sil.h, bsil + 2 * (oc - 2) + 2)
                            te.wait_ge(s_evA.h, bevA + 2 * (oc - 2) + 2)
                        for wi in range(2):
                            for nt in range(2):
                                bank = psA[4 * par + 2 * wi + nt]
                                for ic in range(FC):
                                    mm = te.matmul(bank[:], wbA[2 * par + wi][:, ic, :],
                                                   xf[:, ic, nbase + nt * 512: nbase + (nt + 1) * 512],
                                                   start=(ic == 0), stop=(ic == FC - 1))
                                    if ic == FC - 1:
                                        s_mmA.inc(mm)
                @blk.scalar
                def _(sc):
                    for oc in range(NOC):
                        par = oc % 2
                        for nt in range(2):
                            sc.wait_ge(s_mmA.h, bmmA + 4 * oc + nt + 1)
                            if oc >= 2:
                                sc.wait_ge(s_evA.h, bevA + 2 * (oc - 2) + nt + 1)
                            s_sil.inc(sc.activation(silu1[:, :, 2 * par + nt],
                                                    psA[4 * par + nt][:], AF.Silu,
                                                    scale=scols[:, 0, oc:oc + 1]))
                @blk.vector
                def _(v):
                    for oc in range(NOC):
                        par = oc % 2
                        for nt in range(2):
                            v.wait_ge(s_mmA.h, bmmA + 4 * oc + 2 + nt + 1)
                            v.wait_ge(s_sil.h, bsil + 2 * oc + nt + 1)
                            s_evA.inc(v.scalar_tensor_tensor(
                                Aact[:, oc, nt * 512:(nt + 1) * 512],
                                psA[4 * par + 2 + nt][:], scols[:, 1, oc:oc + 1],
                                silu1[:, :, 2 * par + nt], A.mult, A.mult))

            if dbg_upto == 0:
                sdbg = k.sem("dbg_st")
                with nc.Block() as blkd:
                    @blkd.sync
                    def _(sy):
                        sdbg.inc(sy.dma_start(dbg_out, Aact[:]), 16)
                        sdbg.wait(sy)
                return
            for stage in range(2):
                ht = h1t if stage == 0 else h2t
                src = Aact if stage == 0 else Bact
                dst = Bact if stage == 0 else Aact
                s_wlH = k.sem("ffH_wl"); bwlH = s_wlH.n
                s_mmH = k.sem("ffH_mm"); bmmH = s_mmH.n
                s_evH = k.sem("ffH_ev"); bevH = s_evH.n
                with nc.Block() as blk:
                    @blk.gpsimd
                    def _(g):
                        for oc in range(NOC):
                            par = oc % 2
                            if oc >= 2:
                                g.wait_ge(s_mmH.h, bmmH + 2 * (oc - 2) + 2)
                            wv = ht.rearrange("(ic p) o -> p ic o", p=128)[:, :, oc * 128:(oc + 1) * 128]
                            for q in range(4):
                                s_wlH.inc(g.dma_start(wbH[par][:, q * 8:(q + 1) * 8, :],
                                                      wv[:, q * 8:(q + 1) * 8, :]), 16)
                    @blk.tensor
                    def _(te):
                        for oc in range(NOC):
                            par = oc % 2
                            te.wait_ge(s_wlH.h, bwlH + (4 * oc + 4) * 16)
                            if oc >= 2:
                                te.wait_ge(s_evH.h, bevH + 2 * (oc - 2) + 2)
                            for nt in range(2):
                                bank = psA[4 * par + nt]
                                for ic in range(32):
                                    mm = te.matmul(bank[:], wbH[par][:, ic, :],
                                                   src[:, ic, nt * 512:(nt + 1) * 512],
                                                   start=(ic == 0), stop=(ic == 31))
                                    if ic == 31:
                                        s_mmH.inc(mm)
                    @blk.scalar
                    def _(sc):
                        for oc in range(NOC):
                            par = oc % 2
                            for nt in range(2):
                                sc.wait_ge(s_mmH.h, bmmH + 2 * oc + nt + 1)
                                s_evH.inc(sc.activation(dst[:, oc, nt * 512:(nt + 1) * 512],
                                                        psA[4 * par + nt][:], AF.Gelu,
                                                        scale=scols[:, 2 + stage, oc:oc + 1]))

                if dbg_upto == 1 + stage:
                    sdbg = k.sem("dbg_st")
                    with nc.Block() as blkd:
                        @blkd.sync
                        def _(sy):
                            sdbg.inc(sy.dma_start(dbg_out, dst[:]), 16)
                            sdbg.wait(sy)
                    return
            Cact = Aact
            s_wl3 = k.sem("ff3_wl"); bwl3 = s_wl3.n
            s_mm3 = k.sem("ff3_mm"); bmm3 = s_mm3.n
            s_xr = k.sem("ff3_xr"); bxr = s_xr.n
            s_v = k.sem("ff3_v"); bv3 = s_v.n
            s_l3 = k.sem("ff3_l3"); bl3 = s_l3.n
            s_l4 = k.sem("ff3_l4"); bl4 = s_l4.n
            s_l6 = k.sem("ff3_l6"); bl6 = s_l6.n
            s_st = k.sem("ff3_st"); bst = s_st.n
            with nc.Block() as blk:
                @blk.gpsimd
                def _(g):
                    for tsb in range(2):
                        for ic in range(32):
                            gi = tsb * 32 + ic
                            if gi >= 2:
                                g.wait_ge(s_mm3.h, bmm3 + gi - 1)
                            s_wl3.inc(g.dma_start(
                                wb3[gi % 2],
                                w3t.rearrange("(ic p) e -> p ic e", p=128)[:, ic, :]), 16)
                @blk.sync
                def _(sy):
                    for li in range(8):
                        c = H * 8 + li
                        if li >= 2:
                            sy.wait_ge(s_v.h, bv3 + 2 * (li - 1))
                        s_xr.inc(sy.dma_start(xrsl[:, li % 2, :], xrv[:, c, :]), 16)
                @blk.tensor
                def _(te):
                    for tsb in range(2):
                        if tsb == 1:
                            te.wait_ge(s_v.h, bv3 + 8)
                        for ic in range(32):
                            gi = tsb * 32 + ic
                            te.wait_ge(s_wl3.h, bwl3 + (gi + 1) * 16)
                            for ts in range(4):
                                for et in range(2):
                                    bank = psA[2 * ts + et]
                                    mm = te.matmul(
                                        bank[:],
                                        Cact[:, ic, (tsb * 4 + ts) * 128:(tsb * 4 + ts + 1) * 128],
                                        wb3[gi % 2][:, et * 512:(et + 1) * 512],
                                        start=(ic == 0), stop=(ic == 31),
                                        skip_group_check=True)
                                    if ts == 3 and et == 1:
                                        s_mm3.inc(mm)
                @blk.vector
                def _(v):
                    for li in range(8):
                        tsb, ts = li // 4, li % 4
                        pa = li % 2
                        v.wait_ge(s_xr.h, bxr + (li + 1) * 16)
                        if li >= 2:
                            v.wait_ge(s_st.h, bst + (li - 1) * 16)
                        v.wait_ge(s_mm3.h, bmm3 + 32 * (tsb + 1))
                        for et in range(2):
                            bank = psA[2 * ts + et]
                            v.tensor_tensor(vch[:, pa, et * 512:(et + 1) * 512],
                                            bank[:], s3rep[:, et * 512:(et + 1) * 512],
                                            A.mult)
                            s_v.inc(v.tensor_tensor(
                                vch[:, pa, et * 512:(et + 1) * 512],
                                vch[:, pa, et * 512:(et + 1) * 512],
                                xrsl[:, pa, et * 512:(et + 1) * 512], A.add))
                        if dbg_upto == 3:
                            continue
                        v.tensor_reduce(lst[:, pa:pa + 1, 0:1],
                                        vch[:, pa:pa + 1, :], AX.X, A.add)
                        v.tensor_tensor(junkv[:], vch[:, pa, :], vch[:, pa, :], A.mult)
                        v.tensor_reduce(lst[:, pa:pa + 1, 1:2],
                                        junkv[:].unsqueeze(1), AX.X, A.add)
                        v.tensor_scalar(lst[:, pa, 2:3], lst[:, pa, 0:1], 1.0 / D, None, A.mult)
                        v.tensor_scalar(lst[:, pa, 3:4], lst[:, pa, 1:2], 1.0 / D, None, A.mult)
                        v.tensor_tensor(lst[:, pa, 4:5], lst[:, pa, 2:3], lst[:, pa, 2:3], A.mult)
                        v.tensor_tensor(lst[:, pa, 3:4], lst[:, pa, 3:4],
                                        lst[:, pa, 4:5], A.subtract)
                        s_l3.inc(v.tensor_copy(lst[:, pa, 7:8], lst[:, pa, 3:4]))
                        v.wait_ge(s_l4.h, bl4 + 2 * (li + 1))
                        v.reciprocal(lst[:, pa, 5:6], lst[:, pa, 4:5])
                        s_l6.inc(v.tensor_scalar(vch[:, pa, :], vch[:, pa, :],
                                                 lst[:, pa, 2:3], lst[:, pa, 5:6],
                                                 A.subtract, A.mult))
                        if dbg_upto == 4:
                            v.tensor_copy(statarch[:, H * 8 + li, :], lst[:, pa, :])
                @blk.scalar
                def _(sc):
                    for li in range(8):
                        pa = li % 2
                        if dbg_upto == 3:
                            sc.wait_ge(s_v.h, 2 * (li + 1))
                            s_st.inc(sc.dma_start(yv[:, H * 8 + li, :], vch[:, pa, :]), 16)
                            continue
                        sc.wait_ge(s_l3.h, bl3 + li + 1)
                        sc.activation(lst[:, pa, 4:5], lst[:, pa, 7:8],
                                      AF.Sqrt, bias=k.eps_col[:])
                        s_l4.inc(sc.activation(junkc[:], lst[:, pa, 7:8],
                                               AF.Sqrt, bias=k.eps_col[:]), 2)
                        sc.wait_ge(s_l6.h, bl6 + li + 1)
                        s_st.inc(sc.dma_start(yv[:, H * 8 + li, :], vch[:, pa, :]), 16)
                    sc.wait_ge(s_st.h, bst + 8 * 16)
            if dbg_upto == 4 and H == 1:
                with nc.Block() as blkd2:
                    @blkd2.sync
                    def _(sy):
                        sdbg2 = k.sem("dbg_st2")
                        sdbg2.inc(sy.dma_start(dbg_out[:, 0:1, 0:256].bitcast(F32).rearrange("p a (b c) -> p (a b) c", c=8), statarch[:]), 16)
                        sdbg2.wait(sy)


def host_prep_l0ffn(inputs):
    return {
        "f0_win": inputs["L0_ffn_Win"],
        "f0_binc": inputs["L0_ffn_bin_"].reshape(64, 1),
        "f0_w1": inputs["L0_ffn_W1"],
        "f0_b1": inputs["L0_ffn_b1"],
        "f0_w2": inputs["L0_ffn_W2"],
        "f0_b2c": inputs["L0_ffn_b2"].reshape(64, 1),
        "f0_wout": inputs["L0_ffn_Wout"],
        "f0_bout": inputs["L0_ffn_bout"].reshape(1, 1024),
    }


def host_prep_l1attn(inputs):
    return {"l1_wq": inputs["L1_attn_Wq"], "l1_wk": inputs["L1_attn_Wk"],
            "l1_wv": inputs["L1_attn_Wv"], "l1_wo": inputs["L1_attn_Wo"]}


def host_prep_l1ffn(inputs):
    p = {}
    p["l1_w1t"] = np.ascontiguousarray(inputs["L1_ffn_w1_w"].T)
    p["l1_w2t"] = np.ascontiguousarray(inputs["L1_ffn_w2_w"].T)
    p["l1_h1t"] = np.ascontiguousarray(inputs["L1_ffn_h1_w"].T)
    p["l1_h2t"] = np.ascontiguousarray(inputs["L1_ffn_h2_w"].T)
    p["l1_w3t"] = np.ascontiguousarray(inputs["L1_ffn_w3_w"].T)
    for nm, src in (("l1_s1pc", "L1_ffn_w1_s"), ("l1_s2pc", "L1_ffn_w2_s"),
                    ("l1_sh1pc", "L1_ffn_h1_s"), ("l1_sh2pc", "L1_ffn_h2_s")):
        p[nm] = np.ascontiguousarray(inputs[src].reshape(32, 128).T)
    p["l1_s3row"] = np.ascontiguousarray(inputs["L1_ffn_w3_s"].reshape(1, 1024))
    return p


L1FFN_SPECS = [
    ("l1_w1t", [D, DH]), ("l1_w2t", [D, DH]),
    ("l1_h1t", [DH, DH]), ("l1_h2t", [DH, DH]),
    ("l1_w3t", [DH, D]),
    ("l1_s1pc", [128, 32]), ("l1_s2pc", [128, 32]),
    ("l1_sh1pc", [128, 32]), ("l1_sh2pc", [128, 32]),
    ("l1_s3row", [1, 1024]),
]


def declare_inputs(k, specs):
    for nm, shape in specs:
        k.inp[nm] = k.nc.dram_tensor(nm, shape, F32, kind="ExternalInput").ap()


def build_l1ffn_standalone():
    nc = bass.Bass("TRN2", target_bir_lowering=False, debug=False,
                   detect_race_conditions=False)
    k = K(nc)
    x = nc.dram_tensor("x", [T, D], F32, kind="ExternalInput").ap()
    y = nc.dram_tensor("y", [T, D], F32, kind="ExternalOutput").ap()
    declare_inputs(k, L1FFN_SPECS)
    make_consts(k)
    xf = nc.alloc_sbuf_tensor("xf", [128, FC, T], BF16)
    s_ld = k.sem("xld")
    s_cv = k.sem("xcv")
    with ExitStack() as es:
        xr = es.enter_context(nc.sbuf_tensor("xr", [128, NC, D], F32))
        xb = es.enter_context(nc.sbuf_tensor("xb", [128, NC, D], BF16))
        with nc.Block() as blk:
            @blk.sync
            def _(sy):
                s_ld.inc(sy.dma_start(xr[:], x.rearrange("(c p) d -> p c d", p=128)), 16)
            @blk.vector
            def _(v):
                s_ld.wait(v)
                for c in range(NC):
                    s_cv.inc(v.tensor_copy(xb[:, c, :], xr[:, c, :]))
        emit_xf(k, xb, xf, "xf0")
    emit_l1ffn(k, x, xf, y)
    return nc


def build_l1ffn_debug(upto):
    nc = bass.Bass("TRN2", target_bir_lowering=False, debug=False,
                   detect_race_conditions=False)
    k = K(nc)
    x = nc.dram_tensor("x", [T, D], F32, kind="ExternalInput").ap()
    y = nc.dram_tensor("y", [T, D], F32, kind="ExternalOutput").ap()
    dbg = nc.dram_tensor("dbg", [128, 32 * 1024], BF16, kind="ExternalOutput").ap()
    declare_inputs(k, L1FFN_SPECS)
    make_consts(k)
    xf = nc.alloc_sbuf_tensor("xf", [128, FC, T], BF16)
    s_ld = k.sem("xld")
    s_cv = k.sem("xcv")
    with ExitStack() as es:
        xr = es.enter_context(nc.sbuf_tensor("xr", [128, NC, D], F32))
        xb = es.enter_context(nc.sbuf_tensor("xb", [128, NC, D], BF16))
        with nc.Block() as blk:
            @blk.sync
            def _(sy):
                s_ld.inc(sy.dma_start(xr[:], x.rearrange("(c p) d -> p c d", p=128)), 16)
            @blk.vector
            def _(v):
                s_ld.wait(v)
                for c in range(NC):
                    s_cv.inc(v.tensor_copy(xb[:, c, :], xr[:, c, :]))
        emit_xf(k, xb, xf, "xf0")
    emit_l1ffn(k, x, xf, y, dbg_upto=upto, dbg_out=dbg.rearrange("p (a b) -> p a b", b=1024))
    return nc

_PROGS = {}


def _get_prog(name, builder):
    if name not in _PROGS:
        _PROGS[name] = builder()
    return _PROGS[name]


def kernel_multilaunch(**inputs):
    x = np.ascontiguousarray(inputs["L0_attn_x"])
    p_a0 = host_prep_l0attn(inputs)
    p_f0 = host_prep_l0ffn(inputs)
    p_a1 = host_prep_l1attn(inputs)
    p_f1 = host_prep_l1ffn(inputs)
    progs = [
        ("a0", build_l0attn_standalone, p_a0),
        ("f0", build_l0ffn_standalone, p_f0),
        ("a1", build_l1attn_standalone, p_a1),
        ("f1", build_l1ffn_standalone, p_f1),
    ]
    cur = [np.ascontiguousarray(x[c]) for c in range(8)]
    for name, builder, pp in progs:
        nc = _get_prog(name, builder)
        in_maps = []
        for c in range(8):
            m = {"x": cur[c]}
            m.update(pp)
            in_maps.append(m)
        res = run_bass_kernel_spmd(nc, in_maps, core_ids=list(range(8)))
        cur = [np.ascontiguousarray(res.results[c]["y"]) for c in range(8)]
    return np.stack(cur, axis=0)


L1ATTN_SPECS = [
    ("l1_wq", [D, D]), ("l1_wk", [D, D]), ("l1_wv", [D, D]), ("l1_wo", [D, D]),
]




def emit_tail_bodies(k, tag, blk, *, vch, xrsl, junkv, junkc, lst, xb, xrv_in,
                     xrv_out, ps_slice, s_mm, b_mm, mm_target, s_v, s_st_final=True):
    tag = "tlc"
    """Adds sync/vector/scalar bodies to blk implementing: per token chunk c:
    v = psum + xr_in[c]; LN(v) -> xr_out[c] (dram) and xb[:, c, :] (bf16).
    ps_slice(c, et) -> psum AP [128,512]; mm_target(c, et) -> absolute s_mm count.
    Caller's tensor body must gate psum reuse on s_v (2 incs per chunk)."""
    nc = k.nc
    s_l3 = k.sem(tag + "_tl3")
    s_l4 = k.sem(tag + "_tl4")
    s_l6 = k.sem(tag + "_tl6")
    s_st = k.sem(tag + "_tst")
    b_v, b_l3, b_l4, b_l6, b_st = s_v.n, s_l3.n, s_l4.n, s_l6.n, s_st.n
    s_xr = k.sem(tag + "_txr")
    b_xr = s_xr.n

    @blk.sync
    def _(sy):
        for c in range(NC):
            if c >= 2:
                sy.wait_ge(s_v.h, b_v + 2 * (c - 1))
            s_xr.inc(sy.dma_start(xrsl[:, c % 2, :], xrv_in[:, c, :]), 16)

    @blk.vector
    def _(v):
        for c in range(NC):
            pa = c % 2
            v.wait_ge(s_xr.h, b_xr + (c + 1) * 16)
            if c >= 2:
                v.wait_ge(s_st.h, b_st + (c - 1) * 16)
            for et in range(2):
                v.wait_ge(s_mm.h, b_mm + mm_target(c, et))
                s_v.inc(v.tensor_tensor(vch[:, pa, et * 512:(et + 1) * 512],
                                        ps_slice(c, et),
                                        xrsl[:, pa, et * 512:(et + 1) * 512], A.add))
            chv = k.chainer(tag + "_chv")
            ch2 = lambda i: chv(v, i)
            ch2(v.tensor_reduce(lst[:, pa:pa + 1, 0:1], vch[:, pa:pa + 1, :], AX.X, A.add))
            v.tensor_tensor(junkv[:], vch[:, pa, :], vch[:, pa, :], A.mult)
            ch2(v.tensor_reduce(lst[:, pa:pa + 1, 1:2], junkv[:].unsqueeze(1), AX.X, A.add))
            ch2(v.tensor_scalar(lst[:, pa, 2:3], lst[:, pa, 0:1], 1.0 / D, None, A.mult))
            ch2(v.tensor_scalar(lst[:, pa, 3:4], lst[:, pa, 1:2], 1.0 / D, None, A.mult))
            ch2(v.tensor_tensor(lst[:, pa, 4:5], lst[:, pa, 2:3], lst[:, pa, 2:3], A.mult))
            ch2(v.tensor_tensor(lst[:, pa, 3:4], lst[:, pa, 3:4], lst[:, pa, 4:5], A.subtract))
            s_l3.inc(v.tensor_copy(lst[:, pa, 7:8], lst[:, pa, 3:4]))
            v.wait_ge(s_l4.h, b_l4 + 2 * (c + 1))
            ch2(v.reciprocal(lst[:, pa, 5:6], lst[:, pa, 4:5]))
            v.tensor_scalar(vch[:, pa, :], vch[:, pa, :],
                            lst[:, pa, 2:3], lst[:, pa, 5:6], A.subtract, A.mult)
            if xb is not None:
                s_l6.inc(v.tensor_copy(xb[:, c, :], vch[:, pa, :]))
            else:
                s_l6.inc(v.tensor_copy(junkv[:, 0:1], lst[:, pa, 5:6]))

    @blk.scalar
    def _(sc):
        for c in range(NC):
            pa = c % 2
            sc.wait_ge(s_l3.h, b_l3 + c + 1)
            sc.activation(lst[:, pa, 4:5], lst[:, pa, 7:8], AF.Sqrt, bias=k.eps_col[:])
            s_l4.inc(sc.activation(junkc[:], lst[:, pa, 7:8], AF.Sqrt,
                                   bias=k.eps_col[:]), 2)
            sc.wait_ge(s_l6.h, b_l6 + c + 1)
            s_st.inc(sc.dma_start(xrv_out[:, c, :], vch[:, pa, :]), 16)
        if s_st_final:
            sc.wait_ge(s_st.h, b_st + NC * 16)


def emit_l1attn(k, xf, xr_in_dram, xr_out_dram, tag="a1"):
    nc = k.nc
    wq, wk = k.inp["l1_wq"], k.inp["l1_wk"]
    wv, wo = k.inp["l1_wv"], k.inp["l1_wo"]
    xrv_in = xr_in_dram.rearrange("(c p) d -> p c d", p=128)
    xrv_out = xr_out_dram.rearrange("(c p) d -> p c d", p=128)

    with ExitStack() as es:
        xb = es.enter_context(nc.sbuf_tensor(tag + "_xb", [128, NC, D], BF16))
        outT = es.enter_context(nc.sbuf_tensor(tag + "_outT", [128, FC, T], BF16))
        qf = es.enter_context(nc.sbuf_tensor(tag + "_qf", [128, 4, T], BF16))
        kf = es.enter_context(nc.sbuf_tensor(tag + "_kf", [128, 4, T], BF16))
        vg = es.enter_context(nc.sbuf_tensor(tag + "_vg", [128, NC, 8 * 65], BF16))
        wb = es.enter_context(nc.sbuf_tensor(tag + "_wb", [128, 2, 8, 512], BF16))
        e2 = es.enter_context(nc.sbuf_tensor(tag + "_e2", [128, 2, T], BF16))
        stg = es.enter_context(nc.sbuf_tensor(tag + "_stg", [64, T], BF16))
        shid = es.enter_context(nc.sbuf_tensor(tag + "_shid", [64, 128], BF16))
        cmask = es.enter_context(nc.sbuf_tensor(tag + "_cmask", [128, 4, 512], BF16))
        rsrow = es.enter_context(nc.sbuf_tensor(tag + "_rsrow", [1, T], F32))
        vch = es.enter_context(nc.sbuf_tensor(tag + "_vch", [128, 2, 1024], F32))
        xrsl = es.enter_context(nc.sbuf_tensor(tag + "_xrsl", [128, 2, 1024], F32))
        junkv = es.enter_context(nc.sbuf_tensor(tag + "_junkv", [128, 1024], F32))
        junkc = es.enter_context(nc.sbuf_tensor(tag + "_junkc", [128, 1], F32))
        lst = es.enter_context(nc.sbuf_tensor(tag + "_lst", [128, 2, 8], F32))
        pes = es.enter_context(ExitStack())
        ps_s = pes.enter_context(nc.psum_tensor(tag + "_pss", [128, T], F32))
        po = pes.enter_context(nc.psum_tensor(tag + "_po", [65, T], F32))

        s_ck = k.sem(tag + "_ck")
        for m in range(4):
            with nc.Block() as blk:
                @blk.gpsimd
                def _(g):
                    s_ck.inc(g.iota(junkv[:, 0:512], pattern=[[1, 512]], base=-128 * m,
                                    channel_multiplier=-1, allow_small_or_imprecise_dtypes=True))
                @blk.vector
                def _(v):
                    s_ck.wait(v)
                    s_ck.inc(v.tensor_scalar(cmask[:, m, :], junkv[:, 0:512], 0.0, None, A.is_ge))
        with nc.Block() as blk:
            @blk.gpsimd
            def _(g):
                s_ck.inc(g.iota(junkv[0:64, 0:128], pattern=[[1, 128]], base=-64,
                                channel_multiplier=-1, allow_small_or_imprecise_dtypes=True))
            @blk.vector
            def _(v):
                s_ck.wait(v)
                s_ck.inc(v.tensor_scalar(shid[:], junkv[0:64, 0:128], 0.0, None, A.is_equal))

        for g_i in range(2):
            s_wl = k.sem(tag + "_wlg")
            s_pj = k.sem(tag + "_pjg")
            s_pe = k.sem(tag + "_peg")
            b_wl, b_pj, b_pe = s_wl.n, s_pj.n, s_pe.n
            with nc.Block() as blk:
                @blk.gpsimd
                def _(g):
                    for wi, wt, slot in ((0, wq, 0), (1, wk, 1), (2, wv, 0)):
                        if wi == 2:
                            g.wait_ge(s_pj.h, b_pj + 16)
                        wvw = wt.rearrange("(ic p) o -> p ic o", p=128)[:, :, g_i * 512:(g_i + 1) * 512]
                        s_wl.inc(g.dma_start(wb[:, slot, :, :], wvw), 16)
                @blk.tensor
                def _(te):
                    seq = 0
                    for wi in range(2):
                        te.wait_ge(s_wl.h, b_wl + (wi + 1) * 16)
                        for oc in range(4):
                            for nt in range(4):
                                if seq >= 4:
                                    te.wait_ge(s_pe.h, b_pe + seq - 3)
                                for ic in range(FC):
                                    mm = te.matmul(ps_s[:, nt * 512:(nt + 1) * 512],
                                                   wb[:, wi % 2, ic, oc * 128:(oc + 1) * 128],
                                                   xf[:, ic, nt * 512:(nt + 1) * 512],
                                                   start=(ic == 0), stop=(ic == FC - 1),
                                                   skip_group_check=True)
                                    if ic == FC - 1:
                                        s_pj.inc(mm)
                                seq += 1
                    te.wait_ge(s_wl.h, b_wl + 48)
                    for tc in range(NC):
                        if seq >= 4:
                            te.wait_ge(s_pe.h, b_pe + seq - 3)
                        for ic in range(FC):
                            mm = te.matmul(ps_s[:, (tc % 4) * 512:(tc % 4 + 1) * 512],
                                           xf[:, ic, tc * 128:(tc + 1) * 128],
                                           wb[:, 0, ic, :],
                                           start=(ic == 0), stop=(ic == FC - 1),
                                           skip_group_check=True)
                            if ic == FC - 1:
                                s_pj.inc(mm)
                        seq += 1
                @blk.vector
                def _(v):
                    cnt = 0
                    for wi in range(2):
                        dst = qf if wi == 0 else kf
                        for oc in range(4):
                            for nt in range(4):
                                cnt += 1
                                v.wait_ge(s_pj.h, b_pj + cnt)
                                s_pe.inc(v.tensor_copy(dst[:, oc, nt * 512:(nt + 1) * 512],
                                                       ps_s[:, nt * 512:(nt + 1) * 512]))
                    for tc in range(NC):
                        cnt += 1
                        v.wait_ge(s_pj.h, b_pj + cnt)
                        s_pe.inc(v.tensor_copy(
                            vg[:, tc, 0:520].rearrange("p (h e) -> p h e", e=65)[:, :, 0:64],
                            ps_s[:, (tc % 4) * 512:(tc % 4 + 1) * 512]
                            .rearrange("p (h e) -> p h e", e=64)))
                    s_pe.inc(v.memset(
                        vg[:, :, 0:520].rearrange("p c (h e) -> p c h e", e=65)[:, :, :, 64:65], 1.0))

            for hh in range(8):
                h = g_i * 8 + hh
                hb = (h % 2) * 64
                hc = h // 2
                s_sc = k.sem(tag + "_sc")
                s_ex = k.sem(tag + "_ex")
                s_mk = k.sem(tag + "_mk")
                s_av = k.sem(tag + "_av")
                s_ev = k.sem(tag + "_ev")
                b_sc, b_ex, b_mk, b_av, b_ev = s_sc.n, s_ex.n, s_mk.n, s_av.n, s_ev.n
                with nc.Block() as blk:
                    @blk.tensor
                    def _(te):
                        for jc in range(NC):
                            nt0 = jc // 4
                            if jc > 0:
                                te.wait_ge(s_ex.h, b_ex + jc)
                            for nt in range(nt0, 4):
                                mm = te.matmul(ps_s[:, nt * 512:(nt + 1) * 512],
                                               kf[hb:hb + 64, hc % 4, jc * 128:(jc + 1) * 128],
                                               qf[hb:hb + 64, hc % 4, nt * 512:(nt + 1) * 512],
                                               start=True, stop=True,
                                               skip_group_check=True)
                                if nt == 3:
                                    s_sc.inc(mm)
                            te.wait_ge(s_mk.h, b_mk + jc + 1)
                            for nt in range(nt0, 4):
                                mm = te.matmul(po[:, nt * 512:(nt + 1) * 512],
                                               vg[:, jc, hh * 65:(hh + 1) * 65],
                                               e2[:, jc % 2, nt * 512:(nt + 1) * 512],
                                               start=(jc == 0), stop=(jc == 4 * nt + 3),
                                               skip_group_check=True)
                                if nt == 3:
                                    s_av.inc(mm)
                        te.wait_ge(s_ev.h, b_ev + 1)
                        for nt in range(4):
                            mm = te.matmul(ps_s[:, nt * 512:(nt + 1) * 512],
                                           k.ones_rowf[:, 0:128],
                                           rsrow[:, nt * 512:(nt + 1) * 512],
                                           start=True, stop=True, skip_group_check=True)
                            if nt == 3:
                                s_ev.inc(mm)
                        if hb == 64:
                            te.wait_ge(s_ev.h, b_ev + 3)
                            for nt in range(4):
                                mm = te.matmul(ps_s[:, nt * 512:(nt + 1) * 512],
                                               shid[:], stg[:, nt * 512:(nt + 1) * 512],
                                               start=True, stop=True, skip_group_check=True)
                                if nt == 3:
                                    s_ev.inc(mm)
                    @blk.scalar
                    def _(sc):
                        for jc in range(NC):
                            nt0 = jc // 4
                            sc.wait_ge(s_sc.h, b_sc + jc + 1)
                            if jc >= 2:
                                sc.wait_ge(s_av.h, b_av + jc - 1)
                            s_ex.inc(sc.activation(e2[:, jc % 2, nt0 * 512:T],
                                                   ps_s[:, nt0 * 512:T], AF.Exp, scale=0.125))
                    @blk.vector
                    def _(v):
                        for jc in range(NC):
                            nt0 = jc // 4
                            v.wait_ge(s_ex.h, b_ex + jc + 1)
                            s_mk.inc(v.tensor_tensor(
                                e2[:, jc % 2, nt0 * 512:(nt0 + 1) * 512],
                                e2[:, jc % 2, nt0 * 512:(nt0 + 1) * 512],
                                cmask[:, jc % 4, :], A.mult))
                        v.wait_ge(s_av.h, b_av + NC)
                        v.tensor_copy(rsrow[:, :], po[64:65, :])
                        v.reciprocal(rsrow[:, :], rsrow[:, :])
                        s_ev.inc(v.tensor_copy(stg[:], po[0:64, :]))
                        v.wait_ge(s_ev.h, b_ev + 2)
                        s_ev.inc(v.tensor_tensor(stg[:], stg[:], ps_s[0:64, :], A.mult))
                        if hb == 0:
                            s_ev.inc(v.tensor_copy(outT[0:64, hc, :], stg[:]))
                        else:
                            v.wait_ge(s_ev.h, b_ev + 4)
                            s_ev.inc(v.tensor_copy(outT[64:128, hc, :], ps_s[64:128, :]))

        s_wl = k.sem(tag + "_wlo")
        s_mm = k.sem(tag + "_mmo")
        s_xr = k.sem(tag + "_xro")
        s_v = k.sem(tag + "_vo")
        s_l3 = k.sem(tag + "_l3o")
        s_l4 = k.sem(tag + "_l4o")
        s_l6 = k.sem(tag + "_l6o")
        s_st = k.sem(tag + "_sto")
        with nc.Block() as blk:
            @blk.gpsimd
            def _(g):
                wvw = wo.rearrange("(ic p) o -> p ic o", p=128)
                for half in range(2):
                    s_wl.inc(g.dma_start(wb[:, half, :, :], wvw[:, :, half * 512:(half + 1) * 512]), 16)
            @blk.tensor
            def _(te):
                te.wait_ge(s_wl.h, 32)
                for c in range(NC):
                    if c >= 2:
                        te.wait_ge(s_v.h, 2 * (c - 1))
                    for et in range(2):
                        for fc in range(FC):
                            mm = te.matmul(ps_s[:, (2 * (c % 2) + et) * 512:(2 * (c % 2) + et + 1) * 512],
                                           outT[:, fc, c * 128:(c + 1) * 128],
                                           wb[:, et, fc, :],
                                           start=(fc == 0), stop=(fc == FC - 1),
                                           skip_group_check=True)
                            if fc == FC - 1:
                                s_mm.inc(mm)
            emit_tail_bodies(k, tag + "_t", blk, vch=vch, xrsl=xrsl, junkv=junkv,
                             junkc=junkc, lst=lst, xb=xb, xrv_in=xrv_in,
                             xrv_out=xrv_out,
                             ps_slice=lambda c, et: ps_s[:, (2 * (c % 2) + et) * 512:(2 * (c % 2) + et + 1) * 512],
                             s_mm=s_mm, b_mm=0, mm_target=lambda c, et: 2 * c + et + 1,
                             s_v=s_v)
        pes.close()
        emit_xf(k, xb, xf, tag + "_xf")


def build_l1attn_standalone():
    nc = bass.Bass("TRN2", target_bir_lowering=False, debug=False,
                   detect_race_conditions=False)
    k = K(nc)
    x = nc.dram_tensor("x", [T, D], F32, kind="ExternalInput").ap()
    y = nc.dram_tensor("y", [T, D], F32, kind="ExternalOutput").ap()
    declare_inputs(k, L1ATTN_SPECS)
    make_consts(k)
    xf = nc.alloc_sbuf_tensor("xf", [128, FC, T], BF16)
    s_ld = k.sem("xld")
    s_cv = k.sem("xcv")
    with ExitStack() as es:
        xr = es.enter_context(nc.sbuf_tensor("xr", [128, NC, D], F32))
        xb = es.enter_context(nc.sbuf_tensor("xb", [128, NC, D], BF16))
        with nc.Block() as blk:
            @blk.sync
            def _(sy):
                s_ld.inc(sy.dma_start(xr[:], x.rearrange("(c p) d -> p c d", p=128)), 16)
            @blk.vector
            def _(v):
                s_ld.wait(v)
                for c in range(NC):
                    s_cv.inc(v.tensor_copy(xb[:, c, :], xr[:, c, :]))
        emit_xf(k, xb, xf, "xf0")
    emit_l1attn(k, xf, x, y, tag="a1")
    return nc


_PROGS = {}


def _get_prog(name, builder):
    if name not in _PROGS:
        _PROGS[name] = builder()
    return _PROGS[name]


def kernel_multilaunch(**inputs):
    x = np.ascontiguousarray(inputs["L0_attn_x"])
    p_a0 = host_prep_l0attn(inputs)
    p_f0 = host_prep_l0ffn(inputs)
    p_a1 = host_prep_l1attn(inputs)
    p_f1 = host_prep_l1ffn(inputs)
    progs = [
        ("a0", build_l0attn_standalone, p_a0),
        ("f0", build_l0ffn_standalone, p_f0),
        ("a1", build_l1attn_standalone, p_a1),
        ("f1", build_l1ffn_standalone, p_f1),
    ]
    cur = [np.ascontiguousarray(x[c]) for c in range(8)]
    for name, builder, pp in progs:
        nc = _get_prog(name, builder)
        in_maps = []
        for c in range(8):
            m = {"x": cur[c]}
            m.update(pp)
            in_maps.append(m)
        res = run_bass_kernel_spmd(nc, in_maps, core_ids=list(range(8)))
        cur = [np.ascontiguousarray(res.results[c]["y"]) for c in range(8)]
    return np.stack(cur, axis=0)


L0FFN_SPECS = [
    ("f0_win", [D, 64]), ("f0_binc", [64, 1]), ("f0_w1", [64, 16]),
    ("f0_b1", [64, 16]), ("f0_w2", [64, 16]), ("f0_b2c", [64, 1]),
    ("f0_wout", [64, D]), ("f0_bout", [1, D]),
]


def emit_l0ffn(k, xf, xr_in_dram, xr_out_dram, tag="f0"):
    nc = k.nc
    xrv_in = xr_in_dram.rearrange("(c p) d -> p c d", p=128)
    xrv_out = xr_out_dram.rearrange("(c p) d -> p c d", p=128)
    with ExitStack() as es:
        xb = es.enter_context(nc.sbuf_tensor(tag + "_xb", [128, NC, D], BF16))
        winb = es.enter_context(nc.sbuf_tensor(tag + "_winb", [128, 8, 64], BF16))
        woutb = es.enter_context(nc.sbuf_tensor(tag + "_woutb", [64, D], BF16))
        w1 = es.enter_context(nc.sbuf_tensor(tag + "_w1sb", [64, 16], F32))
        bias2 = es.enter_context(nc.sbuf_tensor(tag + "_bias2", [64, 16], F32))
        b1 = es.enter_context(nc.sbuf_tensor(tag + "_b1sb", [64, 16], F32))
        w2 = es.enter_context(nc.sbuf_tensor(tag + "_w2sb", [64, 16], F32))
        b2c = es.enter_context(nc.sbuf_tensor(tag + "_b2csb", [64, 1], F32))
        binc = es.enter_context(nc.sbuf_tensor(tag + "_bincsb", [64, 1], F32))
        boutr = es.enter_context(nc.sbuf_tensor(tag + "_boutr", [1, D], F32))
        hbuf = es.enter_context(nc.sbuf_tensor(tag + "_hbuf", [64, 16, 512], BF16))
        gbuf = es.enter_context(nc.sbuf_tensor(tag + "_gbuf", [64, T], F32))
        gb = es.enter_context(nc.sbuf_tensor(tag + "_gb", [64, T], BF16))
        vch = es.enter_context(nc.sbuf_tensor(tag + "_vch", [128, 2, 1024], F32))
        xrsl = es.enter_context(nc.sbuf_tensor(tag + "_xrsl", [128, 2, 1024], F32))
        junkv = es.enter_context(nc.sbuf_tensor(tag + "_junkv", [128, 1024], F32))
        junkc = es.enter_context(nc.sbuf_tensor(tag + "_junkc", [128, 1], F32))
        lst = es.enter_context(nc.sbuf_tensor(tag + "_lst", [128, 2, 8], F32))
        pes = es.enter_context(ExitStack())
        ps_s = pes.enter_context(nc.psum_tensor(tag + "_pss", [64, T], F32))
        ps_o = pes.enter_context(nc.psum_tensor(tag + "_pso", [128, 4, 512], F32))

        s_wl = k.sem(tag + "_wl")
        s_fl = k.sem(tag + "_fl")
        s_pp = k.sem(tag + "_pp")
        s_sm = k.sem(tag + "_sm")
        s_ha = k.sem(tag + "_ha")
        s_gg = k.sem(tag + "_gg")
        s_gb = k.sem(tag + "_gb")
        s_mm = k.sem(tag + "_mm")
        s_v = k.sem(tag + "_v")
        with nc.Block() as blk:
            @blk.gpsimd
            def _(g):
                s_wl.inc(g.dma_start(winb[:],
                                     k.inp["f0_win"].rearrange("(ic p) m -> p ic m", p=128)), 16)
                s_wl.inc(g.dma_start(woutb[:], k.inp["f0_wout"]), 16)
            @blk.sync
            def _(sy):
                for nm, dst in (("f0_w1", w1), ("f0_b1", b1), ("f0_w2", w2),
                                ("f0_b2c", b2c), ("f0_binc", binc), ("f0_bout", boutr)):
                    s_fl.inc(sy.dma_start(dst[:], k.inp[nm]), 16)
            @blk.vector
            def _(v):
                s_fl.wait(v)
                s_pp.inc(v.scalar_tensor_tensor(bias2[:], w1[:], binc[:], b1[:],
                                                A.mult, A.add))
            @blk.tensor
            def _(te):
                te.wait_ge(s_wl.h, 16)
                for nt in range(4):
                    for ic in range(FC):
                        mm = te.matmul(ps_s[:, nt * 512:(nt + 1) * 512],
                                       winb[:, ic, :],
                                       xf[:, ic, nt * 512:(nt + 1) * 512],
                                       start=(ic == 0), stop=(ic == FC - 1),
                                       skip_group_check=True)
                        if ic == FC - 1:
                            s_sm.inc(mm)
                te.wait_ge(s_wl.h, 32)
                for c in range(NC):
                    te.wait_ge(s_gb.h, c // 4 + 1)
                    if c >= 2:
                        te.wait_ge(s_v.h, 2 * (c - 1))
                    for et in range(2):
                        bank = ps_o[:, 2 * (c % 2) + et, :]
                        mm = te.matmul(bank, gb[:, c * 128:(c + 1) * 128],
                                       woutb[:, et * 512:(et + 1) * 512],
                                       start=True, stop=False, skip_group_check=True)
                        mm = te.matmul(bank, k.ones_rowf[:, 0:128],
                                       boutr[:, et * 512:(et + 1) * 512],
                                       start=False, stop=True, skip_group_check=True)
                        s_mm.inc(mm)
            @blk.scalar
            def _(sc):
                sc.wait_ge(s_pp.h, 1)
                for nt in range(4):
                    sc.wait_ge(s_sm.h, nt + 1)
                    if nt >= 1:
                        sc.wait_ge(s_gg.h, nt)
                    for i in range(16):
                        s_ha.inc(sc.activation(hbuf[:, i, :],
                                               ps_s[:, nt * 512:(nt + 1) * 512],
                                               AF.Relu, scale=w1[:, i:i + 1],
                                               bias=bias2[:, i:i + 1]))
            @blk.vector
            def _(v):
                for nt in range(4):
                    for i in range(16):
                        v.wait_ge(s_ha.h, 16 * nt + i + 1)
                        if i == 0:
                            v.tensor_scalar(gbuf[:, nt * 512:(nt + 1) * 512],
                                            hbuf[:, 0, :], w2[:, 0:1], None, A.mult)
                        else:
                            v.scalar_tensor_tensor(gbuf[:, nt * 512:(nt + 1) * 512],
                                                   hbuf[:, i, :], w2[:, i:i + 1],
                                                   gbuf[:, nt * 512:(nt + 1) * 512],
                                                   A.mult, A.add)
                    v.tensor_scalar(gbuf[:, nt * 512:(nt + 1) * 512],
                                    gbuf[:, nt * 512:(nt + 1) * 512],
                                    b2c[:], None, A.add)
                    s_gg.inc(v.tensor_copy(gb[:, nt * 512:(nt + 1) * 512],
                                           gbuf[:, nt * 512:(nt + 1) * 512]))
                    s_gb.inc(v.tensor_copy(junkc[0:64, :], gbuf[:, nt * 512:nt * 512 + 1]))
            emit_tail_bodies(k, tag + "_t", blk, vch=vch, xrsl=xrsl, junkv=junkv,
                             junkc=junkc, lst=lst, xb=xb, xrv_in=xrv_in,
                             xrv_out=xrv_out,
                             ps_slice=lambda c, et: ps_o[:, 2 * (c % 2) + et, :],
                             s_mm=s_mm, b_mm=0, mm_target=lambda c, et: 2 * c + et + 1,
                             s_v=s_v)
        pes.close()
        emit_xf(k, xb, xf, tag + "_xf")


def build_l0ffn_standalone():
    nc = bass.Bass("TRN2", target_bir_lowering=False, debug=False,
                   detect_race_conditions=False)
    k = K(nc)
    x = nc.dram_tensor("x", [T, D], F32, kind="ExternalInput").ap()
    y = nc.dram_tensor("y", [T, D], F32, kind="ExternalOutput").ap()
    declare_inputs(k, L0FFN_SPECS)
    make_consts(k)
    xf = nc.alloc_sbuf_tensor("xf", [128, FC, T], BF16)
    s_ld = k.sem("xld")
    s_cv = k.sem("xcv")
    with ExitStack() as es:
        xr = es.enter_context(nc.sbuf_tensor("xr", [128, NC, D], F32))
        xb = es.enter_context(nc.sbuf_tensor("xb", [128, NC, D], BF16))
        with nc.Block() as blk:
            @blk.sync
            def _(sy):
                s_ld.inc(sy.dma_start(xr[:], x.rearrange("(c p) d -> p c d", p=128)), 16)
            @blk.vector
            def _(v):
                s_ld.wait(v)
                for c in range(NC):
                    s_cv.inc(v.tensor_copy(xb[:, c, :], xr[:, c, :]))
        emit_xf(k, xb, xf, "xf0")
    emit_l0ffn(k, xf, x, y, tag="f0")
    return nc


L0ATTN_SPECS = [
    ("a0_wq", [D, D]), ("a0_wk", [D, D]), ("a0_wv", [D, D]), ("a0_wo", [D, D]),
    ("a0_bqc", [128, 8]), ("a0_bkc", [128, 8]), ("a0_bvr", [1, D]), ("a0_bor", [1, D]),
    ("a0_probes", [128, 3 * 16 * 64]),
]
NBL = [128, 8, 32]
CSL = [16, 256, 64]
SHL = [4, 8, 6]
TOFF = [0, 128, 136]
REFL = [2, 0, 1]
NT168 = 168


def emit_l0attn(k, xf, x_in_dram, xr_out_dram, tag="a0", dbg=None):
    nc = k.nc
    xrv_in = x_in_dram.rearrange("(c p) d -> p c d", p=128)
    xrv_out = xr_out_dram.rearrange("(c p) d -> p c d", p=128)
    scr = nc.dram_tensor(tag + "_scr", [1, 2048], BF16, kind="Internal").ap()

    with ExitStack() as es:
        outT = es.enter_context(nc.sbuf_tensor(tag + "_outT", [128, FC, T], BF16))
        probesb = es.enter_context(nc.sbuf_tensor(tag + "_probesb", [128, 3072], BF16))
        L128 = es.enter_context(nc.sbuf_tensor(tag + "_L128", [128, 128], BF16))
        L16 = es.enter_context(nc.sbuf_tensor(tag + "_L16", [16, 16], BF16))
        SELf = es.enter_context(nc.sbuf_tensor(tag + "_SELf", [16, T], F32))
        Unb = [es.enter_context(nc.sbuf_tensor(f"{tag}_Unb{l}", [NBL[l], NBL[l]], F32))
               for l in range(3)]
        iotan = es.enter_context(nc.sbuf_tensor(tag + "_iotan", [128, 128], F32))
        iota64r = es.enter_context(nc.sbuf_tensor(tag + "_iota64r", [128, 64], F32))
        tb = [es.enter_context(nc.sbuf_tensor(f"{tag}_tb{l}", [128, 64], F32))
              for l in range(3)]
        tbi = es.enter_context(nc.sbuf_tensor(tag + "_tbi", [128, 64], I32))
        tbi2 = es.enter_context(nc.sbuf_tensor(tag + "_tbi2", [128, 64], I32))
        shid = es.enter_context(nc.sbuf_tensor(tag + "_shid", [64, 128], BF16))
        bqc = es.enter_context(nc.sbuf_tensor(tag + "_bqcs", [128, 8], F32))
        bkc = es.enter_context(nc.sbuf_tensor(tag + "_bkcs", [128, 8], F32))
        bvr = es.enter_context(nc.sbuf_tensor(tag + "_bvrs", [1, D], F32))

        s_ct = k.sem(tag + "_ct")
        with nc.Block() as blk:
            @blk.sync
            def _(sy):
                s_ct.inc(sy.dma_start(bqc[:], k.inp["a0_bqc"]), 16)
                s_ct.inc(sy.dma_start(bkc[:], k.inp["a0_bkc"]), 16)
                s_ct.inc(sy.dma_start(bvr[:], k.inp["a0_bvr"]), 16)
            @blk.gpsimd
            def _(g):
                s_ctg = k.sem(tag + "_ctg")
                s_ctg.inc(g.dma_start(probesb[:], k.inp["a0_probes"]), 16)
                g.iota(iotan[:], pattern=[[1, 128]], base=0, channel_multiplier=0,
                       allow_small_or_imprecise_dtypes=True)
                g.iota(iota64r[:], pattern=[[1, 64]], base=0, channel_multiplier=0,
                       allow_small_or_imprecise_dtypes=True)
                s_ct.inc(g.iota(SELf[:], pattern=[[1, 16], [0, 128]], base=0,
                                channel_multiplier=-1, allow_small_or_imprecise_dtypes=True))
            @blk.vector
            def _(v):
                s_ct.wait(v)
                s_ct.inc(v.tensor_scalar(SELf[:], SELf[:], 0.0, None, A.is_equal))
                v.tensor_copy(tbi[:], iota64r[:])
                v.tensor_scalar(tbi2[:], tbi[:], 9973, None, A.mult)
                for l in range(3):
                    v.tensor_scalar(tbi[:], tbi2[:], NBL[l] - 1, None, A.bitwise_and)
                    s_ct.inc(v.tensor_copy(tb[l][:], tbi[:]))
        with nc.Block() as blk:
            @blk.gpsimd
            def _(g):
                s_ct.inc(g.iota(SELf[0:16, 0:128].bitcast(F32), pattern=[[1, 128]], base=0,
                                channel_multiplier=-1, allow_small_or_imprecise_dtypes=True)) \
                    if False else None
                s_ct.inc(g.iota(iotan[:], pattern=[[1, 128]], base=0, channel_multiplier=-1,
                                allow_small_or_imprecise_dtypes=True))
            @blk.vector
            def _(v):
                s_ct.wait(v)
                v.tensor_scalar(L128[:], iotan[:], 0.0, None, A.is_gt)
                v.tensor_scalar(L16[:], iotan[0:16, 0:16], 0.0, None, A.is_gt)
                for l in range(3):
                    nb = NBL[l]
                    v.tensor_scalar(Unb[l][:], iotan[0:nb, 0:nb], 0.0, None, A.is_gt)
                v.tensor_scalar(iotan[0:64, :], iotan[0:64, :], 64.0, None, A.subtract)
                s_ct.inc(v.tensor_scalar(shid[:], iotan[0:64, :], 0.0, None, A.is_equal))
        with nc.Block() as blk:
            @blk.gpsimd
            def _(g):
                s_ct.inc(g.iota(iotan[:], pattern=[[1, 128]], base=0, channel_multiplier=0,
                                allow_small_or_imprecise_dtypes=True))

        with ExitStack() as hs:
            E = hs.enter_context(nc.sbuf_tensor(tag + "_E", [128, NC, T], BF16))
            qfh = hs.enter_context(nc.sbuf_tensor(tag + "_qfh", [64, T], BF16))
            kfh = hs.enter_context(nc.sbuf_tensor(tag + "_kfh", [64, T], BF16))
            vh = hs.enter_context(nc.sbuf_tensor(tag + "_vh", [128, NC, 64], BF16))
            wqkv = hs.enter_context(nc.sbuf_tensor(tag + "_wqkv", [128, 3, 8, 64], BF16))
            pmax = hs.enter_context(nc.sbuf_tensor(tag + "_pmax", [128, 16], F32))
            regf = hs.enter_context(nc.sbuf_tensor(tag + "_regf", [128, 16], F32))
            regi = hs.enter_context(nc.sbuf_tensor(tag + "_regi", [128, 16], I32))
            bkt = hs.enter_context(nc.sbuf_tensor(tag + "_bkt", [128, 16], F32))
            Obuf = hs.enter_context(nc.sbuf_tensor(tag + "_Obuf", [128, NC, 128], BF16))
            T16 = hs.enter_context(nc.sbuf_tensor(tag + "_T16b", [16, 128], BF16))
            ccol = hs.enter_context(nc.sbuf_tensor(tag + "_ccol", [128, 1], F32))
            strow = hs.enter_context(nc.sbuf_tensor(tag + "_strow", [1, 128], F32))
            carry = hs.enter_context(nc.sbuf_tensor(tag + "_carry", [16, 128], F32))
            ranks = hs.enter_context(nc.sbuf_tensor(tag + "_ranks", [128, 16], F32))
            ranki = hs.enter_context(nc.sbuf_tensor(tag + "_ranki", [128, 16], I32))
            cvals = hs.enter_context(nc.sbuf_tensor(tag + "_cvals", [128, 3, 16], F32))
            rtmp = hs.enter_context(nc.sbuf_tensor(tag + "_rtmp", [128, 128], F32))
            Ucol = hs.enter_context(nc.sbuf_tensor(tag + "_Ucol", [128, NC, NT168], BF16))
            Scol = hs.enter_context(nc.sbuf_tensor(tag + "_Scol", [128, NC, NT168], BF16))
            rsb = hs.enter_context(nc.sbuf_tensor(tag + "_rsb", [128, 3, 16], F32))
            bkts = hs.enter_context(nc.sbuf_tensor(tag + "_bkts", [128, 3, 16], BF16))
            t16a = hs.enter_context(nc.sbuf_tensor(tag + "_t16a", [16, 3, 128], BF16))
            tsa = hs.enter_context(nc.sbuf_tensor(tag + "_tsa", [1, 1, 512], BF16))
            rstmp = hs.enter_context(nc.sbuf_tensor(tag + "_rstmp", [128, 1024], F32))
            T2sb = hs.enter_context(nc.sbuf_tensor(tag + "_T2sb", [128, T], BF16))
            T01sb = hs.enter_context(nc.sbuf_tensor(tag + "_T01sb", [40, T], BF16))
            TTb = hs.enter_context(nc.sbuf_tensor(tag + "_TTb", [128, NC, NT168], BF16))
            A2T = hs.enter_context(nc.sbuf_tensor(tag + "_A2T", [128, T], BF16))
            PS_A = hs.enter_context(nc.psum_tensor(tag + "_psA", [128, T], F32))
            PS_C = hs.enter_context(nc.psum_tensor(tag + "_psC", [64, T], F32))

            for h in range(16):
                emit_l0attn_head(k, tag, h, locals())
                if dbg is not None and h == 0:
                    sdb = k.sem(tag + "_dbg")
                    with nc.Block() as blkd:
                        @blkd.vector
                        def _(v):
                            v.tensor_copy(rstmp[:, 0:48], cvals[:].rearrange("p a b -> p (a b)"))
                            v.tensor_copy(rstmp[:, 64:65], ccol[:])
                            sdb.inc(v.tensor_copy(rstmp[:, 48:64], ranks[:]))
                        @blkd.sync
                        def _(sy):
                            sdb.wait(sy)
                            sdb.inc(sy.dma_start(dbg[:, 0:65], rstmp[:, 0:65]), 16)
                            sdb.inc(sy.dma_start(dbg[:, 64:112], rsb[:].rearrange("p a b -> p (a b)")), 16)
                            sdb.inc(sy.dma_start(dbg[:, 112:2160].bitcast(BF16).rearrange("p (a b) -> p a b", b=4096)[:, 0, :].rearrange("p (a b) -> p a b", b=2048)[:, 0, :] if False else dbg[:, 112:1136].bitcast(BF16), E[:, 0, :]), 16)
                            sdb.inc(sy.dma_start(dbg[0:64, 1136:3184].bitcast(BF16)[0:64, 0:2048], qfh[:]), 16)
                            sdb.inc(sy.dma_start(dbg[:, 3184:3216].bitcast(BF16)[:, 0:64], vh[:, 0, :]), 16)
                            sdb.inc(sy.dma_start(dbg[:, 3216:3232], bkt[:]), 16)
                            sdb.inc(sy.dma_start(dbg[:, 3232:3248], regf[:]), 16)
                            sdb.inc(sy.dma_start(dbg[0:16, 3248:3376], carry[:]), 16)
                            sdb.inc(sy.dma_start(dbg[0:1, 3376:3504], strow[:]), 16)
                            sdb.inc(sy.dma_start(dbg[0:16, 3508:3572].bitcast(BF16), T16[:]), 16)
                            sdb.inc(sy.dma_start(dbg[:, 3572:3596].bitcast(BF16), bkts[:].rearrange("p a b -> p (a b)")), 16)
                            sdb.inc(sy.dma_start(dbg[0:16, 3620:3812].bitcast(BF16), t16a[:].rearrange("p a b -> p (a b)")), 16)
                            sdb.inc(sy.dma_start(dbg[0:1, 3824:4080].bitcast(BF16)[0:1, 0:512], tsa[0:1, 0, 0:512]), 16)
                            sdb.wait(sy)

        with ExitStack() as ts:
            xb = ts.enter_context(nc.sbuf_tensor(tag + "_xb", [128, NC, D], BF16))
            wob = ts.enter_context(nc.sbuf_tensor(tag + "_wob", [128, 2, 8, 512], BF16))
            bor = ts.enter_context(nc.sbuf_tensor(tag + "_bors", [1, D], F32))
            vch = ts.enter_context(nc.sbuf_tensor(tag + "_vch", [128, 2, 1024], F32))
            xrsl = ts.enter_context(nc.sbuf_tensor(tag + "_xrsl", [128, 2, 1024], F32))
            junkv = ts.enter_context(nc.sbuf_tensor(tag + "_junkv", [128, 1024], F32))
            junkc = ts.enter_context(nc.sbuf_tensor(tag + "_junkc", [128, 1], F32))
            lst = ts.enter_context(nc.sbuf_tensor(tag + "_lst", [128, 2, 8], F32))
            ps_w = ts.enter_context(nc.psum_tensor(tag + "_psw", [128, T], F32))
            s_wl = k.sem(tag + "_wlo")
            s_mm = k.sem(tag + "_mmo")
            s_v = k.sem(tag + "_vo")
            with nc.Block() as blk:
                @blk.gpsimd
                def _(g):
                    wvw = k.inp["a0_wo"].rearrange("(ic p) o -> p ic o", p=128)
                    for half in range(2):
                        s_wl.inc(g.dma_start(wob[:, half, :, :],
                                             wvw[:, :, half * 512:(half + 1) * 512]), 16)
                @blk.sync
                def _(sy):
                    s_bor = k.sem(tag + "_bor")
                    s_bor.inc(sy.dma_start(bor[:], k.inp["a0_bor"]), 16)
                @blk.tensor
                def _(te):
                    te.wait_ge(s_wl.h, 32)
                    te.wait_ge(k.sem(tag + "_bor").h, 16)
                    for c in range(NC):
                        if c >= 2:
                            te.wait_ge(s_v.h, 2 * (c - 1))
                        for et in range(2):
                            bank = ps_w[:, (2 * (c % 2) + et) * 512:(2 * (c % 2) + et + 1) * 512]
                            for fc in range(FC):
                                te.matmul(bank, outT[:, fc, c * 128:(c + 1) * 128],
                                          wob[:, et, fc, :],
                                          start=(fc == 0), stop=False,
                                          skip_group_check=True)
                            mm = te.matmul(bank, k.ones_rowf[:, 0:128],
                                           bor[:, et * 512:(et + 1) * 512],
                                           start=False, stop=True, skip_group_check=True)
                            s_mm.inc(mm)
                emit_tail_bodies(k, tag + "_t", blk, vch=vch, xrsl=xrsl, junkv=junkv,
                                 junkc=junkc, lst=lst, xb=xb, xrv_in=xrv_in,
                                 xrv_out=xrv_out,
                                 ps_slice=lambda c, et: ps_w[:, (2 * (c % 2) + et) * 512:(2 * (c % 2) + et + 1) * 512],
                                 s_mm=s_mm, b_mm=0, mm_target=lambda c, et: 2 * c + et + 1,
                                 s_v=s_v)
            emit_xf(k, xb, xf, tag + "_xf")


def emit_l0attn_head(k, tag, h, B):
    nc = k.nc
    xf = B["xf"]
    E, qfh, kfh, vh = B["E"], B["qfh"], B["kfh"], B["vh"]
    stg = kfh
    wqkv, probesb = B["wqkv"], B["probesb"]
    pmax, regf, regi, bkt = B["pmax"], B["regf"], B["regi"], B["bkt"]
    Obuf, T16, ccol, strow, carry = (B["Obuf"], B["T16"],
                                     B["ccol"], B["strow"], B["carry"])
    ohb = Obuf[:, :, 64:128]
    tstag = B["rstmp"][0:1, 0:1024].bitcast(BF16)
    ranks, ranki, cvals, rtmp = B["ranks"], B["ranki"], B["cvals"], B["rtmp"]
    Ucol, Scol, rsb, rstmp = B["Ucol"], B["Scol"], B["rsb"], B["rstmp"]
    bkts = B["bkts"]
    t16a, tsa = B["t16a"], B["tsa"]
    Acol = Ucol
    T2sb, T01sb, TTb = B["T2sb"], B["T01sb"], B["TTb"]
    A2T = B["A2T"]
    A01T = B["qfh"][0:40, :]
    S2T, S01T = B["T2sb"], B["T01sb"]
    PS_A, PS_C = B["PS_A"], B["PS_C"]
    L128, L16, SELf, Unb, iotan, iota64r, tb = (B["L128"], B["L16"], B["SELf"],
                                                B["Unb"], B["iotan"], B["iota64r"], B["tb"])
    shid, bqc, bkc, bvr, outT, scr = (B["shid"], B["bqc"], B["bkc"], B["bvr"],
                                      B["outT"], B["scr"])
    hb = (h % 2) * 64

    s_wl = k.sem(tag + "_hwl"); b_wl = s_wl.n
    s_mmq = k.sem(tag + "_hmmq"); b_mmq = s_mmq.n
    s_evq = k.sem(tag + "_hevq"); b_evq = s_evq.n
    s_mmv = k.sem(tag + "_hmmv"); b_mmv = s_mmv.n
    s_evv = k.sem(tag + "_hevv"); b_evv = s_evv.n
    s_pp = k.sem(tag + "_hpp"); b_pp = s_pp.n
    s_O = k.sem(tag + "_hO"); b_O = s_O.n
    s_cnt = k.sem(tag + "_hcnt"); b_cnt = s_cnt.n
    s_ts = k.sem(tag + "_hts"); b_ts = s_ts.n
    s_t16 = k.sem(tag + "_ht16"); b_t16 = s_t16.n
    s_pcm = k.sem(tag + "_hpcm"); b_pcm = s_pcm.n
    s_pcv = k.sem(tag + "_hpcv"); b_pcv = s_pcv.n
    s_rp = k.sem(tag + "_hrp"); b_rp = s_rp.n
    s_rk = k.sem(tag + "_hrk"); b_rk = s_rk.n
    s_cv = k.sem(tag + "_hcv"); b_cv = s_cv.n
    s_ust = k.sem(tag + "_hust"); b_ust = s_ust.n
    s_sc = k.sem(tag + "_hsc"); b_sc = s_sc.n
    s_exp = k.sem(tag + "_hexp"); b_exp = s_exp.n
    s_t2 = k.sem(tag + "_ht2"); b_t2 = s_t2.n
    s_tev = k.sem(tag + "_htev"); b_tev = s_tev.n
    s_ttm = k.sem(tag + "_httm"); b_ttm = s_ttm.n
    s_ttv = k.sem(tag + "_httv"); b_ttv = s_ttv.n
    s_rs = k.sem(tag + "_hrs"); b_rs = s_rs.n
    s_f = k.sem(tag + "_hf"); b_f = s_f.n
    s_pm = k.sem(tag + "_hpm"); b_pm = s_pm.n
    s_av = k.sem(tag + "_hav"); b_av = s_av.n
    s_oe = k.sem(tag + "_hoe"); b_oe = s_oe.n

    with nc.Block() as blk:
        @blk.gpsimd
        def _(g):
            for wi, nm in ((0, "a0_wq"), (1, "a0_wk"), (2, "a0_wv")):
                wv = k.inp[nm].rearrange("(ic p) o -> p ic o", p=128)[:, :, h * 64:(h + 1) * 64]
                s_wl.inc(g.dma_start(wqkv[:, wi, :, :], wv), 16)
        @blk.sync
        def _(sy):
            for l in range(3):
                nb = NBL[l]
                sy.wait_ge(s_ts.h, b_ts + l + 1)
                s_t16.inc(sy.dma_start(scr[0:1, 0:16 * nb], tstag[0:1, 0:16 * nb]), 16)
                sy.wait_ge(s_t16.h, b_t16 + 32 * l + 16)
                s_t16.inc(sy.dma_start(
                    T16[0:16, 0:nb],
                    scr[0:1, 0:16 * nb].rearrange("a (c v) -> (a c) v", v=nb)), 16)

        @blk.tensor
        def _(te):
            for wi in range(2):
                te.wait_ge(s_wl.h, b_wl + 16 * (wi + 1))
                if wi == 1:
                    te.wait_ge(s_evq.h, b_evq + 4)
                for nt in range(4):
                    for ic in range(FC):
                        mm = te.matmul(PS_C[:, nt * 512:(nt + 1) * 512],
                                       wqkv[:, wi, ic, :],
                                       xf[:, ic, nt * 512:(nt + 1) * 512],
                                       start=(ic == 0), stop=(ic == FC - 1),
                                       skip_group_check=True)
                        if ic == FC - 1:
                            s_mmq.inc(mm)
            te.wait_ge(s_wl.h, b_wl + 48)
            for tc in range(NC):
                for ic in range(FC):
                    te.matmul(PS_A[:, tc * 64:(tc + 1) * 64],
                              xf[:, ic, tc * 128:(tc + 1) * 128],
                              wqkv[:, 2, ic, :],
                              start=(ic == 0), stop=False, skip_group_check=True)
                mm = te.matmul(PS_A[:, tc * 64:(tc + 1) * 64], k.ones_rowf[:, 0:128],
                               bvr[:, h * 64:(h + 1) * 64],
                               start=False, stop=True, skip_group_check=True)
                s_mmv.inc(mm)
            te.wait_ge(k.sem(tag + "_ctg").h, 16)
            for l in range(3):
                rl = REFL[l]
                nb = NBL[l]
                te.wait_ge(s_evq.h, b_evq + 8)
                te.wait_ge(s_evv.h, b_evv + 1)
                if l > 0:
                    te.wait_ge(s_rk.h, b_rk + 16 * l)
                for c in range(NC):
                    mm = te.matmul(PS_A[:, c * 64:(c + 1) * 64],
                                   qfh[:, c * 128:(c + 1) * 128],
                                   probesb[0:64, (rl * 16 + h) * 64:(rl * 16 + h + 1) * 64],
                                   start=True, stop=True, skip_group_check=True)
                    if c == NC - 1:
                        s_pp.inc(mm)
                te.wait_ge(s_O.h, b_O + l + 1)
                Ofl = Obuf[:].rearrange("p c e -> p (c e)")
                for c in range(NC):
                    mm = te.matmul(PS_A[0:1, c * nb:(c + 1) * nb], k.ones_col[:, :],
                                   Ofl[:, c * nb:(c + 1) * nb],
                                   start=True, stop=True, skip_group_check=True)
                    if c == NC - 1:
                        s_cnt.inc(mm)
                te.wait_ge(s_t16.h, b_t16 + 32 * (l + 1))
                s_pcm.inc(te.matmul(PS_A[0:nb, 1200:1201], T16[0:16, 0:nb],
                                    k.ones_col[0:16, :], start=True, stop=True,
                                    skip_group_check=True))
                s_pcm.inc(te.matmul(PS_A[0:16, 1600:1600 + nb], L16[:],
                                    T16[0:16, 0:nb], start=True, stop=False,
                                    skip_group_check=True))
                te.wait_ge(s_pcv.h, b_pcv + 3 * l + 1)
                s_pcm.inc(te.matmul(PS_A[0:1, 1400:1400 + nb], ccol[0:nb, :],
                                    Unb[l][:], start=True, stop=True,
                                    skip_group_check=True))
                te.wait_ge(s_pcv.h, b_pcv + 3 * l + 2)
                s_pcm.inc(te.matmul(PS_A[0:16, 1600:1600 + nb],
                                    k.ones_rowf[0:1, 0:16],
                                    strow[0:1, 0:nb],
                                    start=False, stop=True, skip_group_check=True))
                te.wait_ge(s_pcv.h, b_pcv + 3 * l + 3)
                for c in range(NC):
                    if c > 0:
                        te.wait_ge(s_rk.h, b_rk + 16 * l + c)
                    te.matmul(PS_A[:, 1024:1024 + nb], L128[:], Ofl[:, c * nb:(c + 1) * nb],
                              start=True, stop=False, skip_group_check=True)
                    mm = te.matmul(PS_A[:, 1024:1024 + nb],
                                   SELf[:, c * 128:(c + 1) * 128],
                                   carry[0:16, 0:nb], start=False, stop=True,
                                   skip_group_check=True)
                    s_rp.inc(mm)
            te.wait_ge(s_cv.h, b_cv + 3)
            for jc in range(NC):
                if jc > 0:
                    te.wait_ge(s_exp.h, b_exp + jc)
                for nt in range(4):
                    mm = te.matmul(PS_A[:, nt * 512:(nt + 1) * 512],
                                   kfh[:, jc * 128:(jc + 1) * 128],
                                   qfh[:, nt * 512:(nt + 1) * 512],
                                   start=True, stop=True, skip_group_check=True)
                    if nt == 3:
                        s_sc.inc(mm)
            te.wait_ge(s_exp.h, b_exp + NC)
            te.wait_ge(s_ust.h, b_ust + 3)
            for nt in range(4):
                for c in range(NC):
                    mm = te.matmul(PS_A[:, nt * 512:(nt + 1) * 512],
                                   Scol[:, c, 0:128], E[:, c, nt * 512:(nt + 1) * 512],
                                   start=(c == 0), stop=(c == NC - 1),
                                   skip_group_check=True)
                    if c == NC - 1:
                        s_t2.inc(mm)
                for c in range(NC):
                    mm = te.matmul(PS_C[0:40, nt * 512:(nt + 1) * 512],
                                   Scol[:, c, 128:168], E[:, c, nt * 512:(nt + 1) * 512],
                                   start=(c == 0), stop=(c == NC - 1),
                                   skip_group_check=True)
                    if c == NC - 1:
                        s_t2.inc(mm)
            te.wait_ge(s_tev.h, b_tev + 8)
            for grp in range(4):
                if grp >= 2:
                    te.wait_ge(s_ttv.h, b_ttv + 2 * (grp - 1))
                for j in range(4):
                    c = grp * 4 + j
                    mm = te.matmul(PS_A[:, (grp % 2) * 512:(grp % 2) * 512 + 512][:, j * 128:(j + 1) * 128],
                                   T2sb[:, c * 128:(c + 1) * 128], k.ident[:],
                                   start=True, stop=True, skip_group_check=True)
                    if j == 3:
                        s_ttm.inc(mm)
                for j in range(4):
                    c = grp * 4 + j
                    mm = te.matmul(PS_A[:, 1024 + (grp % 2) * 512:1024 + (grp % 2) * 512 + 512][:, j * 128:(j + 1) * 128],
                                   T01sb[0:40, c * 128:(c + 1) * 128], k.ident[0:40, :],
                                   start=True, stop=True, skip_group_check=True)
                    if j == 3:
                        s_ttm.inc(mm)
            te.wait_ge(s_rs.h, b_rs + 4)
            te.wait_ge(s_ttv.h, b_ttv + 8)
            for wave in range(2):
                src = Scol if wave == 0 else Acol
                for grp in range(4):
                    gi = wave * 4 + grp
                    if gi >= 2:
                        te.wait_ge(s_ttv.h, b_ttv + 8 + 2 * (gi - 1))
                    for j in range(4):
                        c = grp * 4 + j
                        mm = te.matmul(PS_A[:, (gi % 2) * 512 + j * 128:(gi % 2) * 512 + (j + 1) * 128],
                                       src[:, c, 0:128], k.ident[:],
                                       start=True, stop=True, skip_group_check=True)
                        if j == 3:
                            s_ttm.inc(mm)
                    for j in range(4):
                        c = grp * 4 + j
                        mm = te.matmul(PS_A[0:40, 1024 + (gi % 2) * 512 + j * 128:1024 + (gi % 2) * 512 + (j + 1) * 128],
                                       src[:, c, 128:168], k.ident[:],
                                       start=True, stop=True, skip_group_check=True)
                        if j == 3:
                            s_ttm.inc(mm)
            te.wait_ge(s_ttv.h, b_ttv + 24)
            for jc in range(NC):
                if jc > 0:
                    te.wait_ge(s_pm.h, b_pm + jc)
                for nt in range(4):
                    te.matmul(PS_A[:, nt * 512:(nt + 1) * 512],
                              S2T[:, jc * 128:(jc + 1) * 128],
                              A2T[:, nt * 512:(nt + 1) * 512],
                              start=True, stop=False, skip_group_check=True)
                    mm = te.matmul(PS_A[:, nt * 512:(nt + 1) * 512],
                                   S01T[0:40, jc * 128:(jc + 1) * 128],
                                   A01T[0:40, nt * 512:(nt + 1) * 512],
                                   start=False, stop=True, skip_group_check=True)
                    if nt == 3:
                        s_f.inc(mm)
                te.wait_ge(s_pm.h, b_pm + jc + 1)
                for nt in range(4):
                    mm = te.matmul(PS_C[:, nt * 512:(nt + 1) * 512],
                                   vh[:, jc, :], E[:, jc, nt * 512:(nt + 1) * 512],
                                   start=(jc == 0), stop=(jc == NC - 1),
                                   skip_group_check=True)
                    if nt == 3:
                        s_av.inc(mm)
            if hb == 64:
                te.wait_ge(s_oe.h, b_oe + 1)
                for nt in range(4):
                    mm = te.matmul(PS_A[:, nt * 512:(nt + 1) * 512], shid[:],
                                   stg[:, nt * 512:(nt + 1) * 512],
                                   start=True, stop=True, skip_group_check=True)
                    if nt == 3:
                        s_oe.inc(mm)

        @blk.scalar
        def _(sc):
            for wi in range(2):
                dst = qfh if wi == 0 else kfh
                bcol = bqc if wi == 0 else bkc
                for nt in range(4):
                    sc.wait_ge(s_mmq.h, b_mmq + 4 * wi + nt + 1)
                    s_evq.inc(sc.activation(dst[:, nt * 512:(nt + 1) * 512],
                                            PS_C[:, nt * 512:(nt + 1) * 512],
                                            AF.Identity,
                                            bias=bcol[hb:hb + 64, h // 2:h // 2 + 1]))
            for jc in range(NC):
                sc.wait_ge(s_sc.h, b_sc + jc + 1)
                s_exp.inc(sc.activation(E[:, jc, :], PS_A[:, :], AF.Exp, scale=0.125))

        ch = k.chainer(tag + "_hch")
        @blk.vector
        def _(v):
            v.wait_ge(s_mmv.h, b_mmv + NC)
            s_evv.inc(v.tensor_copy(vh[:].rearrange("p c e -> p (c e)"), PS_A[:, 0:1024]))
            for l in range(3):
                nb = NBL[l]
                toff = TOFF[l]
                if l >= 1:
                    v.wait_ge(s_t16.h, b_t16 + 32 * (l - 1) + 16)
                v.wait_ge(s_pp.h, b_pp + l + 1)
                v.tensor_copy(rstmp[:, 0:1024], PS_A[:, 0:1024])
                pp3 = rstmp[:, 0:1024].rearrange("p (c e) -> p c e", e=64)
                ch(v, v.tensor_reduce(pmax[:].unsqueeze(2), pp3, AX.X, A.max))
                v.tensor_tensor(ohb, pp3, pmax[:].unsqueeze(2).broadcast_to([128, 16, 64]),
                                A.is_equal)
                v.tensor_tensor(rstmp[:, 0:1024].rearrange("p (c e) -> p c e", e=64),
                                ohb,
                                iota64r[:].rearrange("p (o e) -> p o e", o=1)
                                .broadcast_to([128, 16, 64]), A.mult)
                ch(v, v.tensor_reduce(regf[:].unsqueeze(2),
                                      rstmp[:, 0:1024].rearrange("p (c e) -> p c e", e=64),
                                      AX.X, A.add))
                ch(v, v.tensor_copy(regi[:], regf[:]))
                ch(v, v.tensor_scalar(regi[:], regi[:], 9973, None, A.mult))
                ch(v, v.tensor_scalar(regi[:], regi[:], nb - 1, None, A.bitwise_and))
                ch(v, v.tensor_copy(bkt[:], regi[:]))
                v.tensor_copy(bkts[:, l, :], bkt[:])
                s_O.inc(v.tensor_tensor(
                    Obuf[:].rearrange("p c e -> p (c e)")[:, 0:16 * nb]
                    .rearrange("p (c e) -> p c e", e=nb),
                    iotan[:, 0:nb].rearrange("p (o e) -> p o e", o=1)
                    .broadcast_to([128, 16, nb]),
                    bkt[:].unsqueeze(2).broadcast_to([128, 16, nb]), A.is_equal))
                v.wait_ge(s_cnt.h, b_cnt + l + 1)
                if l == 2:
                    v.tensor_copy(tsa[0:1, 0, 0:512], PS_A[0:1, 0:512])
                s_ts.inc(v.tensor_copy(tstag[0:1, 0:16 * nb], PS_A[0:1, 0:16 * nb]))
                v.wait_ge(s_t16.h, b_t16 + 32 * (l + 1))
                v.tensor_copy(t16a[:, l, :], T16[:])
                v.wait_ge(s_pcm.h, b_pcm + 4 * l + 1)
                s_pcv.inc(v.tensor_copy(ccol[0:nb, :], PS_A[0:nb, 1200:1201]))
                v.wait_ge(s_pcm.h, b_pcm + 4 * l + 3)
                s_pcv.inc(v.tensor_copy(strow[0:1, 0:nb], PS_A[0:1, 1400:1400 + nb]))
                v.wait_ge(s_pcm.h, b_pcm + 4 * l + 4)
                s_pcv.inc(v.tensor_copy(carry[0:16, 0:nb], PS_A[0:16, 1600:1600 + nb]))
                for c in range(NC):
                    v.wait_ge(s_rp.h, b_rp + 16 * l + c + 1)
                    ch(v, v.tensor_tensor(rtmp[:, 0:nb],
                                          Obuf[:].rearrange("p c e -> p (c e)")[:, c * nb:(c + 1) * nb],
                                          PS_A[:, 1024:1024 + nb], A.mult))
                    s_rk.inc(v.tensor_reduce(ranks[:, c:c + 1].unsqueeze(2),
                                             rtmp[:, 0:nb].unsqueeze(1), AX.X, A.add))
                v.wait_ge(s_rk.h, b_rk + 16 * (l + 1))
                ch(v, v.tensor_copy(ranki[:], ranks[:]))
                ch(v, v.tensor_scalar(ranki[:], ranki[:], SHL[l], None, A.arith_shift_right))
                s_cv.inc(v.tensor_copy(cvals[:, l, :], ranki[:]))
                v.wait_ge(s_cv.h, b_cv + l + 1)
                v.tensor_tensor(
                    Ucol[:, :, toff:toff + nb],
                    iotan[:, 0:nb].rearrange("p (o e) -> p o e", o=1)
                    .broadcast_to([128, 16, nb]),
                    cvals[:, l, :].unsqueeze(2).broadcast_to([128, 16, nb]), A.is_equal)
                v.tensor_copy(Scol[:, :, toff:toff + nb], Ucol[:, :, toff:toff + nb])
                s_ust.inc(v.tensor_tensor(Scol[:, :, toff + 1:toff + nb],
                                          Scol[:, :, toff + 1:toff + nb],
                                          Ucol[:, :, toff:toff + nb - 1], A.add))
            for nt in range(4):
                v.wait_ge(s_t2.h, b_t2 + 2 * nt + 1)
                s_tev.inc(v.tensor_copy(T2sb[:, nt * 512:(nt + 1) * 512],
                                        PS_A[:, nt * 512:(nt + 1) * 512]))
                v.wait_ge(s_t2.h, b_t2 + 2 * nt + 2)
                s_tev.inc(v.tensor_copy(T01sb[0:40, nt * 512:(nt + 1) * 512],
                                        PS_C[0:40, nt * 512:(nt + 1) * 512]))
            for grp in range(4):
                v.wait_ge(s_ttm.h, b_ttm + 2 * grp + 1)
                s_ttv.inc(v.tensor_copy(
                    TTb[:, grp * 4:(grp + 1) * 4, 0:128],
                    PS_A[:, (grp % 2) * 512:(grp % 2) * 512 + 512]
                    .rearrange("p (j e) -> p j e", e=128)))
                v.wait_ge(s_ttm.h, b_ttm + 2 * grp + 2)
                s_ttv.inc(v.tensor_copy(
                    TTb[:, grp * 4:(grp + 1) * 4, 128:168],
                    PS_A[:, 1024 + (grp % 2) * 512:1024 + (grp % 2) * 512 + 512]
                    .rearrange("p (j e) -> p j e", e=128)[:, :, 0:40]))
            for l in range(3):
                nb = NBL[l]
                toff = TOFF[l]
                nh = 8 if nb == 128 else 16
                for half in range(2 if nb * 16 > 1024 else 1):
                    cs0 = half * 8
                    ncs = 8 if nb * 16 > 1024 else 16
                    dst = rsb[:, l, cs0:cs0 + ncs] if nb * 16 > 1024 else rsb[:, l, :]
                    v.tensor_tensor(rstmp[:, 0:ncs * nb].rearrange("p (c e) -> p c e", e=nb),
                                    Ucol[:, cs0:cs0 + ncs, toff:toff + nb],
                                    TTb[:, cs0:cs0 + ncs, toff:toff + nb], A.mult)
                    ch(v, v.tensor_reduce(dst.unsqueeze(2),
                                          rstmp[:, 0:ncs * nb].rearrange("p (c e) -> p c e", e=nb),
                                          AX.X, A.add))
                    v.tensor_tensor(rstmp[:, 0:ncs * (nb - 1)]
                                    .rearrange("p (c e) -> p c e", e=nb - 1),
                                    Ucol[:, cs0:cs0 + ncs, toff + 1:toff + nb],
                                    TTb[:, cs0:cs0 + ncs, toff:toff + nb - 1], A.mult)
                    ch(v, v.tensor_reduce(regf[:, cs0:cs0 + ncs].unsqueeze(2),
                                          rstmp[:, 0:ncs * (nb - 1)]
                                          .rearrange("p (c e) -> p c e", e=nb - 1),
                                          AX.X, A.add))
                    ch(v, v.tensor_tensor(dst, dst, regf[:, cs0:cs0 + ncs], A.add))
                    if nb * 16 <= 1024:
                        break
                ch(v, v.tensor_scalar(rsb[:, l, :], rsb[:, l, :], 3.0, None, A.mult))
                ch(v, v.reciprocal(rsb[:, l, :], rsb[:, l, :]))
                s_rs.inc(v.tensor_tensor(Acol[:, :, toff:toff + nb],
                                         Ucol[:, :, toff:toff + nb],
                                         rsb[:, l, :].unsqueeze(2)
                                         .broadcast_to([128, 16, nb]), A.mult))
            s_rs.inc(v.tensor_copy(rtmp[:, 0:1], rsb[:, 0, 0:1]))
            for wave in range(2):
                d2 = S2T if wave == 0 else A2T
                d01 = S01T if wave == 0 else A01T
                for grp in range(4):
                    gi = wave * 4 + grp
                    v.wait_ge(s_ttm.h, b_ttm + 8 + 2 * gi + 1)
                    s_ttv.inc(v.tensor_copy(
                        d2[:, grp * 512:(grp + 1) * 512],
                        PS_A[:, (gi % 2) * 512:(gi % 2) * 512 + 512]))
                    v.wait_ge(s_ttm.h, b_ttm + 8 + 2 * gi + 2)
                    s_ttv.inc(v.tensor_copy(
                        d01[0:40, grp * 512:(grp + 1) * 512],
                        PS_A[0:40, 1024 + (gi % 2) * 512:1024 + (gi % 2) * 512 + 512]))
            for jc in range(NC):
                v.wait_ge(s_f.h, b_f + jc + 1)
                s_pm.inc(v.tensor_tensor(E[:, jc, :], E[:, jc, :], PS_A[:, :], A.mult))
            v.wait_ge(s_av.h, b_av + NC)
            s_oe.inc(v.tensor_copy(stg[:], PS_C[0:64, :]))
            if hb == 0:
                s_oe.inc(v.tensor_copy(outT[0:64, h // 2, :], stg[:]))
            else:
                v.wait_ge(s_oe.h, b_oe + 2)
                s_oe.inc(v.tensor_copy(outT[64:128, h // 2, :], PS_A[64:128, :]))


def host_prep_l0attn(inputs):
    pr = np.ascontiguousarray(inputs["L0_attn_probes"].transpose(2, 0, 1, 3).reshape(64, 3 * 16 * 64))
    return {
        "a0_wq": inputs["L0_attn_Wq"], "a0_wk": inputs["L0_attn_Wk"],
        "a0_wv": inputs["L0_attn_Wv"], "a0_wo": inputs["L0_attn_Wo"],
        "a0_bqc": np.ascontiguousarray(inputs["L0_attn_bq"].reshape(8, 128).T),
        "a0_bkc": np.ascontiguousarray(inputs["L0_attn_bk"].reshape(8, 128).T),
        "a0_bvr": inputs["L0_attn_bv"].reshape(1, 1024),
        "a0_bor": inputs["L0_attn_bo"].reshape(1, 1024),
        "a0_probes": np.ascontiguousarray(np.tile(pr, (2, 1))),
    }


def build_l0attn_standalone():
    nc = bass.Bass("TRN2", target_bir_lowering=False, debug=False,
                   detect_race_conditions=False)
    k = K(nc)
    x = nc.dram_tensor("x", [T, D], F32, kind="ExternalInput").ap()
    y = nc.dram_tensor("y", [T, D], F32, kind="ExternalOutput").ap()
    declare_inputs(k, L0ATTN_SPECS)
    make_consts(k)
    xf = nc.alloc_sbuf_tensor("xf", [128, FC, T], BF16)
    s_ld = k.sem("xld")
    s_cv = k.sem("xcv")
    with ExitStack() as es:
        xr = es.enter_context(nc.sbuf_tensor("xr", [128, NC, D], F32))
        xb = es.enter_context(nc.sbuf_tensor("xb", [128, NC, D], BF16))
        with nc.Block() as blk:
            @blk.sync
            def _(sy):
                s_ld.inc(sy.dma_start(xr[:], x.rearrange("(c p) d -> p c d", p=128)), 16)
            @blk.vector
            def _(v):
                s_ld.wait(v)
                for c in range(NC):
                    s_cv.inc(v.tensor_copy(xb[:, c, :], xr[:, c, :]))
        emit_xf(k, xb, xf, "xf0")
    emit_l0attn(k, xf, x, y, tag="a0")
    return nc


def build_l0attn_debug():
    nc = bass.Bass("TRN2", target_bir_lowering=False, debug=False,
                   detect_race_conditions=False)
    k = K(nc)
    x = nc.dram_tensor("x", [T, D], F32, kind="ExternalInput").ap()
    y = nc.dram_tensor("y", [T, D], F32, kind="ExternalOutput").ap()
    dbg = nc.dram_tensor("dbg", [128, 4096], F32, kind="ExternalOutput").ap()
    declare_inputs(k, L0ATTN_SPECS)
    make_consts(k)
    xf = nc.alloc_sbuf_tensor("xf", [128, FC, T], BF16)
    s_ld = k.sem("xld")
    s_cv = k.sem("xcv")
    with ExitStack() as es:
        xr = es.enter_context(nc.sbuf_tensor("xr", [128, NC, D], F32))
        xb = es.enter_context(nc.sbuf_tensor("xb", [128, NC, D], BF16))
        with nc.Block() as blk:
            @blk.sync
            def _(sy):
                s_ld.inc(sy.dma_start(xr[:], x.rearrange("(c p) d -> p c d", p=128)), 16)
            @blk.vector
            def _(v):
                s_ld.wait(v)
                for c in range(NC):
                    s_cv.inc(v.tensor_copy(xb[:, c, :], xr[:, c, :]))
        emit_xf(k, xb, xf, "xf0")
    emit_l0attn(k, xf, x, y, tag="a0", dbg=dbg)
    return nc


def build_fused():
    nc = bass.Bass("TRN2", target_bir_lowering=False, debug=False,
                   detect_race_conditions=False)
    k = K(nc)
    x = nc.dram_tensor("x", [T, D], F32, kind="ExternalInput").ap()
    y = nc.dram_tensor("y", [T, D], F32, kind="ExternalOutput").ap()
    xr1 = nc.dram_tensor("fz_xr1", [T, D], F32, kind="Internal").ap()
    xr2 = nc.dram_tensor("fz_xr2", [T, D], F32, kind="Internal").ap()
    xr3 = nc.dram_tensor("fz_xr3", [T, D], F32, kind="Internal").ap()
    declare_inputs(k, L0ATTN_SPECS)
    declare_inputs(k, L0FFN_SPECS)
    declare_inputs(k, L1ATTN_SPECS)
    declare_inputs(k, L1FFN_SPECS)
    make_consts(k)
    xf = nc.alloc_sbuf_tensor("xf", [128, FC, T], BF16)
    s_ld = k.sem("xld")
    s_cv = k.sem("xcv")
    with ExitStack() as es:
        xr = es.enter_context(nc.sbuf_tensor("xr", [128, NC, D], F32))
        xb0 = es.enter_context(nc.sbuf_tensor("xb0", [128, NC, D], BF16))
        with nc.Block() as blk:
            @blk.sync
            def _(sy):
                s_ld.inc(sy.dma_start(xr[:], x.rearrange("(c p) d -> p c d", p=128)), 16)
            @blk.vector
            def _(v):
                s_ld.wait(v)
                for c in range(NC):
                    s_cv.inc(v.tensor_copy(xb0[:, c, :], xr[:, c, :]))
        emit_xf(k, xb0, xf, "xf0")
    emit_l0attn(k, xf, x, xr1, tag="a0")
    emit_l0ffn(k, xf, xr1, xr2, tag="f0")
    emit_l1attn(k, xf, xr2, xr3, tag="a1")
    emit_l1ffn(k, xr3, xf, y)
    return nc


def kernel(**inputs):
    x = np.ascontiguousarray(inputs["L0_attn_x"])
    pp = {}
    pp.update(host_prep_l0attn(inputs))
    pp.update(host_prep_l0ffn(inputs))
    pp.update(host_prep_l1attn(inputs))
    pp.update(host_prep_l1ffn(inputs))
    nc = _get_prog("fused", build_fused)
    in_maps = []
    for c in range(8):
        m = {"x": np.ascontiguousarray(x[c])}
        m.update(pp)
        in_maps.append(m)
    res = run_bass_kernel_spmd(nc, in_maps, core_ids=list(range(8)))
    return np.stack([res.results[c]["y"] for c in range(8)], axis=0)


# version 04 (final)
# speedup vs baseline: 1.1637x; 1.1445x over previous
import numpy as np
import ml_dtypes
from contextlib import ExitStack
import concourse.bass as bass
import concourse.mybir as mybir
from concourse.alu_op_type import AluOpType as A
from concourse.bass_utils import run_bass_kernel_spmd

F32 = mybir.dt.float32
BF16 = mybir.dt.bfloat16
I32 = mybir.dt.int32
AF = mybir.ActivationFunctionType
AX = mybir.AxisListType

T = 2048
D = 1024
DMAW = [16]
NC = 16
FC = 8
DH = 4096
BF = ml_dtypes.bfloat16


class Sem:
    def __init__(self, nc, name):
        self.h = nc.alloc_semaphore(name)
        self.n = 0

    def inc(self, instr, k=1):
        instr.then_inc(self.h, k)
        self.n += k
        return instr

    def wait(self, eng, n=None):
        eng.wait_ge(self.h, self.n if n is None else n)


class K:
    def __init__(self, nc):
        self.nc = nc
        self.sems = {}
        self.inp = {}

    def sem(self, name):
        if name not in self.sems:
            self.sems[name] = Sem(self.nc, name)
        return self.sems[name]

    def chainer(self, name):
        s = self.sem(name)
        def ch(eng, instr):
            s.inc(instr)
            eng.wait_ge(s.h, s.n)
            return instr
        return ch


def make_consts(k):
    nc = k.nc
    k.ident = nc.alloc_sbuf_tensor("c_ident", [128, 128], BF16)
    k.identf = nc.alloc_sbuf_tensor("c_identf", [128, 128], F32)
    k.ones_col = nc.alloc_sbuf_tensor("c_ones_col", [128, 1], BF16)
    k.ones_colf = nc.alloc_sbuf_tensor("c_ones_colf", [128, 1], F32)
    k.ones_row = nc.alloc_sbuf_tensor("c_ones_row", [1, 128], BF16)
    k.ones_rowf = nc.alloc_sbuf_tensor("c_ones_rowf", [1, 128], F32)
    k.eps_col = nc.alloc_sbuf_tensor("c_eps_col", [128, 1], F32)
    s = k.sem("consts")
    with nc.Block() as blk:
        @blk.gpsimd
        def _(g):
            g.iota(k.identf[:], pattern=[[1, 128]], base=0, channel_multiplier=-1,
                   allow_small_or_imprecise_dtypes=True)
            g.memset(k.ones_col[:], 1.0)
            g.memset(k.ones_row[:], 1.0)
            g.memset(k.ones_rowf[:], 1.0)
            g.memset(k.eps_col[:], 1e-5)
            s.inc(g.memset(k.ones_colf[:], 1.0))
        @blk.vector
        def _(v):
            s.wait(v)
            s.inc(v.tensor_scalar(k.ident[:], k.identf[:], 0.0, None, A.is_equal))
            s.inc(v.tensor_scalar(k.identf[:], k.identf[:], 0.0, None, A.is_equal))


_XFC = [0]


def emit_xf(k, xb, xf, tag):
    _XFC[0] += 1
    uniq = f"xfu{_XFC[0]}"
    tag = "xfc"
    """xb [128,16,1024] bf16 token-major -> xf [128,8,2048] bf16 feature-major."""
    nc = k.nc
    s_mm = k.sem(tag + "_mm")
    s_ev = k.sem(tag + "_ev")
    b_mm, b_ev = s_mm.n, s_ev.n
    with ExitStack() as es:
        ps = [es.enter_context(nc.psum_tensor(f"{uniq}_ps{i}", [128, 512], F32)) for i in range(2)]
        with nc.Block() as blk:
            @blk.tensor
            def _(te):
                grp = 0
                for fc in range(FC):
                    for cb in range(4):
                        buf = ps[grp % 2]
                        if grp >= 2:
                            te.wait_ge(s_ev.h, b_ev + grp - 1)
                        for j in range(4):
                            c = cb * 4 + j
                            mm = te.matmul(buf[:, j * 128:(j + 1) * 128],
                                           xb[:, c, fc * 128:(fc + 1) * 128],
                                           k.ident[:], start=True, stop=True)
                            if j == 3:
                                s_mm.inc(mm)
                        grp += 1
            @blk.vector
            def _(v):
                grp = 0
                for fc in range(FC):
                    for cb in range(4):
                        buf = ps[grp % 2]
                        v.wait_ge(s_mm.h, b_mm + grp + 1)
                        s_ev.inc(v.tensor_copy(xf[:, fc, cb * 512:(cb + 1) * 512], buf[:]))
                        grp += 1


def emit_ln(k, xr, tag, y_dram=None, xb=None):
    nc = k.nc
    s_sum = k.sem(tag + "_sum")
    s_sq = k.sem(tag + "_sq")
    s_stat = k.sem(tag + "_stat")
    s_app = k.sem(tag + "_app")
    s_bf = k.sem(tag + "_bf")
    s_st = k.sem(tag + "_st")
    with ExitStack() as es:
        junk = es.enter_context(nc.sbuf_tensor(tag + "_junk", [128, 1024], BF16))
        ssum = es.enter_context(nc.sbuf_tensor(tag + "_ssum", [128, 16], F32))
        ssq = es.enter_context(nc.sbuf_tensor(tag + "_ssq", [128, 16], F32))
        mean = es.enter_context(nc.sbuf_tensor(tag + "_mean", [128, 16], F32))
        var = es.enter_context(nc.sbuf_tensor(tag + "_var", [128, 16], F32))
        rstd = es.enter_context(nc.sbuf_tensor(tag + "_rstd", [128, 16], F32))
        nm = es.enter_context(nc.sbuf_tensor(tag + "_nm", [128, 16], F32))
        with nc.Block() as blk:
            @blk.vector
            def _(v):
                for c in range(NC):
                    s_sum.inc(v.tensor_reduce(
                        ssum[:, c:c + 1].unsqueeze(2),
                        xr[:, c:c + 1, :], AX.X, A.add))
            @blk.scalar
            def _(sc):
                for c in range(NC):
                    s_sq.inc(sc.activation(junk[:], xr[:, c, :], AF.Square,
                                           accum_out=ssq[:, c:c + 1]))
            @blk.vector
            def _(v):
                s_sum.wait(v)
                s_sq.wait(v)
                v.tensor_scalar(mean[:], ssum[:], 1.0 / D, None, A.mult)
                v.tensor_scalar(var[:], ssq[:], 1.0 / D, None, A.mult)
                v.tensor_tensor(nm[:], mean[:], mean[:], A.mult)
                s_stat.inc(v.tensor_tensor(var[:], var[:], nm[:], A.subtract))
            @blk.scalar
            def _(sc):
                s_stat.wait(sc)
                s_stat.inc(sc.activation(var[:], var[:], AF.Sqrt, bias=k.eps_col[:]))
            @blk.vector
            def _(v):
                s_stat.wait(v)
                v.reciprocal(rstd[:], var[:])
                v.tensor_tensor(nm[:], mean[:], rstd[:], A.mult)
                s_stat.inc(v.tensor_scalar(nm[:], nm[:], -1.0, None, A.mult))
            @blk.scalar
            def _(sc):
                s_stat.wait(sc)
                for c in range(NC):
                    s_app.inc(sc.activation(xr[:, c, :], xr[:, c, :], AF.Identity,
                                            bias=nm[:, c:c + 1], scale=rstd[:, c:c + 1]))
            if xb is not None:
                @blk.vector
                def _(v):
                    s_app.wait(v)
                    for c in range(NC):
                        s_bf.inc(v.tensor_copy(xb[:, c, :], xr[:, c, :]))
            if y_dram is not None:
                @blk.sync
                def _(sy):
                    s_app.wait(sy)
                    yv = y_dram.rearrange("(c p) d -> p c d", p=128)
                    s_st.inc(sy.dma_start(yv, xr[:]), 16)
                    s_st.wait(sy)


def emit_l1ffn(k, xr_dram, xf, y_dram, dbg_upto=None, dbg_out=None):
    nc = k.nc
    w1t, w2t = k.inp["l1_w1t"], k.inp["l1_w2t"]
    h1t, h2t = k.inp["l1_h1t"], k.inp["l1_h2t"]
    w3t = k.inp["l1_w3t"]
    HT = T // 2
    NOC = DH // 128

    xrv = xr_dram.rearrange("(c p) d -> p c d", p=128)
    yv = y_dram.rearrange("(c p) d -> p c d", p=128)

    s_sc = k.sem("ff_sc")
    s_s3 = k.sem("ff_s3")

    with ExitStack() as es:
        Aact = es.enter_context(nc.sbuf_tensor("ff_Aact", [128, NOC, HT], BF16))
        Bact = es.enter_context(nc.sbuf_tensor("ff_Bact", [128, NOC, HT], BF16))
        wbig = es.enter_context(nc.sbuf_tensor("ff_wbig", [128, 8192], BF16))
        wbA = [wbig[:, s * 1024:(s + 1) * 1024].rearrange("p (ic o) -> p ic o", o=128)
               for s in range(4)]
        wbH = [wbig[:, par * 4096:(par + 1) * 4096].rearrange("p (ic o) -> p ic o", o=128)
               for par in range(2)]
        wb3 = [wbig[:, par * 1024:(par + 1) * 1024] for par in range(2)]
        scols = es.enter_context(nc.sbuf_tensor("ff_scols", [128, 4, 32], F32))
        s3rep = es.enter_context(nc.sbuf_tensor("ff_s3rep", [128, 1024], F32))
        silu1 = es.enter_context(nc.sbuf_tensor("ff_silu1", [128, 512, 4], BF16))
        xrsl = es.enter_context(nc.sbuf_tensor("ff_xrsl", [128, 2, 1024], F32))
        vch = es.enter_context(nc.sbuf_tensor("ff_vch", [128, 2, 1024], F32))
        junkv = es.enter_context(nc.sbuf_tensor("ff_junkv", [128, 1024], F32))
        junkc = es.enter_context(nc.sbuf_tensor("ff_junkc", [128, 1], F32))
        statarch = es.enter_context(nc.sbuf_tensor("ff_statarch", [128, 16, 8], F32))
        lst = es.enter_context(nc.sbuf_tensor("ff_lst", [128, 2, 8], F32))
        psA = [es.enter_context(nc.psum_tensor(f"ff_ps{i}", [128, 512], F32)) for i in range(8)]

        with nc.Block() as blk0:
            @blk0.sync
            def _(sy):
                s_sc.inc(sy.dma_start(scols[:, 0, :], k.inp["l1_s1pc"]), 16)
                s_sc.inc(sy.dma_start(scols[:, 1, :], k.inp["l1_s2pc"]), 16)
                s_sc.inc(sy.dma_start(scols[:, 2, :], k.inp["l1_sh1pc"]), 16)
                s_sc.inc(sy.dma_start(scols[:, 3, :], k.inp["l1_sh2pc"]), 16)
                s_sc.inc(sy.dma_start(s3rep[0:1, :], k.inp["l1_s3row"]), 16)
            @blk0.tensor
            def _(te):
                s_sc.wait(te)
                s_s3.inc(te.matmul(psA[0][:], k.ones_rowf[:], s3rep[0:1, 0:512],
                                   start=True, stop=True))
                s_s3.inc(te.matmul(psA[1][:], k.ones_rowf[:], s3rep[0:1, 512:1024],
                                   start=True, stop=True))
            @blk0.vector
            def _(v):
                s_s3.wait(v)
                v.tensor_copy(s3rep[:, 0:512], psA[0][:])
                s_s3.inc(v.tensor_copy(s3rep[:, 512:1024], psA[1][:]))

        halves = [0] if dbg_upto is not None and dbg_upto < 4 else [0, 1]
        for H in halves:
            nbase = H * HT
            s_wl = k.sem("ffA_wl"); bwl = s_wl.n
            s_mmA = k.sem("ffA_mm"); bmmA = s_mmA.n
            s_sil = k.sem("ffA_sil"); bsil = s_sil.n
            s_evA = k.sem("ffA_ev"); bevA = s_evA.n

            with nc.named_scope(f"P_ffA{H}"), nc.Block() as blk:
                @blk.gpsimd
                def _(g):
                    for oc in range(NOC):
                        par = oc % 2
                        for wi, wt in ((0, w1t), (1, w2t)):
                            if oc >= 2:
                                g.wait_ge(s_mmA.h, bmmA + 4 * (oc - 2) + 2 * (wi + 1))
                            wv = wt.rearrange("(ic p) o -> p ic o", p=128)[:, :, oc * 128:(oc + 1) * 128]
                            s_wl.inc(g.dma_start(wbA[2 * par + wi], wv), 16)
                @blk.tensor
                def _(te):
                    for oc in range(NOC):
                        par = oc % 2
                        te.wait_ge(s_wl.h, bwl + (2 * oc + 2) * 16)
                        if oc >= 2:
                            te.wait_ge(s_sil.h, bsil + 2 * (oc - 2) + 2)
                            te.wait_ge(s_evA.h, bevA + 2 * (oc - 2) + 2)
                        for wi in range(2):
                            for nt in range(2):
                                bank = psA[4 * par + 2 * wi + nt]
                                for ic in range(FC):
                                    mm = te.matmul(bank[:], wbA[2 * par + wi][:, ic, :],
                                                   xf[:, ic, nbase + nt * 512: nbase + (nt + 1) * 512],
                                                   start=(ic == 0), stop=(ic == FC - 1))
                                    if ic == FC - 1:
                                        s_mmA.inc(mm)
                @blk.scalar
                def _(sc):
                    for oc in range(NOC):
                        par = oc % 2
                        for nt in range(2):
                            sc.wait_ge(s_mmA.h, bmmA + 4 * oc + nt + 1)
                            if oc >= 2:
                                sc.wait_ge(s_evA.h, bevA + 2 * (oc - 2) + nt + 1)
                            s_sil.inc(sc.activation(silu1[:, :, 2 * par + nt],
                                                    psA[4 * par + nt][:], AF.Silu,
                                                    scale=scols[:, 0, oc:oc + 1]))
                @blk.vector
                def _(v):
                    for oc in range(NOC):
                        par = oc % 2
                        for nt in range(2):
                            v.wait_ge(s_mmA.h, bmmA + 4 * oc + 2 + nt + 1)
                            v.wait_ge(s_sil.h, bsil + 2 * oc + nt + 1)
                            s_evA.inc(v.scalar_tensor_tensor(
                                Aact[:, oc, nt * 512:(nt + 1) * 512],
                                psA[4 * par + 2 + nt][:], scols[:, 1, oc:oc + 1],
                                silu1[:, :, 2 * par + nt], A.mult, A.mult))

            if dbg_upto == 0:
                sdbg = k.sem("dbg_st")
                with nc.Block() as blkd:
                    @blkd.sync
                    def _(sy):
                        sdbg.inc(sy.dma_start(dbg_out, Aact[:]), 16)
                        sdbg.wait(sy)
                return
            for stage in range(2):
                ht = h1t if stage == 0 else h2t
                src = Aact if stage == 0 else Bact
                dst = Bact if stage == 0 else Aact
                s_wlH = k.sem("ffH_wl"); bwlH = s_wlH.n
                s_mmH = k.sem("ffH_mm"); bmmH = s_mmH.n
                s_evH = k.sem("ffH_ev"); bevH = s_evH.n
                with nc.named_scope(f"P_ffH{H}{stage}"), nc.Block() as blk:
                    @blk.gpsimd
                    def _(g):
                        for oc in range(NOC):
                            par = oc % 2
                            if oc >= 2:
                                g.wait_ge(s_mmH.h, bmmH + 2 * (oc - 2) + 2)
                            wv = ht.rearrange("(ic p) o -> p ic o", p=128)[:, :, oc * 128:(oc + 1) * 128]
                            for q in range(4):
                                s_wlH.inc(g.dma_start(wbH[par][:, q * 8:(q + 1) * 8, :],
                                                      wv[:, q * 8:(q + 1) * 8, :]), 16)
                    @blk.tensor
                    def _(te):
                        for oc in range(NOC):
                            par = oc % 2
                            te.wait_ge(s_wlH.h, bwlH + (4 * oc + 4) * 16)
                            if oc >= 2:
                                te.wait_ge(s_evH.h, bevH + 2 * (oc - 2) + 2)
                            for nt in range(2):
                                bank = psA[4 * par + nt]
                                for ic in range(32):
                                    mm = te.matmul(bank[:], wbH[par][:, ic, :],
                                                   src[:, ic, nt * 512:(nt + 1) * 512],
                                                   start=(ic == 0), stop=(ic == 31))
                                    if ic == 31:
                                        s_mmH.inc(mm)
                    @blk.scalar
                    def _(sc):
                        for oc in range(NOC):
                            par = oc % 2
                            for nt in range(2):
                                sc.wait_ge(s_mmH.h, bmmH + 2 * oc + nt + 1)
                                s_evH.inc(sc.activation(dst[:, oc, nt * 512:(nt + 1) * 512],
                                                        psA[4 * par + nt][:], AF.Gelu,
                                                        scale=scols[:, 2 + stage, oc:oc + 1]))

                if dbg_upto == 1 + stage:
                    sdbg = k.sem("dbg_st")
                    with nc.Block() as blkd:
                        @blkd.sync
                        def _(sy):
                            sdbg.inc(sy.dma_start(dbg_out, dst[:]), 16)
                            sdbg.wait(sy)
                    return
            Cact = Aact
            s_wl3 = k.sem("ff3_wl"); bwl3 = s_wl3.n
            s_mm3 = k.sem("ff3_mm"); bmm3 = s_mm3.n
            s_xr = k.sem("ff3_xr"); bxr = s_xr.n
            s_v = k.sem("ff3_v"); bv3 = s_v.n
            s_l3 = k.sem("ff3_l3"); bl3 = s_l3.n
            s_l4 = k.sem("ff3_l4"); bl4 = s_l4.n
            s_l6 = k.sem("ff3_l6"); bl6 = s_l6.n
            s_st = k.sem("ff3_st"); bst = s_st.n
            with nc.named_scope(f"P_ff3{H}"), nc.Block() as blk:
                @blk.gpsimd
                def _(g):
                    for tsb in range(2):
                        for ic in range(32):
                            gi = tsb * 32 + ic
                            if gi >= 2:
                                g.wait_ge(s_mm3.h, bmm3 + gi - 1)
                            s_wl3.inc(g.dma_start(
                                wb3[gi % 2],
                                w3t.rearrange("(ic p) e -> p ic e", p=128)[:, ic, :]), 16)
                @blk.sync
                def _(sy):
                    for li in range(8):
                        c = H * 8 + li
                        if li >= 2:
                            sy.wait_ge(s_v.h, bv3 + 2 * (li - 1))
                        s_xr.inc(sy.dma_start(xrsl[:, li % 2, :], xrv[:, c, :]), 16)
                @blk.tensor
                def _(te):
                    for tsb in range(2):
                        if tsb == 1:
                            te.wait_ge(s_v.h, bv3 + 8)
                        for ic in range(32):
                            gi = tsb * 32 + ic
                            te.wait_ge(s_wl3.h, bwl3 + (gi + 1) * 16)
                            for ts in range(4):
                                for et in range(2):
                                    bank = psA[2 * ts + et]
                                    mm = te.matmul(
                                        bank[:],
                                        Cact[:, ic, (tsb * 4 + ts) * 128:(tsb * 4 + ts + 1) * 128],
                                        wb3[gi % 2][:, et * 512:(et + 1) * 512],
                                        start=(ic == 0), stop=(ic == 31),
                                        skip_group_check=True)
                                    if ts == 3 and et == 1:
                                        s_mm3.inc(mm)
                @blk.vector
                def _(v):
                    for li in range(8):
                        tsb, ts = li // 4, li % 4
                        pa = li % 2
                        v.wait_ge(s_xr.h, bxr + (li + 1) * 16)
                        if li >= 2:
                            v.wait_ge(s_st.h, bst + (li - 1) * 16)
                        v.wait_ge(s_mm3.h, bmm3 + 32 * (tsb + 1))
                        for et in range(2):
                            bank = psA[2 * ts + et]
                            v.tensor_tensor(vch[:, pa, et * 512:(et + 1) * 512],
                                            bank[:], s3rep[:, et * 512:(et + 1) * 512],
                                            A.mult)
                            s_v.inc(v.tensor_tensor(
                                vch[:, pa, et * 512:(et + 1) * 512],
                                vch[:, pa, et * 512:(et + 1) * 512],
                                xrsl[:, pa, et * 512:(et + 1) * 512], A.add))
                        if dbg_upto == 3:
                            continue
                        v.tensor_reduce(lst[:, pa:pa + 1, 0:1],
                                        vch[:, pa:pa + 1, :], AX.X, A.add)
                        v.tensor_tensor(junkv[:], vch[:, pa, :], vch[:, pa, :], A.mult)
                        v.tensor_reduce(lst[:, pa:pa + 1, 1:2],
                                        junkv[:].unsqueeze(1), AX.X, A.add)
                        v.tensor_scalar(lst[:, pa, 2:3], lst[:, pa, 0:1], 1.0 / D, None, A.mult)
                        v.tensor_scalar(lst[:, pa, 3:4], lst[:, pa, 1:2], 1.0 / D, None, A.mult)
                        v.tensor_tensor(lst[:, pa, 4:5], lst[:, pa, 2:3], lst[:, pa, 2:3], A.mult)
                        v.tensor_tensor(lst[:, pa, 3:4], lst[:, pa, 3:4],
                                        lst[:, pa, 4:5], A.subtract)
                        s_l3.inc(v.tensor_copy(lst[:, pa, 7:8], lst[:, pa, 3:4]))
                        v.wait_ge(s_l4.h, bl4 + 2 * (li + 1))
                        v.reciprocal(lst[:, pa, 5:6], lst[:, pa, 4:5])
                        s_l6.inc(v.tensor_scalar(vch[:, pa, :], vch[:, pa, :],
                                                 lst[:, pa, 2:3], lst[:, pa, 5:6],
                                                 A.subtract, A.mult))
                        if dbg_upto == 4:
                            v.tensor_copy(statarch[:, H * 8 + li, :], lst[:, pa, :])
                @blk.scalar
                def _(sc):
                    for li in range(8):
                        pa = li % 2
                        if dbg_upto == 3:
                            sc.wait_ge(s_v.h, 2 * (li + 1))
                            s_st.inc(sc.dma_start(yv[:, H * 8 + li, :], vch[:, pa, :]), 16)
                            continue
                        sc.wait_ge(s_l3.h, bl3 + li + 1)
                        sc.activation(lst[:, pa, 4:5], lst[:, pa, 7:8],
                                      AF.Sqrt, bias=k.eps_col[:])
                        s_l4.inc(sc.activation(junkc[:], lst[:, pa, 7:8],
                                               AF.Sqrt, bias=k.eps_col[:]), 2)
                        sc.wait_ge(s_l6.h, bl6 + li + 1)
                        s_st.inc(sc.dma_start(yv[:, H * 8 + li, :], vch[:, pa, :]), 16)
                    sc.wait_ge(s_st.h, bst + 8 * 16)
            if dbg_upto == 4 and H == 1:
                with nc.Block() as blkd2:
                    @blkd2.sync
                    def _(sy):
                        sdbg2 = k.sem("dbg_st2")
                        sdbg2.inc(sy.dma_start(dbg_out[:, 0:1, 0:256].bitcast(F32).rearrange("p a (b c) -> p (a b) c", c=8), statarch[:]), 16)
                        sdbg2.wait(sy)


def host_prep_l0ffn(inputs):
    return {
        "f0_win": inputs["L0_ffn_Win"],
        "f0_binc": inputs["L0_ffn_bin_"].reshape(64, 1),
        "f0_w1": inputs["L0_ffn_W1"],
        "f0_b1": inputs["L0_ffn_b1"],
        "f0_w2": inputs["L0_ffn_W2"],
        "f0_b2c": inputs["L0_ffn_b2"].reshape(64, 1),
        "f0_wout": inputs["L0_ffn_Wout"],
        "f0_bout": inputs["L0_ffn_bout"].reshape(1, 1024),
    }


def host_prep_l1attn(inputs):
    return {"l1_wq": inputs["L1_attn_Wq"], "l1_wk": inputs["L1_attn_Wk"],
            "l1_wv": inputs["L1_attn_Wv"], "l1_wo": inputs["L1_attn_Wo"]}


def host_prep_l1ffn(inputs):
    p = {}
    p["l1_w1t"] = np.ascontiguousarray(inputs["L1_ffn_w1_w"].T)
    p["l1_w2t"] = np.ascontiguousarray(inputs["L1_ffn_w2_w"].T)
    p["l1_h1t"] = np.ascontiguousarray(inputs["L1_ffn_h1_w"].T)
    p["l1_h2t"] = np.ascontiguousarray(inputs["L1_ffn_h2_w"].T)
    p["l1_w3t"] = np.ascontiguousarray(inputs["L1_ffn_w3_w"].T)
    for nm, src in (("l1_s1pc", "L1_ffn_w1_s"), ("l1_s2pc", "L1_ffn_w2_s"),
                    ("l1_sh1pc", "L1_ffn_h1_s"), ("l1_sh2pc", "L1_ffn_h2_s")):
        p[nm] = np.ascontiguousarray(inputs[src].reshape(32, 128).T)
    p["l1_s3row"] = np.ascontiguousarray(inputs["L1_ffn_w3_s"].reshape(1, 1024))
    return p


L1FFN_SPECS = [
    ("l1_w1t", [D, DH]), ("l1_w2t", [D, DH]),
    ("l1_h1t", [DH, DH]), ("l1_h2t", [DH, DH]),
    ("l1_w3t", [DH, D]),
    ("l1_s1pc", [128, 32]), ("l1_s2pc", [128, 32]),
    ("l1_sh1pc", [128, 32]), ("l1_sh2pc", [128, 32]),
    ("l1_s3row", [1, 1024]),
]


def declare_inputs(k, specs):
    for nm, shape in specs:
        k.inp[nm] = k.nc.dram_tensor(nm, shape, F32, kind="ExternalInput").ap()


def build_l1ffn_standalone():
    nc = bass.Bass("TRN2", target_bir_lowering=False, debug=False,
                   detect_race_conditions=False)
    k = K(nc)
    x = nc.dram_tensor("x", [T, D], F32, kind="ExternalInput").ap()
    y = nc.dram_tensor("y", [T, D], F32, kind="ExternalOutput").ap()
    declare_inputs(k, L1FFN_SPECS)
    make_consts(k)
    xf = nc.alloc_sbuf_tensor("xf", [128, FC, T], BF16)
    s_ld = k.sem("xld")
    s_cv = k.sem("xcv")
    with ExitStack() as es:
        xr = es.enter_context(nc.sbuf_tensor("xr", [128, NC, D], F32))
        xb = es.enter_context(nc.sbuf_tensor("xb", [128, NC, D], BF16))
        with nc.Block() as blk:
            @blk.sync
            def _(sy):
                s_ld.inc(sy.dma_start(xr[:], x.rearrange("(c p) d -> p c d", p=128)), 16)
            @blk.vector
            def _(v):
                s_ld.wait(v)
                for c in range(NC):
                    s_cv.inc(v.tensor_copy(xb[:, c, :], xr[:, c, :]))
        emit_xf(k, xb, xf, "xf0")
    emit_l1ffn(k, x, xf, y)
    return nc


def build_l1ffn_debug(upto):
    nc = bass.Bass("TRN2", target_bir_lowering=False, debug=False,
                   detect_race_conditions=False)
    k = K(nc)
    x = nc.dram_tensor("x", [T, D], F32, kind="ExternalInput").ap()
    y = nc.dram_tensor("y", [T, D], F32, kind="ExternalOutput").ap()
    dbg = nc.dram_tensor("dbg", [128, 32 * 1024], BF16, kind="ExternalOutput").ap()
    declare_inputs(k, L1FFN_SPECS)
    make_consts(k)
    xf = nc.alloc_sbuf_tensor("xf", [128, FC, T], BF16)
    s_ld = k.sem("xld")
    s_cv = k.sem("xcv")
    with ExitStack() as es:
        xr = es.enter_context(nc.sbuf_tensor("xr", [128, NC, D], F32))
        xb = es.enter_context(nc.sbuf_tensor("xb", [128, NC, D], BF16))
        with nc.Block() as blk:
            @blk.sync
            def _(sy):
                s_ld.inc(sy.dma_start(xr[:], x.rearrange("(c p) d -> p c d", p=128)), 16)
            @blk.vector
            def _(v):
                s_ld.wait(v)
                for c in range(NC):
                    s_cv.inc(v.tensor_copy(xb[:, c, :], xr[:, c, :]))
        emit_xf(k, xb, xf, "xf0")
    emit_l1ffn(k, x, xf, y, dbg_upto=upto, dbg_out=dbg.rearrange("p (a b) -> p a b", b=1024))
    return nc

_PROGS = {}


def _get_prog(name, builder):
    if name not in _PROGS:
        _PROGS[name] = builder()
    return _PROGS[name]


def kernel_multilaunch(**inputs):
    x = np.ascontiguousarray(inputs["L0_attn_x"])
    p_a0 = host_prep_l0attn(inputs)
    p_f0 = host_prep_l0ffn(inputs)
    p_a1 = host_prep_l1attn(inputs)
    p_f1 = host_prep_l1ffn(inputs)
    progs = [
        ("a0", build_l0attn_standalone, p_a0),
        ("f0", build_l0ffn_standalone, p_f0),
        ("a1", build_l1attn_standalone, p_a1),
        ("f1", build_l1ffn_standalone, p_f1),
    ]
    cur = [np.ascontiguousarray(x[c]) for c in range(8)]
    for name, builder, pp in progs:
        nc = _get_prog(name, builder)
        in_maps = []
        for c in range(8):
            m = {"x": cur[c]}
            m.update(pp)
            in_maps.append(m)
        res = run_bass_kernel_spmd(nc, in_maps, core_ids=list(range(8)))
        cur = [np.ascontiguousarray(res.results[c]["y"]) for c in range(8)]
    return np.stack(cur, axis=0)


L1ATTN_SPECS = [
    ("l1_wq", [D, D]), ("l1_wk", [D, D]), ("l1_wv", [D, D]), ("l1_wo", [D, D]),
]




def emit_tail_bodies(k, tag, blk, *, vch, xrsl, junkv, junkc, lst, xb, xrv_in,
                     xrv_out, ps_slice, s_mm, b_mm, mm_target, s_v, s_st_final=True):
    tag = "tlc"
    """Adds sync/vector/scalar bodies to blk implementing: per token chunk c:
    v = psum + xr_in[c]; LN(v) -> xr_out[c] (dram) and xb[:, c, :] (bf16).
    ps_slice(c, et) -> psum AP [128,512]; mm_target(c, et) -> absolute s_mm count.
    Caller's tensor body must gate psum reuse on s_v (2 incs per chunk)."""
    nc = k.nc
    s_l3 = k.sem(tag + "_tl3")
    s_l4 = k.sem(tag + "_tl4")
    s_l6 = k.sem(tag + "_tl6")
    s_st = k.sem(tag + "_tst")
    b_v, b_l3, b_l4, b_l6, b_st = s_v.n, s_l3.n, s_l4.n, s_l6.n, s_st.n
    s_xr = k.sem(tag + "_txr")
    b_xr = s_xr.n

    @blk.sync
    def _(sy):
        for c in range(NC):
            if c >= 2:
                sy.wait_ge(s_v.h, b_v + 2 * (c - 1))
            s_xr.inc(sy.dma_start(xrsl[:, c % 2, :], xrv_in[:, c, :]), 16)

    @blk.vector
    def _(v):
        for c in range(NC):
            pa = c % 2
            v.wait_ge(s_xr.h, b_xr + (c + 1) * 16)
            if c >= 2:
                v.wait_ge(s_st.h, b_st + (c - 1) * 16)
            for et in range(2):
                v.wait_ge(s_mm.h, b_mm + mm_target(c, et))
                s_v.inc(v.tensor_tensor(vch[:, pa, et * 512:(et + 1) * 512],
                                        ps_slice(c, et),
                                        xrsl[:, pa, et * 512:(et + 1) * 512], A.add))
            chv = k.chainer(tag + "_chv")
            ch2 = lambda i: chv(v, i)
            ch2(v.tensor_reduce(lst[:, pa:pa + 1, 0:1], vch[:, pa:pa + 1, :], AX.X, A.add))
            v.tensor_tensor(junkv[:], vch[:, pa, :], vch[:, pa, :], A.mult)
            ch2(v.tensor_reduce(lst[:, pa:pa + 1, 1:2], junkv[:].unsqueeze(1), AX.X, A.add))
            ch2(v.tensor_scalar(lst[:, pa, 2:3], lst[:, pa, 0:1], 1.0 / D, None, A.mult))
            ch2(v.tensor_scalar(lst[:, pa, 3:4], lst[:, pa, 1:2], 1.0 / D, None, A.mult))
            ch2(v.tensor_tensor(lst[:, pa, 4:5], lst[:, pa, 2:3], lst[:, pa, 2:3], A.mult))
            ch2(v.tensor_tensor(lst[:, pa, 3:4], lst[:, pa, 3:4], lst[:, pa, 4:5], A.subtract))
            s_l3.inc(v.tensor_copy(lst[:, pa, 7:8], lst[:, pa, 3:4]))
            v.wait_ge(s_l4.h, b_l4 + 2 * (c + 1))
            ch2(v.reciprocal(lst[:, pa, 5:6], lst[:, pa, 4:5]))
            v.tensor_scalar(vch[:, pa, :], vch[:, pa, :],
                            lst[:, pa, 2:3], lst[:, pa, 5:6], A.subtract, A.mult)
            if xb is not None:
                s_l6.inc(v.tensor_copy(xb[:, c, :], vch[:, pa, :]))
            else:
                s_l6.inc(v.tensor_copy(junkv[:, 0:1], lst[:, pa, 5:6]))

    @blk.scalar
    def _(sc):
        for c in range(NC):
            pa = c % 2
            sc.wait_ge(s_l3.h, b_l3 + c + 1)
            sc.activation(lst[:, pa, 4:5], lst[:, pa, 7:8], AF.Sqrt, bias=k.eps_col[:])
            s_l4.inc(sc.activation(junkc[:], lst[:, pa, 7:8], AF.Sqrt,
                                   bias=k.eps_col[:]), 2)
            sc.wait_ge(s_l6.h, b_l6 + c + 1)
            s_st.inc(sc.dma_start(xrv_out[:, c, :], vch[:, pa, :]), 16)
        if s_st_final:
            sc.wait_ge(s_st.h, b_st + NC * 16)


def emit_l1attn(k, xf, xr_in_dram, xr_out_dram, tag="a1"):
    nc = k.nc
    wq, wk = k.inp["l1_wq"], k.inp["l1_wk"]
    wv, wo = k.inp["l1_wv"], k.inp["l1_wo"]
    xrv_in = xr_in_dram.rearrange("(c p) d -> p c d", p=128)
    xrv_out = xr_out_dram.rearrange("(c p) d -> p c d", p=128)

    with ExitStack() as es:
        xb = es.enter_context(nc.sbuf_tensor(tag + "_xb", [128, NC, D], BF16))
        outT = es.enter_context(nc.sbuf_tensor(tag + "_outT", [128, FC, T], BF16))
        qf = es.enter_context(nc.sbuf_tensor(tag + "_qf", [128, 4, T], BF16))
        kf = es.enter_context(nc.sbuf_tensor(tag + "_kf", [128, 4, T], BF16))
        vg = es.enter_context(nc.sbuf_tensor(tag + "_vg", [128, NC, 8 * 65], BF16))
        wb = es.enter_context(nc.sbuf_tensor(tag + "_wb", [128, 2, 8, 512], BF16))
        e2 = es.enter_context(nc.sbuf_tensor(tag + "_e2", [128, 2, T], BF16))
        stg = es.enter_context(nc.sbuf_tensor(tag + "_stg", [64, T], BF16))
        shid = es.enter_context(nc.sbuf_tensor(tag + "_shid", [64, 128], BF16))
        cmask = es.enter_context(nc.sbuf_tensor(tag + "_cmask", [128, 4, 512], BF16))
        rsrow = es.enter_context(nc.sbuf_tensor(tag + "_rsrow", [1, T], F32))
        vch = es.enter_context(nc.sbuf_tensor(tag + "_vch", [128, 2, 1024], F32))
        xrsl = es.enter_context(nc.sbuf_tensor(tag + "_xrsl", [128, 2, 1024], F32))
        junkv = es.enter_context(nc.sbuf_tensor(tag + "_junkv", [128, 1024], F32))
        junkc = es.enter_context(nc.sbuf_tensor(tag + "_junkc", [128, 1], F32))
        lst = es.enter_context(nc.sbuf_tensor(tag + "_lst", [128, 2, 8], F32))
        pes = es.enter_context(ExitStack())
        ps_s = pes.enter_context(nc.psum_tensor(tag + "_pss", [128, T], F32))
        po = pes.enter_context(nc.psum_tensor(tag + "_po", [65, T], F32))

        s_ck = k.sem(tag + "_ck")
        for m in range(4):
            with nc.Block() as blk:
                @blk.gpsimd
                def _(g):
                    s_ck.inc(g.iota(junkv[:, 0:512], pattern=[[1, 512]], base=-128 * m,
                                    channel_multiplier=-1, allow_small_or_imprecise_dtypes=True))
                @blk.vector
                def _(v):
                    s_ck.wait(v)
                    s_ck.inc(v.tensor_scalar(cmask[:, m, :], junkv[:, 0:512], 0.0, None, A.is_ge))
        with nc.Block() as blk:
            @blk.gpsimd
            def _(g):
                s_ck.inc(g.iota(junkv[0:64, 0:128], pattern=[[1, 128]], base=-64,
                                channel_multiplier=-1, allow_small_or_imprecise_dtypes=True))
            @blk.vector
            def _(v):
                s_ck.wait(v)
                s_ck.inc(v.tensor_scalar(shid[:], junkv[0:64, 0:128], 0.0, None, A.is_equal))

        for g_i in range(2):
            s_wl = k.sem(tag + "_wlg")
            s_pj = k.sem(tag + "_pjg")
            s_pe = k.sem(tag + "_peg")
            b_wl, b_pj, b_pe = s_wl.n, s_pj.n, s_pe.n
            with nc.Block() as blk:
                @blk.gpsimd
                def _(g):
                    for wi, wt, slot in ((0, wq, 0), (1, wk, 1), (2, wv, 0)):
                        if wi == 2:
                            g.wait_ge(s_pj.h, b_pj + 16)
                        wvw = wt.rearrange("(ic p) o -> p ic o", p=128)[:, :, g_i * 512:(g_i + 1) * 512]
                        s_wl.inc(g.dma_start(wb[:, slot, :, :], wvw), 16)
                @blk.tensor
                def _(te):
                    seq = 0
                    for wi in range(2):
                        te.wait_ge(s_wl.h, b_wl + (wi + 1) * 16)
                        for oc in range(4):
                            for nt in range(4):
                                if seq >= 4:
                                    te.wait_ge(s_pe.h, b_pe + seq - 3)
                                for ic in range(FC):
                                    mm = te.matmul(ps_s[:, nt * 512:(nt + 1) * 512],
                                                   wb[:, wi % 2, ic, oc * 128:(oc + 1) * 128],
                                                   xf[:, ic, nt * 512:(nt + 1) * 512],
                                                   start=(ic == 0), stop=(ic == FC - 1),
                                                   skip_group_check=True)
                                    if ic == FC - 1:
                                        s_pj.inc(mm)
                                seq += 1
                    te.wait_ge(s_wl.h, b_wl + 48)
                    for tc in range(NC):
                        if seq >= 4:
                            te.wait_ge(s_pe.h, b_pe + seq - 3)
                        for ic in range(FC):
                            mm = te.matmul(ps_s[:, (tc % 4) * 512:(tc % 4 + 1) * 512],
                                           xf[:, ic, tc * 128:(tc + 1) * 128],
                                           wb[:, 0, ic, :],
                                           start=(ic == 0), stop=(ic == FC - 1),
                                           skip_group_check=True)
                            if ic == FC - 1:
                                s_pj.inc(mm)
                        seq += 1
                @blk.vector
                def _(v):
                    cnt = 0
                    for wi in range(2):
                        dst = qf if wi == 0 else kf
                        for oc in range(4):
                            for nt in range(4):
                                cnt += 1
                                v.wait_ge(s_pj.h, b_pj + cnt)
                                s_pe.inc(v.tensor_copy(dst[:, oc, nt * 512:(nt + 1) * 512],
                                                       ps_s[:, nt * 512:(nt + 1) * 512]))
                    for tc in range(NC):
                        cnt += 1
                        v.wait_ge(s_pj.h, b_pj + cnt)
                        s_pe.inc(v.tensor_copy(
                            vg[:, tc, 0:520].rearrange("p (h e) -> p h e", e=65)[:, :, 0:64],
                            ps_s[:, (tc % 4) * 512:(tc % 4 + 1) * 512]
                            .rearrange("p (h e) -> p h e", e=64)))
                    s_pe.inc(v.memset(
                        vg[:, :, 0:520].rearrange("p c (h e) -> p c h e", e=65)[:, :, :, 64:65], 1.0))

            for hh in range(8):
                h = g_i * 8 + hh
                hb = (h % 2) * 64
                hc = h // 2
                s_sc = k.sem(tag + "_sc")
                s_ex = k.sem(tag + "_ex")
                s_mk = k.sem(tag + "_mk")
                s_av = k.sem(tag + "_av")
                s_ev = k.sem(tag + "_ev")
                b_sc, b_ex, b_mk, b_av, b_ev = s_sc.n, s_ex.n, s_mk.n, s_av.n, s_ev.n
                with nc.Block() as blk:
                    @blk.tensor
                    def _(te):
                        for jc in range(NC):
                            nt0 = jc // 4
                            if jc > 0:
                                te.wait_ge(s_ex.h, b_ex + jc)
                            for nt in range(nt0, 4):
                                mm = te.matmul(ps_s[:, nt * 512:(nt + 1) * 512],
                                               kf[hb:hb + 64, hc % 4, jc * 128:(jc + 1) * 128],
                                               qf[hb:hb + 64, hc % 4, nt * 512:(nt + 1) * 512],
                                               start=True, stop=True,
                                               skip_group_check=True)
                                if nt == 3:
                                    s_sc.inc(mm)
                            te.wait_ge(s_mk.h, b_mk + jc + 1)
                            for nt in range(nt0, 4):
                                mm = te.matmul(po[:, nt * 512:(nt + 1) * 512],
                                               vg[:, jc, hh * 65:(hh + 1) * 65],
                                               e2[:, jc % 2, nt * 512:(nt + 1) * 512],
                                               start=(jc == 0), stop=(jc == 4 * nt + 3),
                                               skip_group_check=True)
                                if nt == 3:
                                    s_av.inc(mm)
                        te.wait_ge(s_ev.h, b_ev + 1)
                        for nt in range(4):
                            mm = te.matmul(ps_s[:, nt * 512:(nt + 1) * 512],
                                           k.ones_rowf[:, 0:128],
                                           rsrow[:, nt * 512:(nt + 1) * 512],
                                           start=True, stop=True, skip_group_check=True)
                            if nt == 3:
                                s_ev.inc(mm)
                        if hb == 64:
                            te.wait_ge(s_ev.h, b_ev + 3)
                            for nt in range(4):
                                mm = te.matmul(ps_s[:, nt * 512:(nt + 1) * 512],
                                               shid[:], stg[:, nt * 512:(nt + 1) * 512],
                                               start=True, stop=True, skip_group_check=True)
                                if nt == 3:
                                    s_ev.inc(mm)
                    @blk.scalar
                    def _(sc):
                        for jc in range(NC):
                            nt0 = jc // 4
                            sc.wait_ge(s_sc.h, b_sc + jc + 1)
                            if jc >= 2:
                                sc.wait_ge(s_av.h, b_av + jc - 1)
                            s_ex.inc(sc.activation(e2[:, jc % 2, nt0 * 512:T],
                                                   ps_s[:, nt0 * 512:T], AF.Exp, scale=0.125))
                    @blk.vector
                    def _(v):
                        for jc in range(NC):
                            nt0 = jc // 4
                            v.wait_ge(s_ex.h, b_ex + jc + 1)
                            s_mk.inc(v.tensor_tensor(
                                e2[:, jc % 2, nt0 * 512:(nt0 + 1) * 512],
                                e2[:, jc % 2, nt0 * 512:(nt0 + 1) * 512],
                                cmask[:, jc % 4, :], A.mult))
                        v.wait_ge(s_av.h, b_av + NC)
                        v.tensor_copy(rsrow[:, :], po[64:65, :])
                        v.reciprocal(rsrow[:, :], rsrow[:, :])
                        s_ev.inc(v.tensor_copy(stg[:], po[0:64, :]))
                        v.wait_ge(s_ev.h, b_ev + 2)
                        s_ev.inc(v.tensor_tensor(stg[:], stg[:], ps_s[0:64, :], A.mult))
                        if hb == 0:
                            s_ev.inc(v.tensor_copy(outT[0:64, hc, :], stg[:]))
                        else:
                            v.wait_ge(s_ev.h, b_ev + 4)
                            s_ev.inc(v.tensor_copy(outT[64:128, hc, :], ps_s[64:128, :]))

        s_wl = k.sem(tag + "_wlo")
        s_mm = k.sem(tag + "_mmo")
        s_xr = k.sem(tag + "_xro")
        s_v = k.sem(tag + "_vo")
        s_l3 = k.sem(tag + "_l3o")
        s_l4 = k.sem(tag + "_l4o")
        s_l6 = k.sem(tag + "_l6o")
        s_st = k.sem(tag + "_sto")
        with nc.Block() as blk:
            @blk.gpsimd
            def _(g):
                wvw = wo.rearrange("(ic p) o -> p ic o", p=128)
                for half in range(2):
                    s_wl.inc(g.dma_start(wb[:, half, :, :], wvw[:, :, half * 512:(half + 1) * 512]), 16)
            @blk.tensor
            def _(te):
                te.wait_ge(s_wl.h, 32)
                for c in range(NC):
                    if c >= 2:
                        te.wait_ge(s_v.h, 2 * (c - 1))
                    for et in range(2):
                        for fc in range(FC):
                            mm = te.matmul(ps_s[:, (2 * (c % 2) + et) * 512:(2 * (c % 2) + et + 1) * 512],
                                           outT[:, fc, c * 128:(c + 1) * 128],
                                           wb[:, et, fc, :],
                                           start=(fc == 0), stop=(fc == FC - 1),
                                           skip_group_check=True)
                            if fc == FC - 1:
                                s_mm.inc(mm)
            emit_tail_bodies(k, tag + "_t", blk, vch=vch, xrsl=xrsl, junkv=junkv,
                             junkc=junkc, lst=lst, xb=xb, xrv_in=xrv_in,
                             xrv_out=xrv_out,
                             ps_slice=lambda c, et: ps_s[:, (2 * (c % 2) + et) * 512:(2 * (c % 2) + et + 1) * 512],
                             s_mm=s_mm, b_mm=0, mm_target=lambda c, et: 2 * c + et + 1,
                             s_v=s_v)
        pes.close()
        emit_xf(k, xb, xf, tag + "_xf")


def build_l1attn_standalone():
    nc = bass.Bass("TRN2", target_bir_lowering=False, debug=False,
                   detect_race_conditions=False)
    k = K(nc)
    x = nc.dram_tensor("x", [T, D], F32, kind="ExternalInput").ap()
    y = nc.dram_tensor("y", [T, D], F32, kind="ExternalOutput").ap()
    declare_inputs(k, L1ATTN_SPECS)
    make_consts(k)
    xf = nc.alloc_sbuf_tensor("xf", [128, FC, T], BF16)
    s_ld = k.sem("xld")
    s_cv = k.sem("xcv")
    with ExitStack() as es:
        xr = es.enter_context(nc.sbuf_tensor("xr", [128, NC, D], F32))
        xb = es.enter_context(nc.sbuf_tensor("xb", [128, NC, D], BF16))
        with nc.Block() as blk:
            @blk.sync
            def _(sy):
                s_ld.inc(sy.dma_start(xr[:], x.rearrange("(c p) d -> p c d", p=128)), 16)
            @blk.vector
            def _(v):
                s_ld.wait(v)
                for c in range(NC):
                    s_cv.inc(v.tensor_copy(xb[:, c, :], xr[:, c, :]))
        emit_xf(k, xb, xf, "xf0")
    emit_l1attn(k, xf, x, y, tag="a1")
    return nc


_PROGS = {}


def _get_prog(name, builder):
    if name not in _PROGS:
        _PROGS[name] = builder()
    return _PROGS[name]


def kernel_multilaunch(**inputs):
    x = np.ascontiguousarray(inputs["L0_attn_x"])
    p_a0 = host_prep_l0attn(inputs)
    p_f0 = host_prep_l0ffn(inputs)
    p_a1 = host_prep_l1attn(inputs)
    p_f1 = host_prep_l1ffn(inputs)
    progs = [
        ("a0", build_l0attn_standalone, p_a0),
        ("f0", build_l0ffn_standalone, p_f0),
        ("a1", build_l1attn_standalone, p_a1),
        ("f1", build_l1ffn_standalone, p_f1),
    ]
    cur = [np.ascontiguousarray(x[c]) for c in range(8)]
    for name, builder, pp in progs:
        nc = _get_prog(name, builder)
        in_maps = []
        for c in range(8):
            m = {"x": cur[c]}
            m.update(pp)
            in_maps.append(m)
        res = run_bass_kernel_spmd(nc, in_maps, core_ids=list(range(8)))
        cur = [np.ascontiguousarray(res.results[c]["y"]) for c in range(8)]
    return np.stack(cur, axis=0)


L0FFN_SPECS = [
    ("f0_win", [D, 64]), ("f0_binc", [64, 1]), ("f0_w1", [64, 16]),
    ("f0_b1", [64, 16]), ("f0_w2", [64, 16]), ("f0_b2c", [64, 1]),
    ("f0_wout", [64, D]), ("f0_bout", [1, D]),
]


def emit_l0ffn(k, xf, xr_in_dram, xr_out_dram, tag="f0"):
    nc = k.nc
    xrv_in = xr_in_dram.rearrange("(c p) d -> p c d", p=128)
    xrv_out = xr_out_dram.rearrange("(c p) d -> p c d", p=128)
    with ExitStack() as es:
        xb = es.enter_context(nc.sbuf_tensor(tag + "_xb", [128, NC, D], BF16))
        winb = es.enter_context(nc.sbuf_tensor(tag + "_winb", [128, 8, 64], BF16))
        woutb = es.enter_context(nc.sbuf_tensor(tag + "_woutb", [64, D], BF16))
        w1 = es.enter_context(nc.sbuf_tensor(tag + "_w1sb", [64, 16], F32))
        bias2 = es.enter_context(nc.sbuf_tensor(tag + "_bias2", [64, 16], F32))
        b1 = es.enter_context(nc.sbuf_tensor(tag + "_b1sb", [64, 16], F32))
        w2 = es.enter_context(nc.sbuf_tensor(tag + "_w2sb", [64, 16], F32))
        b2c = es.enter_context(nc.sbuf_tensor(tag + "_b2csb", [64, 1], F32))
        binc = es.enter_context(nc.sbuf_tensor(tag + "_bincsb", [64, 1], F32))
        boutr = es.enter_context(nc.sbuf_tensor(tag + "_boutr", [1, D], F32))
        hbuf = es.enter_context(nc.sbuf_tensor(tag + "_hbuf", [64, 16, 512], BF16))
        gbuf = es.enter_context(nc.sbuf_tensor(tag + "_gbuf", [64, T], F32))
        gb = es.enter_context(nc.sbuf_tensor(tag + "_gb", [64, T], BF16))
        vch = es.enter_context(nc.sbuf_tensor(tag + "_vch", [128, 2, 1024], F32))
        xrsl = es.enter_context(nc.sbuf_tensor(tag + "_xrsl", [128, 2, 1024], F32))
        junkv = es.enter_context(nc.sbuf_tensor(tag + "_junkv", [128, 1024], F32))
        junkc = es.enter_context(nc.sbuf_tensor(tag + "_junkc", [128, 1], F32))
        lst = es.enter_context(nc.sbuf_tensor(tag + "_lst", [128, 2, 8], F32))
        pes = es.enter_context(ExitStack())
        ps_s = pes.enter_context(nc.psum_tensor(tag + "_pss", [64, T], F32))
        ps_o = pes.enter_context(nc.psum_tensor(tag + "_pso", [128, 4, 512], F32))

        s_wl = k.sem(tag + "_wl")
        s_fl = k.sem(tag + "_fl")
        s_pp = k.sem(tag + "_pp")
        s_sm = k.sem(tag + "_sm")
        s_ha = k.sem(tag + "_ha")
        s_gg = k.sem(tag + "_gg")
        s_gb = k.sem(tag + "_gb")
        s_mm = k.sem(tag + "_mm")
        s_v = k.sem(tag + "_v")
        with nc.Block() as blk:
            @blk.gpsimd
            def _(g):
                s_wl.inc(g.dma_start(winb[:],
                                     k.inp["f0_win"].rearrange("(ic p) m -> p ic m", p=128)), 16)
                s_wl.inc(g.dma_start(woutb[:], k.inp["f0_wout"]), 16)
            @blk.sync
            def _(sy):
                for nm, dst in (("f0_w1", w1), ("f0_b1", b1), ("f0_w2", w2),
                                ("f0_b2c", b2c), ("f0_binc", binc), ("f0_bout", boutr)):
                    s_fl.inc(sy.dma_start(dst[:], k.inp[nm]), 16)
            @blk.vector
            def _(v):
                s_fl.wait(v)
                s_pp.inc(v.scalar_tensor_tensor(bias2[:], w1[:], binc[:], b1[:],
                                                A.mult, A.add))
            @blk.tensor
            def _(te):
                te.wait_ge(s_wl.h, 16)
                for nt in range(4):
                    for ic in range(FC):
                        mm = te.matmul(ps_s[:, nt * 512:(nt + 1) * 512],
                                       winb[:, ic, :],
                                       xf[:, ic, nt * 512:(nt + 1) * 512],
                                       start=(ic == 0), stop=(ic == FC - 1),
                                       skip_group_check=True)
                        if ic == FC - 1:
                            s_sm.inc(mm)
                te.wait_ge(s_wl.h, 32)
                for c in range(NC):
                    te.wait_ge(s_gb.h, c // 4 + 1)
                    if c >= 2:
                        te.wait_ge(s_v.h, 2 * (c - 1))
                    for et in range(2):
                        bank = ps_o[:, 2 * (c % 2) + et, :]
                        mm = te.matmul(bank, gb[:, c * 128:(c + 1) * 128],
                                       woutb[:, et * 512:(et + 1) * 512],
                                       start=True, stop=False, skip_group_check=True)
                        mm = te.matmul(bank, k.ones_rowf[:, 0:128],
                                       boutr[:, et * 512:(et + 1) * 512],
                                       start=False, stop=True, skip_group_check=True)
                        s_mm.inc(mm)
            @blk.scalar
            def _(sc):
                sc.wait_ge(s_pp.h, 1)
                for nt in range(4):
                    sc.wait_ge(s_sm.h, nt + 1)
                    if nt >= 1:
                        sc.wait_ge(s_gg.h, nt)
                    for i in range(16):
                        s_ha.inc(sc.activation(hbuf[:, i, :],
                                               ps_s[:, nt * 512:(nt + 1) * 512],
                                               AF.Relu, scale=w1[:, i:i + 1],
                                               bias=bias2[:, i:i + 1]))
            @blk.vector
            def _(v):
                for nt in range(4):
                    for i in range(16):
                        v.wait_ge(s_ha.h, 16 * nt + i + 1)
                        if i == 0:
                            v.tensor_scalar(gbuf[:, nt * 512:(nt + 1) * 512],
                                            hbuf[:, 0, :], w2[:, 0:1], None, A.mult)
                        else:
                            v.scalar_tensor_tensor(gbuf[:, nt * 512:(nt + 1) * 512],
                                                   hbuf[:, i, :], w2[:, i:i + 1],
                                                   gbuf[:, nt * 512:(nt + 1) * 512],
                                                   A.mult, A.add)
                    v.tensor_scalar(gbuf[:, nt * 512:(nt + 1) * 512],
                                    gbuf[:, nt * 512:(nt + 1) * 512],
                                    b2c[:], None, A.add)
                    s_gg.inc(v.tensor_copy(gb[:, nt * 512:(nt + 1) * 512],
                                           gbuf[:, nt * 512:(nt + 1) * 512]))
                    s_gb.inc(v.tensor_copy(junkc[0:64, :], gbuf[:, nt * 512:nt * 512 + 1]))
            emit_tail_bodies(k, tag + "_t", blk, vch=vch, xrsl=xrsl, junkv=junkv,
                             junkc=junkc, lst=lst, xb=xb, xrv_in=xrv_in,
                             xrv_out=xrv_out,
                             ps_slice=lambda c, et: ps_o[:, 2 * (c % 2) + et, :],
                             s_mm=s_mm, b_mm=0, mm_target=lambda c, et: 2 * c + et + 1,
                             s_v=s_v)
        pes.close()
        emit_xf(k, xb, xf, tag + "_xf")


def build_l0ffn_standalone():
    nc = bass.Bass("TRN2", target_bir_lowering=False, debug=False,
                   detect_race_conditions=False)
    k = K(nc)
    x = nc.dram_tensor("x", [T, D], F32, kind="ExternalInput").ap()
    y = nc.dram_tensor("y", [T, D], F32, kind="ExternalOutput").ap()
    declare_inputs(k, L0FFN_SPECS)
    make_consts(k)
    xf = nc.alloc_sbuf_tensor("xf", [128, FC, T], BF16)
    s_ld = k.sem("xld")
    s_cv = k.sem("xcv")
    with ExitStack() as es:
        xr = es.enter_context(nc.sbuf_tensor("xr", [128, NC, D], F32))
        xb = es.enter_context(nc.sbuf_tensor("xb", [128, NC, D], BF16))
        with nc.Block() as blk:
            @blk.sync
            def _(sy):
                s_ld.inc(sy.dma_start(xr[:], x.rearrange("(c p) d -> p c d", p=128)), 16)
            @blk.vector
            def _(v):
                s_ld.wait(v)
                for c in range(NC):
                    s_cv.inc(v.tensor_copy(xb[:, c, :], xr[:, c, :]))
        emit_xf(k, xb, xf, "xf0")
    emit_l0ffn(k, xf, x, y, tag="f0")
    return nc


L0ATTN_SPECS = [
    ("a0_wq", [D, D]), ("a0_wk", [D, D]), ("a0_wv", [D, D]), ("a0_wo", [D, D]),
    ("a0_bqc", [128, 8]), ("a0_bkc", [128, 8]), ("a0_bvr", [1, D]), ("a0_bor", [1, D]),
    ("a0_probes", [64, 3 * 16 * 64]),
]
NBL = [128, 8, 32]
CSL = [16, 256, 64]
SHL = [4, 8, 6]
TOFF = [0, 128, 136]
REFL = [2, 0, 1]
NT168 = 168


def emit_l0attn(k, xf, x_in_dram, xr_out_dram, tag="a0", dbg=None):
    nc = k.nc
    xrv_in = x_in_dram.rearrange("(c p) d -> p c d", p=128)
    xrv_out = xr_out_dram.rearrange("(c p) d -> p c d", p=128)
    scr = nc.dram_tensor(tag + "_scr", [1, 2048], BF16, kind="Internal").ap()

    with ExitStack() as es:
        outT = es.enter_context(nc.sbuf_tensor(tag + "_outT", [128, FC, T], BF16))
        probesb = es.enter_context(nc.sbuf_tensor(tag + "_probesb", [128, 3072], BF16))
        L128 = es.enter_context(nc.sbuf_tensor(tag + "_L128", [128, 128], BF16))
        L16 = es.enter_context(nc.sbuf_tensor(tag + "_L16", [16, 16], BF16))
        SELf = es.enter_context(nc.sbuf_tensor(tag + "_SELf", [16, T], F32))
        Unb = [es.enter_context(nc.sbuf_tensor(f"{tag}_Unb{l}", [NBL[l], NBL[l]], F32))
               for l in range(3)]
        iotan = es.enter_context(nc.sbuf_tensor(tag + "_iotan", [128, 128], F32))
        iota64r = es.enter_context(nc.sbuf_tensor(tag + "_iota64r", [128, 64], F32))
        tb = [es.enter_context(nc.sbuf_tensor(f"{tag}_tb{l}", [128, 64], F32))
              for l in range(3)]
        tbi = es.enter_context(nc.sbuf_tensor(tag + "_tbi", [128, 64], I32))
        tbi2 = es.enter_context(nc.sbuf_tensor(tag + "_tbi2", [128, 64], I32))
        shid = es.enter_context(nc.sbuf_tensor(tag + "_shid", [64, 128], BF16))
        bqc = es.enter_context(nc.sbuf_tensor(tag + "_bqcs", [128, 8], F32))
        bkc = es.enter_context(nc.sbuf_tensor(tag + "_bkcs", [128, 8], F32))
        bvr = es.enter_context(nc.sbuf_tensor(tag + "_bvrs", [1, D], F32))

        s_ct = k.sem(tag + "_ct")
        with nc.Block() as blk:
            @blk.sync
            def _(sy):
                for _dst, _src in ((bqc, "a0_bqc"), (bkc, "a0_bkc"), (bvr, "a0_bvr")):
                    sy.dma_start(_dst[:], k.inp[_src]).then_inc(s_ct.h, 16)
                    s_ct.n += DMAW[0]
            @blk.gpsimd
            def _(g):
                s_ctg = k.sem(tag + "_ctg")
                s_ctg.inc(g.dma_start(probesb[:], k.inp["a0_probes"]), 16)
                g.iota(iotan[:], pattern=[[1, 128]], base=0, channel_multiplier=0,
                       allow_small_or_imprecise_dtypes=True)
                g.iota(iota64r[:], pattern=[[1, 64]], base=0, channel_multiplier=0,
                       allow_small_or_imprecise_dtypes=True)
                s_ct.inc(g.iota(SELf[:], pattern=[[1, 16], [0, 128]], base=0,
                                channel_multiplier=-1, allow_small_or_imprecise_dtypes=True))
            @blk.vector
            def _(v):
                s_ct.wait(v)
                s_ct.inc(v.tensor_scalar(SELf[:], SELf[:], 0.0, None, A.is_equal))
                v.tensor_copy(tbi[:], iota64r[:])
                v.tensor_scalar(tbi2[:], tbi[:], 9973, None, A.mult)
                for l in range(3):
                    v.tensor_scalar(tbi[:], tbi2[:], NBL[l] - 1, None, A.bitwise_and)
                    s_ct.inc(v.tensor_copy(tb[l][:], tbi[:]))
        with nc.Block() as blk:
            @blk.gpsimd
            def _(g):
                s_ct.inc(g.iota(SELf[0:16, 0:128].bitcast(F32), pattern=[[1, 128]], base=0,
                                channel_multiplier=-1, allow_small_or_imprecise_dtypes=True)) \
                    if False else None
                s_ct.inc(g.iota(iotan[:], pattern=[[1, 128]], base=0, channel_multiplier=-1,
                                allow_small_or_imprecise_dtypes=True))
            @blk.vector
            def _(v):
                s_ct.wait(v)
                v.tensor_scalar(L128[:], iotan[:], 0.0, None, A.is_gt)
                v.tensor_scalar(L16[:], iotan[0:16, 0:16], 0.0, None, A.is_gt)
                for l in range(3):
                    nb = NBL[l]
                    v.tensor_scalar(Unb[l][:], iotan[0:nb, 0:nb], 0.0, None, A.is_gt)
                v.tensor_scalar(iotan[0:64, :], iotan[0:64, :], 64.0, None, A.subtract)
                s_ct.inc(v.tensor_scalar(shid[:], iotan[0:64, :], 0.0, None, A.is_equal))
        with nc.Block() as blk:
            @blk.gpsimd
            def _(g):
                s_ct.inc(g.iota(iotan[:], pattern=[[1, 128]], base=0, channel_multiplier=0,
                                allow_small_or_imprecise_dtypes=True))

        with ExitStack() as hs:
            E = hs.enter_context(nc.sbuf_tensor(tag + "_E", [128, NC, T], BF16))
            qfh = hs.enter_context(nc.sbuf_tensor(tag + "_qfh", [64, T], BF16))
            kfh = hs.enter_context(nc.sbuf_tensor(tag + "_kfh", [64, T], BF16))
            vh = hs.enter_context(nc.sbuf_tensor(tag + "_vh", [128, NC, 64], BF16))
            wqkv = hs.enter_context(nc.sbuf_tensor(tag + "_wqkv", [128, 3, 8, 64], BF16))
            pmax = hs.enter_context(nc.sbuf_tensor(tag + "_pmax", [128, 16], F32))
            regf = hs.enter_context(nc.sbuf_tensor(tag + "_regf", [128, 16], F32))
            regi = hs.enter_context(nc.sbuf_tensor(tag + "_regi", [128, 16], I32))
            bkt = hs.enter_context(nc.sbuf_tensor(tag + "_bkt", [128, 16], F32))
            Obuf = hs.enter_context(nc.sbuf_tensor(tag + "_Obuf", [128, NC, 128], BF16))
            T16 = hs.enter_context(nc.sbuf_tensor(tag + "_T16b", [16, 128], BF16))
            ccol = hs.enter_context(nc.sbuf_tensor(tag + "_ccol", [128, 1], F32))
            strow = hs.enter_context(nc.sbuf_tensor(tag + "_strow", [1, 128], F32))
            carry = hs.enter_context(nc.sbuf_tensor(tag + "_carry", [16, 128], F32))
            ranks = hs.enter_context(nc.sbuf_tensor(tag + "_ranks", [128, 16], F32))
            ranki = hs.enter_context(nc.sbuf_tensor(tag + "_ranki", [128, 16], I32))
            cvals = hs.enter_context(nc.sbuf_tensor(tag + "_cvals", [128, 3, 16], F32))
            rtmp = hs.enter_context(nc.sbuf_tensor(tag + "_rtmp", [128, 128], F32))
            Ucol = hs.enter_context(nc.sbuf_tensor(tag + "_Ucol", [128, NC, NT168], BF16))
            Scol = hs.enter_context(nc.sbuf_tensor(tag + "_Scol", [128, NC, NT168], BF16))
            rsb = hs.enter_context(nc.sbuf_tensor(tag + "_rsb", [128, 3, 16], F32))
            bkts = hs.enter_context(nc.sbuf_tensor(tag + "_bkts", [128, 3, 16], BF16))
            t16a = hs.enter_context(nc.sbuf_tensor(tag + "_t16a", [16, 3, 128], BF16))
            tsa = hs.enter_context(nc.sbuf_tensor(tag + "_tsa", [1, 1, 512], BF16))
            rstmp = hs.enter_context(nc.sbuf_tensor(tag + "_rstmp", [128, 1024], F32))
            T2sb = hs.enter_context(nc.sbuf_tensor(tag + "_T2sb", [128, T], BF16))
            T01sb = hs.enter_context(nc.sbuf_tensor(tag + "_T01sb", [40, T], BF16))
            TTb = hs.enter_context(nc.sbuf_tensor(tag + "_TTb", [128, NC, NT168], BF16))
            A2T = hs.enter_context(nc.sbuf_tensor(tag + "_A2T", [128, T], BF16))
            PS_A = hs.enter_context(nc.psum_tensor(tag + "_psA", [128, T], F32))
            PS_C = hs.enter_context(nc.psum_tensor(tag + "_psC", [64, T], F32))

            for h in range(16):
                with nc.named_scope(f"P_a0h{h}"):
                    emit_l0attn_head(k, tag, h, locals())
                if dbg is not None and h == 0:
                    sdb = k.sem(tag + "_dbg")
                    with nc.Block() as blkd:
                        @blkd.vector
                        def _(v):
                            v.tensor_copy(rstmp[:, 0:48], cvals[:].rearrange("p a b -> p (a b)"))
                            v.tensor_copy(rstmp[:, 64:65], ccol[:])
                            sdb.inc(v.tensor_copy(rstmp[:, 48:64], ranks[:]))
                        @blkd.sync
                        def _(sy):
                            sdb.wait(sy)
                            sdb.inc(sy.dma_start(dbg[:, 0:65], rstmp[:, 0:65]), 16)
                            sdb.inc(sy.dma_start(dbg[:, 64:112], rsb[:].rearrange("p a b -> p (a b)")), 16)
                            sdb.inc(sy.dma_start(dbg[:, 112:2160].bitcast(BF16).rearrange("p (a b) -> p a b", b=4096)[:, 0, :].rearrange("p (a b) -> p a b", b=2048)[:, 0, :] if False else dbg[:, 112:1136].bitcast(BF16), E[:, 0, :]), 16)
                            sdb.inc(sy.dma_start(dbg[0:64, 1136:3184].bitcast(BF16)[0:64, 0:2048], qfh[:]), 16)
                            sdb.inc(sy.dma_start(dbg[:, 3184:3216].bitcast(BF16)[:, 0:64], vh[:, 0, :]), 16)
                            sdb.inc(sy.dma_start(dbg[:, 3216:3232], bkt[:]), 16)
                            sdb.inc(sy.dma_start(dbg[:, 3232:3248], regf[:]), 16)
                            sdb.inc(sy.dma_start(dbg[0:16, 3248:3376], carry[:]), 16)
                            sdb.inc(sy.dma_start(dbg[0:1, 3376:3504], strow[:]), 16)
                            sdb.inc(sy.dma_start(dbg[0:16, 3508:3572].bitcast(BF16), T16[:]), 16)
                            sdb.inc(sy.dma_start(dbg[:, 3572:3596].bitcast(BF16), bkts[:].rearrange("p a b -> p (a b)")), 16)
                            sdb.inc(sy.dma_start(dbg[0:16, 3620:3812].bitcast(BF16), t16a[:].rearrange("p a b -> p (a b)")), 16)
                            sdb.inc(sy.dma_start(dbg[0:1, 3824:4080].bitcast(BF16)[0:1, 0:512], tsa[0:1, 0, 0:512]), 16)
                            sdb.wait(sy)

        with ExitStack() as ts:
            xb = ts.enter_context(nc.sbuf_tensor(tag + "_xb", [128, NC, D], BF16))
            wob = ts.enter_context(nc.sbuf_tensor(tag + "_wob", [128, 2, 8, 512], BF16))
            bor = ts.enter_context(nc.sbuf_tensor(tag + "_bors", [1, D], F32))
            vch = ts.enter_context(nc.sbuf_tensor(tag + "_vch", [128, 2, 1024], F32))
            xrsl = ts.enter_context(nc.sbuf_tensor(tag + "_xrsl", [128, 2, 1024], F32))
            junkv = ts.enter_context(nc.sbuf_tensor(tag + "_junkv", [128, 1024], F32))
            junkc = ts.enter_context(nc.sbuf_tensor(tag + "_junkc", [128, 1], F32))
            lst = ts.enter_context(nc.sbuf_tensor(tag + "_lst", [128, 2, 8], F32))
            ps_w = ts.enter_context(nc.psum_tensor(tag + "_psw", [128, T], F32))
            s_wl = k.sem(tag + "_wlo")
            s_mm = k.sem(tag + "_mmo")
            s_v = k.sem(tag + "_vo")
            with nc.Block() as blk:
                @blk.gpsimd
                def _(g):
                    wvw = k.inp["a0_wo"].rearrange("(ic p) o -> p ic o", p=128)
                    for half in range(2):
                        s_wl.inc(g.dma_start(wob[:, half, :, :],
                                             wvw[:, :, half * 512:(half + 1) * 512]), 16)
                @blk.sync
                def _(sy):
                    s_bor = k.sem(tag + "_bor")
                    s_bor.inc(sy.dma_start(bor[:], k.inp["a0_bor"]), 16)
                @blk.tensor
                def _(te):
                    te.wait_ge(s_wl.h, 32)
                    te.wait_ge(k.sem(tag + "_bor").h, 16)
                    for c in range(NC):
                        if c >= 2:
                            te.wait_ge(s_v.h, 2 * (c - 1))
                        for et in range(2):
                            bank = ps_w[:, (2 * (c % 2) + et) * 512:(2 * (c % 2) + et + 1) * 512]
                            for fc in range(FC):
                                te.matmul(bank, outT[:, fc, c * 128:(c + 1) * 128],
                                          wob[:, et, fc, :],
                                          start=(fc == 0), stop=False,
                                          skip_group_check=True)
                            mm = te.matmul(bank, k.ones_rowf[:, 0:128],
                                           bor[:, et * 512:(et + 1) * 512],
                                           start=False, stop=True, skip_group_check=True)
                            s_mm.inc(mm)
                emit_tail_bodies(k, tag + "_t", blk, vch=vch, xrsl=xrsl, junkv=junkv,
                                 junkc=junkc, lst=lst, xb=xb, xrv_in=xrv_in,
                                 xrv_out=xrv_out,
                                 ps_slice=lambda c, et: ps_w[:, (2 * (c % 2) + et) * 512:(2 * (c % 2) + et + 1) * 512],
                                 s_mm=s_mm, b_mm=0, mm_target=lambda c, et: 2 * c + et + 1,
                                 s_v=s_v)
            emit_xf(k, xb, xf, tag + "_xf")


def emit_l0attn_head(k, tag, h, B):
    nc = k.nc
    xf = B["xf"]
    E, qfh, kfh, vh = B["E"], B["qfh"], B["kfh"], B["vh"]
    stg = kfh
    wqkv, probesb = B["wqkv"], B["probesb"]
    pmax, regf, regi, bkt = B["pmax"], B["regf"], B["regi"], B["bkt"]
    Obuf, T16, ccol, strow, carry = (B["Obuf"], B["T16"],
                                     B["ccol"], B["strow"], B["carry"])
    ohb = Obuf[:, :, 64:128]
    tstag = B["rstmp"][0:1, 0:1024].bitcast(BF16)
    ranks, ranki, cvals, rtmp = B["ranks"], B["ranki"], B["cvals"], B["rtmp"]
    Ucol, Scol, rsb, rstmp = B["Ucol"], B["Scol"], B["rsb"], B["rstmp"]
    bkts = B["bkts"]
    t16a, tsa = B["t16a"], B["tsa"]
    Acol = Ucol
    T2sb, T01sb, TTb = B["T2sb"], B["T01sb"], B["TTb"]
    A2T = B["A2T"]
    A01T = B["qfh"][0:40, :]
    S2T, S01T = B["T2sb"], B["T01sb"]
    PS_A, PS_C = B["PS_A"], B["PS_C"]
    L128, L16, SELf, Unb, iotan, iota64r, tb = (B["L128"], B["L16"], B["SELf"],
                                                B["Unb"], B["iotan"], B["iota64r"], B["tb"])
    shid, bqc, bkc, bvr, outT, scr = (B["shid"], B["bqc"], B["bkc"], B["bvr"],
                                      B["outT"], B["scr"])
    hb = (h % 2) * 64

    s_wl = k.sem(tag + "_hwl"); b_wl = s_wl.n
    s_mmq = k.sem(tag + "_hmmq"); b_mmq = s_mmq.n
    s_evq = k.sem(tag + "_hevq"); b_evq = s_evq.n
    s_mmv = k.sem(tag + "_hmmv"); b_mmv = s_mmv.n
    s_evv = k.sem(tag + "_hevv"); b_evv = s_evv.n
    s_pp = k.sem(tag + "_hpp"); b_pp = s_pp.n
    s_O = k.sem(tag + "_hO"); b_O = s_O.n
    s_cnt = k.sem(tag + "_hcnt"); b_cnt = s_cnt.n
    s_ts = k.sem(tag + "_hts"); b_ts = s_ts.n
    s_t16 = k.sem(tag + "_ht16"); b_t16 = s_t16.n
    s_pcm = k.sem(tag + "_hpcm"); b_pcm = s_pcm.n
    s_pcv = k.sem(tag + "_hpcv"); b_pcv = s_pcv.n
    s_rp = k.sem(tag + "_hrp"); b_rp = s_rp.n
    s_rk = k.sem(tag + "_hrk"); b_rk = s_rk.n
    s_cv = k.sem(tag + "_hcv"); b_cv = s_cv.n
    s_ust = k.sem(tag + "_hust"); b_ust = s_ust.n
    s_sc = k.sem(tag + "_hsc"); b_sc = s_sc.n
    s_exp = k.sem(tag + "_hexp"); b_exp = s_exp.n
    s_t2 = k.sem(tag + "_ht2"); b_t2 = s_t2.n
    s_tev = k.sem(tag + "_htev"); b_tev = s_tev.n
    s_ttm = k.sem(tag + "_httm"); b_ttm = s_ttm.n
    s_ttv = k.sem(tag + "_httv"); b_ttv = s_ttv.n
    s_rs = k.sem(tag + "_hrs"); b_rs = s_rs.n
    s_f = k.sem(tag + "_hf"); b_f = s_f.n
    s_pm = k.sem(tag + "_hpm"); b_pm = s_pm.n
    s_av = k.sem(tag + "_hav"); b_av = s_av.n
    s_oe = k.sem(tag + "_hoe"); b_oe = s_oe.n

    with nc.Block() as blk:
        @blk.gpsimd
        def _(g):
            for wi, nm in ((0, "a0_wq"), (1, "a0_wk"), (2, "a0_wv")):
                wv = k.inp[nm].rearrange("(ic p) o -> p ic o", p=128)[:, :, h * 64:(h + 1) * 64]
                s_wl.inc(g.dma_start(wqkv[:, wi, :, :], wv), 16)
        @blk.sync
        def _(sy):
            for l in range(3):
                nb = NBL[l]
                sy.wait_ge(s_ts.h, b_ts + l + 1)
                s_t16.inc(sy.dma_start(scr[0:1, 0:16 * nb], tstag[0:1, 0:16 * nb]), 16)
                sy.wait_ge(s_t16.h, b_t16 + 32 * l + 16)
                s_t16.inc(sy.dma_start(
                    T16[0:16, 0:nb],
                    scr[0:1, 0:16 * nb].rearrange("a (c v) -> (a c) v", v=nb)), 16)

        @blk.tensor
        def _(te):
            _sid = [nc.enter_named_scope("S_qkv", False)[0]]
            for wi in range(2):
                te.wait_ge(s_wl.h, b_wl + 16 * (wi + 1))
                if wi == 1:
                    te.wait_ge(s_evq.h, b_evq + 4)
                for nt in range(4):
                    for ic in range(FC):
                        mm = te.matmul(PS_C[:, nt * 512:(nt + 1) * 512],
                                       wqkv[:, wi, ic, :],
                                       xf[:, ic, nt * 512:(nt + 1) * 512],
                                       start=(ic == 0), stop=(ic == FC - 1),
                                       skip_group_check=True)
                        if ic == FC - 1:
                            s_mmq.inc(mm)
            te.wait_ge(s_wl.h, b_wl + 48)
            for tc in range(NC):
                for ic in range(FC):
                    te.matmul(PS_A[:, tc * 64:(tc + 1) * 64],
                              xf[:, ic, tc * 128:(tc + 1) * 128],
                              wqkv[:, 2, ic, :],
                              start=(ic == 0), stop=False, skip_group_check=True)
                mm = te.matmul(PS_A[:, tc * 64:(tc + 1) * 64], k.ones_rowf[:, 0:128],
                               bvr[:, h * 64:(h + 1) * 64],
                               start=False, stop=True, skip_group_check=True)
                s_mmv.inc(mm)
            nc.leave_named_scope("S_qkv", _sid[0], False)
            _sid[0] = nc.enter_named_scope("S_rank", False)[0]
            te.wait_ge(k.sem(tag + "_ctg").h, 16)
            for l in range(3):
                rl = REFL[l]
                nb = NBL[l]
                te.wait_ge(s_evq.h, b_evq + 8)
                te.wait_ge(s_evv.h, b_evv + 1)
                if l > 0:
                    te.wait_ge(s_rk.h, b_rk + 16 * l)
                for c in range(NC):
                    mm = te.matmul(PS_A[:, c * 64:(c + 1) * 64],
                                   qfh[:, c * 128:(c + 1) * 128],
                                   probesb[0:64, (rl * 16 + h) * 64:(rl * 16 + h + 1) * 64],
                                   start=True, stop=True, skip_group_check=True)
                    if c == NC - 1:
                        s_pp.inc(mm)
                te.wait_ge(s_O.h, b_O + l + 1)
                Ofl = Obuf[:].rearrange("p c e -> p (c e)")
                for c in range(NC):
                    mm = te.matmul(PS_A[0:1, c * nb:(c + 1) * nb], k.ones_col[:, :],
                                   Ofl[:, c * nb:(c + 1) * nb],
                                   start=True, stop=True, skip_group_check=True)
                    if c == NC - 1:
                        s_cnt.inc(mm)
                te.wait_ge(s_t16.h, b_t16 + 32 * (l + 1))
                s_pcm.inc(te.matmul(PS_A[0:nb, 1200:1201], T16[0:16, 0:nb],
                                    k.ones_col[0:16, :], start=True, stop=True,
                                    skip_group_check=True))
                s_pcm.inc(te.matmul(PS_A[0:16, 1600:1600 + nb], L16[:],
                                    T16[0:16, 0:nb], start=True, stop=False,
                                    skip_group_check=True))
                te.wait_ge(s_pcv.h, b_pcv + 3 * l + 1)
                s_pcm.inc(te.matmul(PS_A[0:1, 1400:1400 + nb], ccol[0:nb, :],
                                    Unb[l][:], start=True, stop=True,
                                    skip_group_check=True))
                te.wait_ge(s_pcv.h, b_pcv + 3 * l + 2)
                s_pcm.inc(te.matmul(PS_A[0:16, 1600:1600 + nb],
                                    k.ones_rowf[0:1, 0:16],
                                    strow[0:1, 0:nb],
                                    start=False, stop=True, skip_group_check=True))
                te.wait_ge(s_pcv.h, b_pcv + 3 * l + 3)
                for c in range(NC):
                    if c > 0:
                        te.wait_ge(s_rk.h, b_rk + 16 * l + c)
                    te.matmul(PS_A[:, 1024:1024 + nb], L128[:], Ofl[:, c * nb:(c + 1) * nb],
                              start=True, stop=False, skip_group_check=True)
                    mm = te.matmul(PS_A[:, 1024:1024 + nb],
                                   SELf[:, c * 128:(c + 1) * 128],
                                   carry[0:16, 0:nb], start=False, stop=True,
                                   skip_group_check=True)
                    s_rp.inc(mm)
            nc.leave_named_scope("S_rank", _sid[0], False)
            _sid[0] = nc.enter_named_scope("S_scores", False)[0]
            te.wait_ge(s_cv.h, b_cv + 3)
            for jc in range(NC):
                if jc > 0:
                    te.wait_ge(s_exp.h, b_exp + jc)
                for nt in range(4):
                    mm = te.matmul(PS_A[:, nt * 512:(nt + 1) * 512],
                                   kfh[:, jc * 128:(jc + 1) * 128],
                                   qfh[:, nt * 512:(nt + 1) * 512],
                                   start=True, stop=True, skip_group_check=True)
                    if nt == 3:
                        s_sc.inc(mm)
            nc.leave_named_scope("S_scores", _sid[0], False)
            _sid[0] = nc.enter_named_scope("S_T", False)[0]
            te.wait_ge(s_exp.h, b_exp + NC)
            te.wait_ge(s_ust.h, b_ust + 3)
            for nt in range(4):
                for c in range(NC):
                    mm = te.matmul(PS_A[:, nt * 512:(nt + 1) * 512],
                                   Scol[:, c, 0:128], E[:, c, nt * 512:(nt + 1) * 512],
                                   start=(c == 0), stop=(c == NC - 1),
                                   skip_group_check=True)
                    if c == NC - 1:
                        s_t2.inc(mm)
                for c in range(NC):
                    mm = te.matmul(PS_C[0:40, nt * 512:(nt + 1) * 512],
                                   Scol[:, c, 128:168], E[:, c, nt * 512:(nt + 1) * 512],
                                   start=(c == 0), stop=(c == NC - 1),
                                   skip_group_check=True)
                    if c == NC - 1:
                        s_t2.inc(mm)
            te.wait_ge(s_tev.h, b_tev + 8)
            for grp in range(4):
                if grp >= 2:
                    te.wait_ge(s_ttv.h, b_ttv + 2 * (grp - 1))
                for j in range(4):
                    c = grp * 4 + j
                    mm = te.matmul(PS_A[:, (grp % 2) * 512:(grp % 2) * 512 + 512][:, j * 128:(j + 1) * 128],
                                   T2sb[:, c * 128:(c + 1) * 128], k.ident[:],
                                   start=True, stop=True, skip_group_check=True)
                    if j == 3:
                        s_ttm.inc(mm)
                for j in range(4):
                    c = grp * 4 + j
                    mm = te.matmul(PS_A[:, 1024 + (grp % 2) * 512:1024 + (grp % 2) * 512 + 512][:, j * 128:(j + 1) * 128],
                                   T01sb[0:40, c * 128:(c + 1) * 128], k.ident[0:40, :],
                                   start=True, stop=True, skip_group_check=True)
                    if j == 3:
                        s_ttm.inc(mm)
            te.wait_ge(s_rs.h, b_rs + 4)
            te.wait_ge(s_ttv.h, b_ttv + 8)
            for wave in range(2):
                src = Scol if wave == 0 else Acol
                for grp in range(4):
                    gi = wave * 4 + grp
                    if gi >= 2:
                        te.wait_ge(s_ttv.h, b_ttv + 8 + 2 * (gi - 1))
                    for j in range(4):
                        c = grp * 4 + j
                        mm = te.matmul(PS_A[:, (gi % 2) * 512 + j * 128:(gi % 2) * 512 + (j + 1) * 128],
                                       src[:, c, 0:128], k.ident[:],
                                       start=True, stop=True, skip_group_check=True)
                        if j == 3:
                            s_ttm.inc(mm)
                    for j in range(4):
                        c = grp * 4 + j
                        mm = te.matmul(PS_A[0:40, 1024 + (gi % 2) * 512 + j * 128:1024 + (gi % 2) * 512 + (j + 1) * 128],
                                       src[:, c, 128:168], k.ident[:],
                                       start=True, stop=True, skip_group_check=True)
                        if j == 3:
                            s_ttm.inc(mm)
            nc.leave_named_scope("S_T", _sid[0], False)
            _sid[0] = nc.enter_named_scope("S_FAV", False)[0]
            te.wait_ge(s_ttv.h, b_ttv + 24)
            for jc in range(NC):
                if jc > 0:
                    te.wait_ge(s_pm.h, b_pm + jc)
                for nt in range(4):
                    te.matmul(PS_A[:, nt * 512:(nt + 1) * 512],
                              S2T[:, jc * 128:(jc + 1) * 128],
                              A2T[:, nt * 512:(nt + 1) * 512],
                              start=True, stop=False, skip_group_check=True)
                    mm = te.matmul(PS_A[:, nt * 512:(nt + 1) * 512],
                                   S01T[0:40, jc * 128:(jc + 1) * 128],
                                   A01T[0:40, nt * 512:(nt + 1) * 512],
                                   start=False, stop=True, skip_group_check=True)
                    if nt == 3:
                        s_f.inc(mm)
                te.wait_ge(s_pm.h, b_pm + jc + 1)
                for nt in range(4):
                    mm = te.matmul(PS_C[:, nt * 512:(nt + 1) * 512],
                                   vh[:, jc, :], E[:, jc, nt * 512:(nt + 1) * 512],
                                   start=(jc == 0), stop=(jc == NC - 1),
                                   skip_group_check=True)
                    if nt == 3:
                        s_av.inc(mm)
            nc.leave_named_scope("S_FAV", _sid[0], False)
            _sid[0] = nc.enter_named_scope("S_epi", False)[0]
            if hb == 64:
                te.wait_ge(s_oe.h, b_oe + 1)
                for nt in range(4):
                    mm = te.matmul(PS_A[:, nt * 512:(nt + 1) * 512], shid[:],
                                   stg[:, nt * 512:(nt + 1) * 512],
                                   start=True, stop=True, skip_group_check=True)
                    if nt == 3:
                        s_oe.inc(mm)
            nc.leave_named_scope("S_epi", _sid[0], False)

        @blk.scalar
        def _(sc):
            for wi in range(2):
                dst = qfh if wi == 0 else kfh
                bcol = bqc if wi == 0 else bkc
                for nt in range(4):
                    sc.wait_ge(s_mmq.h, b_mmq + 4 * wi + nt + 1)
                    s_evq.inc(sc.activation(dst[:, nt * 512:(nt + 1) * 512],
                                            PS_C[:, nt * 512:(nt + 1) * 512],
                                            AF.Identity,
                                            bias=bcol[hb:hb + 64, h // 2:h // 2 + 1]))
            for jc in range(NC):
                sc.wait_ge(s_sc.h, b_sc + jc + 1)
                s_exp.inc(sc.activation(E[:, jc, :], PS_A[:, :], AF.Exp, scale=0.125))

        ch = k.chainer(tag + "_hch")
        @blk.vector
        def _(v):
            v.wait_ge(s_mmv.h, b_mmv + NC)
            s_evv.inc(v.tensor_copy(vh[:].rearrange("p c e -> p (c e)"), PS_A[:, 0:1024]))
            for l in range(3):
                nb = NBL[l]
                toff = TOFF[l]
                if l >= 1:
                    v.wait_ge(s_t16.h, b_t16 + 32 * (l - 1) + 16)
                v.wait_ge(s_pp.h, b_pp + l + 1)
                v.tensor_copy(rstmp[:, 0:1024], PS_A[:, 0:1024])
                pp3 = rstmp[:, 0:1024].rearrange("p (c e) -> p c e", e=64)
                ch(v, v.tensor_reduce(pmax[:].unsqueeze(2), pp3, AX.X, A.max))
                v.tensor_tensor(ohb, pp3, pmax[:].unsqueeze(2).broadcast_to([128, 16, 64]),
                                A.is_equal)
                v.tensor_tensor(rstmp[:, 0:1024].rearrange("p (c e) -> p c e", e=64),
                                ohb,
                                iota64r[:].rearrange("p (o e) -> p o e", o=1)
                                .broadcast_to([128, 16, 64]), A.mult)
                ch(v, v.tensor_reduce(regf[:].unsqueeze(2),
                                      rstmp[:, 0:1024].rearrange("p (c e) -> p c e", e=64),
                                      AX.X, A.add))
                ch(v, v.tensor_copy(regi[:], regf[:]))
                ch(v, v.tensor_scalar(regi[:], regi[:], 9973, None, A.mult))
                ch(v, v.tensor_scalar(regi[:], regi[:], nb - 1, None, A.bitwise_and))
                ch(v, v.tensor_copy(bkt[:], regi[:]))
                v.tensor_copy(bkts[:, l, :], bkt[:])
                s_O.inc(v.tensor_tensor(
                    Obuf[:].rearrange("p c e -> p (c e)")[:, 0:16 * nb]
                    .rearrange("p (c e) -> p c e", e=nb),
                    iotan[:, 0:nb].rearrange("p (o e) -> p o e", o=1)
                    .broadcast_to([128, 16, nb]),
                    bkt[:].unsqueeze(2).broadcast_to([128, 16, nb]), A.is_equal))
                v.wait_ge(s_cnt.h, b_cnt + l + 1)
                if l == 2:
                    v.tensor_copy(tsa[0:1, 0, 0:512], PS_A[0:1, 0:512])
                s_ts.inc(v.tensor_copy(tstag[0:1, 0:16 * nb], PS_A[0:1, 0:16 * nb]))
                v.wait_ge(s_t16.h, b_t16 + 32 * (l + 1))
                v.tensor_copy(t16a[:, l, :], T16[:])
                v.wait_ge(s_pcm.h, b_pcm + 4 * l + 1)
                s_pcv.inc(v.tensor_copy(ccol[0:nb, :], PS_A[0:nb, 1200:1201]))
                v.wait_ge(s_pcm.h, b_pcm + 4 * l + 3)
                s_pcv.inc(v.tensor_copy(strow[0:1, 0:nb], PS_A[0:1, 1400:1400 + nb]))
                v.wait_ge(s_pcm.h, b_pcm + 4 * l + 4)
                s_pcv.inc(v.tensor_copy(carry[0:16, 0:nb], PS_A[0:16, 1600:1600 + nb]))
                for c in range(NC):
                    v.wait_ge(s_rp.h, b_rp + 16 * l + c + 1)
                    ch(v, v.tensor_tensor(rtmp[:, 0:nb],
                                          Obuf[:].rearrange("p c e -> p (c e)")[:, c * nb:(c + 1) * nb],
                                          PS_A[:, 1024:1024 + nb], A.mult))
                    s_rk.inc(v.tensor_reduce(ranks[:, c:c + 1].unsqueeze(2),
                                             rtmp[:, 0:nb].unsqueeze(1), AX.X, A.add))
                v.wait_ge(s_rk.h, b_rk + 16 * (l + 1))
                ch(v, v.tensor_copy(ranki[:], ranks[:]))
                ch(v, v.tensor_scalar(ranki[:], ranki[:], SHL[l], None, A.arith_shift_right))
                s_cv.inc(v.tensor_copy(cvals[:, l, :], ranki[:]))
                v.wait_ge(s_cv.h, b_cv + l + 1)
                v.tensor_tensor(
                    Ucol[:, :, toff:toff + nb],
                    iotan[:, 0:nb].rearrange("p (o e) -> p o e", o=1)
                    .broadcast_to([128, 16, nb]),
                    cvals[:, l, :].unsqueeze(2).broadcast_to([128, 16, nb]), A.is_equal)
                v.tensor_copy(Scol[:, :, toff:toff + nb], Ucol[:, :, toff:toff + nb])
                s_ust.inc(v.tensor_tensor(Scol[:, :, toff + 1:toff + nb],
                                          Scol[:, :, toff + 1:toff + nb],
                                          Ucol[:, :, toff:toff + nb - 1], A.add))
            for nt in range(4):
                v.wait_ge(s_t2.h, b_t2 + 2 * nt + 1)
                s_tev.inc(v.tensor_copy(T2sb[:, nt * 512:(nt + 1) * 512],
                                        PS_A[:, nt * 512:(nt + 1) * 512]))
                v.wait_ge(s_t2.h, b_t2 + 2 * nt + 2)
                s_tev.inc(v.tensor_copy(T01sb[0:40, nt * 512:(nt + 1) * 512],
                                        PS_C[0:40, nt * 512:(nt + 1) * 512]))
            for grp in range(4):
                v.wait_ge(s_ttm.h, b_ttm + 2 * grp + 1)
                s_ttv.inc(v.tensor_copy(
                    TTb[:, grp * 4:(grp + 1) * 4, 0:128],
                    PS_A[:, (grp % 2) * 512:(grp % 2) * 512 + 512]
                    .rearrange("p (j e) -> p j e", e=128)))
                v.wait_ge(s_ttm.h, b_ttm + 2 * grp + 2)
                s_ttv.inc(v.tensor_copy(
                    TTb[:, grp * 4:(grp + 1) * 4, 128:168],
                    PS_A[:, 1024 + (grp % 2) * 512:1024 + (grp % 2) * 512 + 512]
                    .rearrange("p (j e) -> p j e", e=128)[:, :, 0:40]))
            for l in range(3):
                nb = NBL[l]
                toff = TOFF[l]
                nh = 8 if nb == 128 else 16
                for half in range(2 if nb * 16 > 1024 else 1):
                    cs0 = half * 8
                    ncs = 8 if nb * 16 > 1024 else 16
                    dst = rsb[:, l, cs0:cs0 + ncs] if nb * 16 > 1024 else rsb[:, l, :]
                    v.tensor_tensor(rstmp[:, 0:ncs * nb].rearrange("p (c e) -> p c e", e=nb),
                                    Ucol[:, cs0:cs0 + ncs, toff:toff + nb],
                                    TTb[:, cs0:cs0 + ncs, toff:toff + nb], A.mult)
                    ch(v, v.tensor_reduce(dst.unsqueeze(2),
                                          rstmp[:, 0:ncs * nb].rearrange("p (c e) -> p c e", e=nb),
                                          AX.X, A.add))
                    v.tensor_tensor(rstmp[:, 0:ncs * (nb - 1)]
                                    .rearrange("p (c e) -> p c e", e=nb - 1),
                                    Ucol[:, cs0:cs0 + ncs, toff + 1:toff + nb],
                                    TTb[:, cs0:cs0 + ncs, toff:toff + nb - 1], A.mult)
                    ch(v, v.tensor_reduce(regf[:, cs0:cs0 + ncs].unsqueeze(2),
                                          rstmp[:, 0:ncs * (nb - 1)]
                                          .rearrange("p (c e) -> p c e", e=nb - 1),
                                          AX.X, A.add))
                    ch(v, v.tensor_tensor(dst, dst, regf[:, cs0:cs0 + ncs], A.add))
                    if nb * 16 <= 1024:
                        break
                ch(v, v.tensor_scalar(rsb[:, l, :], rsb[:, l, :], 3.0, None, A.mult))
                ch(v, v.reciprocal(rsb[:, l, :], rsb[:, l, :]))
                s_rs.inc(v.tensor_tensor(Acol[:, :, toff:toff + nb],
                                         Ucol[:, :, toff:toff + nb],
                                         rsb[:, l, :].unsqueeze(2)
                                         .broadcast_to([128, 16, nb]), A.mult))
            s_rs.inc(v.tensor_copy(rtmp[:, 0:1], rsb[:, 0, 0:1]))
            for wave in range(2):
                d2 = S2T if wave == 0 else A2T
                d01 = S01T if wave == 0 else A01T
                for grp in range(4):
                    gi = wave * 4 + grp
                    v.wait_ge(s_ttm.h, b_ttm + 8 + 2 * gi + 1)
                    s_ttv.inc(v.tensor_copy(
                        d2[:, grp * 512:(grp + 1) * 512],
                        PS_A[:, (gi % 2) * 512:(gi % 2) * 512 + 512]))
                    v.wait_ge(s_ttm.h, b_ttm + 8 + 2 * gi + 2)
                    s_ttv.inc(v.tensor_copy(
                        d01[0:40, grp * 512:(grp + 1) * 512],
                        PS_A[0:40, 1024 + (gi % 2) * 512:1024 + (gi % 2) * 512 + 512]))
            for jc in range(NC):
                v.wait_ge(s_f.h, b_f + jc + 1)
                s_pm.inc(v.tensor_tensor(E[:, jc, :], E[:, jc, :], PS_A[:, :], A.mult))
            v.wait_ge(s_av.h, b_av + NC)
            s_oe.inc(v.tensor_copy(stg[:], PS_C[0:64, :]))
            if hb == 0:
                s_oe.inc(v.tensor_copy(outT[0:64, h // 2, :], stg[:]))
            else:
                v.wait_ge(s_oe.h, b_oe + 2)
                s_oe.inc(v.tensor_copy(outT[64:128, h // 2, :], PS_A[64:128, :]))


def _emit_l0a2_pair(k, tag, g, B):
    nc = k.nc
    xf = B["xf"]
    E, qfh2, kfh2, vh2 = B["E"], B["qfh2"], B["kfh2"], B["vh2"]
    wqkv, probeb = B["wqkv"], B["probeb"]
    Obuf, ohb, pmax, regf, regi, bkt = (B["Obuf"], B["ohb"], B["pmax"], B["regf"],
                                        B["regi"], B["bkt"])
    ranks, ranki, cvals, cvb, ctr = B["ranks"], B["ranki"], B["cvals"], B["cvb"], B["ctr"]
    tstag, T163, ccol3, strow3, carr3 = (B["tstag"], B["T163"], B["ccol3"],
                                         B["strow3"], B["carr3"])
    rtmp, Scol = B["rtmp"], B["Scol"]
    crep0, crep12, U0T, S2T, U01T, S01T = (B["crep0"], B["crep12"], B["U0T"],
                                           B["S2T"], B["U01T"], B["S01T"])
    TUT, brow, Utmp = B["TUT"], B["brow"], B["Utmp"]
    b0rep, b12rep = B["b0rep"], B["b12rep"]
    PS_A, PS_C = B["PS_A"], B["PS_C"]
    L128, L16, SELf, Unb, iotan, iota64r, shid = (B["L128"], B["L16"], B["SELf"],
                                                  B["Unb"], B["iotan"], B["iota64r"],
                                                  B["shid"])
    iotac, iotacm1, iotacm32, iotacm33, e1col = (B["iotac"], B["iotacm1"],
                                                 B["iotacm32"], B["iotacm33"],
                                                 B["e1col"])
    bqc, bkc, bvr, outs = B["bqc"], B["bkc"], B["bvr"], B["outs"]
    scr3, cview, cscr, outTd = B["scr3"], B["cview"], B["cscr"], B["outTd"]
    sA, sV, sC, sY = B["sA"], B["sV"], B["sC"], B["sY"]
    s_wl, s_pb, ch = B["s_wl"], B["s_pb"], B["ch"]
    prev, hist_pa, hist_v = B["prev"], B["hist_pa"], B["hist_v"]
    xfev, xf_tgt = B["xfev"], B["xf_tgt"]
    s_ct = k.sem(tag + "_ct2")
    ct_tgt = s_ct.n
    gi = g % 2
    tel, vl, scl, syl, gl = [], [], [], [], []

    def rA():
        sA.n += 1
        return sA.n

    def rV():
        sV.n += 1
        return sV.n

    def rC():
        sC.n += 1
        return sC.n

    def rY():
        sY.n += DMAW[0]
        return sY.n

    wl_base, pb_base = s_wl.n, s_pb.n
    pa_guard = hist_pa.get(g - 1, 0)
    pv_guard = hist_v.get(g - 2, 0)
    s_wl.n += 6 * DMAW[0]
    s_pb.n += 6 * DMAW[0]

    def _gload(gg, pa_guard=pa_guard, pv_guard=pv_guard, wl_base=wl_base,
               pb_base=pb_base):
        if pa_guard:
            gg.wait_ge(sA.h, pa_guard)
        if pv_guard:
            gg.wait_ge(sA.h, pv_guard)
        for wi, nm in ((0, "a0_wq"), (1, "a0_wk"), (2, "a0_wv")):
            wv = k.inp[nm].rearrange("(ic p) o -> p ic o", p=128)[:, :, g * 128:(g + 1) * 128]
            for hh in range(2):
                gg.dma_start(wqkv[gi][:, wi, 4 * hh:4 * hh + 4, :],
                             wv[:, 4 * hh:4 * hh + 4, :]).then_inc(s_wl.h, 16)
        for l in range(3):
            src = k.inp["a0_probes"][:, (REFL[l] * 16 + 2 * g) * 64:(REFL[l] * 16 + 2 * g + 2) * 64]
            gg.dma_start(probeb[gi][0:64, l, :], src).then_inc(s_pb.h, 16)
            gg.dma_start(probeb[gi][64:128, l, :], src).then_inc(s_pb.h, 16)
    gl.append(_gload)

    prev_pm, prev_sh = prev["pm"], prev["sh"]
    a_qk = [[0] * 4 for _ in range(2)]
    c_ev = [[0] * 4 for _ in range(2)]
    for wi in range(2):
        for nt in range(4):
            a_qk[wi][nt] = rA()

            def _qk(te, wi=wi, nt=nt, a=a_qk[wi][nt]):
                if wi == 0 and nt == 0:
                    te.wait_ge(xfev.h, xf_tgt)
                    te.wait_ge(s_ct.h, ct_tgt)
                    if prev_pm:
                        te.wait_ge(sV.h, prev_pm)
                    if prev_sh:
                        te.wait_ge(sV.h, prev_sh)
                te.wait_ge(s_wl.h, wl_base + 2 * DMAW[0] * (wi + 1))
                if wi == 1:
                    te.wait_ge(sC.h, c_ev[0][nt])
                for ic in range(FC):
                    mm = te.matmul(PS_A[:, nt * 512:(nt + 1) * 512],
                                   wqkv[gi][:, wi, ic, 0:128],
                                   xf[:, ic, nt * 512:(nt + 1) * 512],
                                   start=(ic == 0), stop=(ic == FC - 1),
                                   skip_group_check=True)
                mm.then_inc(sA.h, 1)
            tel.append(_qk)
            c_ev[wi][nt] = rC()

            def _ev(sc, wi=wi, nt=nt, a=a_qk[wi][nt]):
                sc.wait_ge(sA.h, a)
                dst = qfh2 if wi == 0 else kfh2
                bc = bqc if wi == 0 else bkc
                sc.activation(dst[:, nt * 512:(nt + 1) * 512],
                              PS_A[:, nt * 512:(nt + 1) * 512], AF.Identity,
                              bias=bc[:, g:g + 1]).then_inc(sC.h, 1)
            scl.append(_ev)

    st = [{"a_pr": [0] * 3, "v_pfree": [0] * 3, "v_O": [0] * 3, "y_t16": [0] * 3,
           "v_car": [0] * 3, "v_rkm": [0] * 3, "v_scol": [0] * 3, "v_ctr": 0,
           "v_us": 0, "a_pr_last": 0} for _ in range(2)]

    def rk_probe(hi, l):
        s = st[hi]
        preg = PS_C[:, 1024:2048] if hi == 0 else PS_C[:, 0:1024]
        a = rA()
        s["a_pr"][l] = a
        s["a_pr_last"] = a
        pav = prev["avep"] if (hi == 0 and l == 0) else 0
        wait_ctr = st[0]["v_ctr"] if (hi == 1 and l == 0) else 0
        wait_free = s["v_pfree"][l - 1] if (hi == 0 and l > 0) else 0
        wait_rk = s["v_scol"][l - 1] if (hi == 1 and l > 0) else 0

        def _pr(te, hi=hi, l=l, a=a, pav=pav, wait_ctr=wait_ctr,
                wait_free=wait_free, wait_rk=wait_rk, preg=preg):
            te.wait_ge(s_pb.h, pb_base + 2 * DMAW[0] * (l + 1))
            te.wait_ge(sC.h, c_ev[0][3])
            for w in (pav, wait_ctr, wait_free, wait_rk):
                if w:
                    te.wait_ge(sV.h, w)
            for tc in range(16):
                mm = te.matmul(preg[:, tc * 64:(tc + 1) * 64],
                               qfh2[hi * 64:hi * 64 + 64, tc * 128:(tc + 1) * 128],
                               probeb[gi][hi * 64:hi * 64 + 64, l, hi * 64:hi * 64 + 64],
                               start=True, stop=True, skip_group_check=True)
            mm.then_inc(sA.h, 1)
        tel.append(_pr)
        nb = NBL[l]
        vpf = rV()
        s["v_pfree"][l] = vpf
        vO = rV()
        s["v_O"][l] = vO

        def _am(v, hi=hi, l=l, a=a, nb=nb, vpf=vpf, vO=vO, preg=preg):
            v.wait_ge(sA.h, a)
            pp3 = preg.rearrange("p (c e) -> p c e", e=64)
            ohb3 = ohb[:].rearrange("p (c e) -> p c e", e=64)
            ch(v, v.tensor_reduce(pmax[:].unsqueeze(2), pp3, AX.X, A.max))
            v.tensor_tensor(ohb3, pp3,
                            pmax[:].unsqueeze(2).broadcast_to([128, 16, 64]),
                            A.is_equal).then_inc(sV.h, 1)
            v.tensor_tensor(rtmp[:].rearrange("p (c e) -> p c e", e=64), ohb3,
                            iota64r[:].rearrange("p (o e) -> p o e", o=1)
                            .broadcast_to([128, 16, 64]), A.mult)
            ch(v, v.tensor_reduce(regf[:].unsqueeze(2),
                                  rtmp[:].rearrange("p (c e) -> p c e", e=64),
                                  AX.X, A.add))
            ch(v, v.tensor_copy(regi[:], regf[:]))
            ch(v, v.tensor_scalar(regi[:], regi[:], 9973, None, A.mult))
            ch(v, v.tensor_scalar(regi[:], regi[:], nb - 1, None, A.bitwise_and))
            ch(v, v.tensor_copy(bkt[:], regi[:]))
            v.tensor_tensor(
                Obuf[:, OB_OFF[l]:OB_OFF[l] + 16 * nb].rearrange("p (c e) -> p c e", e=nb),
                iotan[:, 0:nb].rearrange("p (o e) -> p o e", o=1)
                .broadcast_to([128, 16, nb]),
                bkt[:].unsqueeze(2).broadcast_to([128, 16, nb]),
                A.is_equal).then_inc(sV.h, 1)
        vl.append(_am)

    def rk_counts(hi, l):
        s = st[hi]
        nb = NBL[l]
        halves = 2 if l == 0 else 1
        nk = 8 if l == 0 else 16
        wprev = prev["ctr"] if (hi == 0 and l == 0) else 0
        wpf = s["v_pfree"][l] if hi == 1 else 0
        wmach = s["v_rkm"][l - 1] if l > 0 else 0
        v_ts = [0] * halves
        for half in range(halves):
            a = rA()

            def _cm(te, l=l, half=half, nk=nk, nb=nb, a=a, vO=s["v_O"][l],
                    wprev=wprev, wpf=wpf, wmach=wmach, vts_prev=v_ts[half - 1] if half else 0):
                te.wait_ge(sV.h, vO)
                for w in (wprev, wpf, wmach, vts_prev):
                    if w:
                        te.wait_ge(sV.h, w)
                for kc in range(half * nk, (half + 1) * nk if l == 0 else 16):
                    mm = te.matmul(PS_C[0:1, (kc - half * nk) * nb:(kc - half * nk + 1) * nb],
                                   k.ones_col[:, :],
                                   Obuf[:, OB_OFF[l] + kc * nb:OB_OFF[l] + (kc + 1) * nb],
                                   start=True, stop=True, skip_group_check=True)
                mm.then_inc(sA.h, 1)
            tel.append(_cm)
            v_ts[half] = rV()

            def _ev(v, l=l, half=half, nk=nk, nb=nb, a=a):
                v.wait_ge(sA.h, a)
                v.tensor_copy(tstag[0:1, OB_OFF[l] + half * 1024:OB_OFF[l] + half * 1024 + nk * nb],
                              PS_C[0:1, 0:nk * nb]).then_inc(sV.h, 1)
            vl.append(_ev)
            yv = rY()

            def _bo(sy, l=l, half=half, nk=nk, nb=nb, vts=v_ts[half]):
                sy.wait_ge(sV.h, vts)
                sy.dma_start(scr3[0:1, OB_OFF[l] + half * 1024:OB_OFF[l] + half * 1024 + nk * nb],
                             tstag[0:1, OB_OFF[l] + half * 1024:OB_OFF[l] + half * 1024 + nk * nb]) \
                    .then_inc(sY.h, 16)
            syl.append(_bo)
        s["y_t16"][l] = rY()

        def _bi(sy, l=l, nb=nb):
            sy.dma_start(T163[0:16, T16OFF[l]:T16OFF[l] + nb],
                         scr3[0:1, OB_OFF[l]:OB_OFF[l] + 16 * nb]
                         .rearrange("a (c v) -> (a c) v", v=nb)).then_inc(sY.h, 16)
        syl.append(_bi)

    def rk_prefix(hi, l):
        s = st[hi]
        nb = NBL[l]
        a_cc = rA()

        def _cc(te, l=l, nb=nb, yt=s["y_t16"][l]):
            te.wait_ge(sY.h, yt)
            te.matmul(PS_C[0:nb, 0:1], T163[0:16, T16OFF[l]:T16OFF[l] + nb],
                      k.ones_col[0:16, :], start=True, stop=True,
                      skip_group_check=True).then_inc(sA.h, 1)
        tel.append(_cc)
        v_cc = rV()

        def _ccv(v, l=l, nb=nb, a=a_cc):
            v.wait_ge(sA.h, a)
            v.tensor_copy(ccol3[0:nb, l:l + 1], PS_C[0:nb, 0:1]).then_inc(sV.h, 1)
        vl.append(_ccv)
        a_st = rA()

        def _st(te, l=l, nb=nb, vcc=v_cc):
            te.wait_ge(sV.h, vcc)
            te.matmul(PS_C[0:1, 256:256 + nb], ccol3[0:nb, l:l + 1],
                      Unb[l][0:nb, 0:nb], start=True, stop=True,
                      skip_group_check=True).then_inc(sA.h, 1)
        tel.append(_st)
        v_st = rV()

        def _stv(v, l=l, nb=nb, a=a_st):
            v.wait_ge(sA.h, a)
            v.tensor_copy(strow3[0:1, l * 128:l * 128 + nb],
                          PS_C[0:1, 256:256 + nb]).then_inc(sV.h, 1)
        vl.append(_stv)
        a_ca = rA()

        def _ca(te, l=l, nb=nb, vst=v_st):
            te.matmul(PS_C[0:16, 512:512 + nb], L16[:],
                      T163[0:16, T16OFF[l]:T16OFF[l] + nb], start=True, stop=False,
                      skip_group_check=True)
            te.wait_ge(sV.h, vst)
            te.matmul(PS_C[0:16, 512:512 + nb], k.ones_rowf[0:1, 0:16],
                      strow3[0:1, l * 128:l * 128 + nb], start=False, stop=True,
                      skip_group_check=True).then_inc(sA.h, 1)
        tel.append(_ca)
        s["v_car"][l] = rV()

        def _cav(v, l=l, nb=nb, a=a_ca):
            v.wait_ge(sA.h, a)
            v.tensor_copy(carr3[0:16, l * 128:l * 128 + nb],
                          PS_C[0:16, 512:512 + nb]).then_inc(sV.h, 1)
        vl.append(_cav)

    def rk_rp(hi, l):
        s = st[hi]
        nb = NBL[l]
        halves = 2 if l == 0 else 1
        nk = 8 if l == 0 else 16
        w = nk * nb
        vrkm = [0] * halves
        for half in range(halves):
            a = rA()

            def _rp(te, l=l, half=half, nk=nk, nb=nb, w=w, a=a, vcar=s["v_car"][l],
                    vprev=vrkm[half - 1] if half else 0):
                te.wait_ge(sV.h, vcar)
                if vprev:
                    te.wait_ge(sV.h, vprev)
                for i in range((w + 511) // 512):
                    wi0 = i * 512
                    wi1 = min(w, wi0 + 512)
                    te.matmul(PS_C[:, wi0:wi1], L128[:],
                              Obuf[:, OB_OFF[l] + half * nk * nb + wi0:OB_OFF[l] + half * nk * nb + wi1],
                              start=True, stop=False, skip_group_check=True)
                for kc in range(half * nk, half * nk + nk):
                    mm = te.matmul(PS_C[:, (kc - half * nk) * nb:(kc - half * nk + 1) * nb],
                                   SELf[0:16, kc * 128:(kc + 1) * 128],
                                   carr3[0:16, l * 128:l * 128 + nb],
                                   start=False, stop=True, skip_group_check=True)
                mm.then_inc(sA.h, 1)
            tel.append(_rp)
            vrkm[half] = rV()

            def _rm(v, l=l, half=half, nk=nk, nb=nb, w=w, a=a):
                v.wait_ge(sA.h, a)
                v.tensor_tensor(rtmp[:, 0:w],
                                Obuf[:, OB_OFF[l] + half * nk * nb:OB_OFF[l] + (half + 1) * nk * nb],
                                PS_C[:, 0:w], A.mult).then_inc(sV.h, 1)
                ch(v, v.tensor_reduce(ranks[:, half * nk:(half + 1) * nk].unsqueeze(2),
                                      rtmp[:, 0:w].rearrange("p (c e) -> p c e", e=nb),
                                      AX.X, A.add))
            vl.append(_rm)
        s["v_rkm"][l] = vrkm[-1]
        s["v_scol"][l] = rV()

        def _cv(v, hi=hi, l=l, nb=nb):
            ch(v, v.tensor_copy(ranki[:], ranks[:]))
            ch(v, v.tensor_scalar(ranki[:], ranki[:], SHL[l], None, A.arith_shift_right))
            ch(v, v.tensor_copy(cvals[:, l, :], ranki[:]))
            ch(v, v.tensor_copy(cvb[:, l * 16:(l + 1) * 16], cvals[:, l, :]))
            ut3 = Utmp[:, 0:16 * nb].rearrange("p (c e) -> p c e", e=nb)
            v.tensor_tensor(ut3,
                            iotan[:, 0:nb].rearrange("p (o e) -> p o e", o=1)
                            .broadcast_to([128, 16, nb]),
                            cvals[:, l, :].unsqueeze(2).broadcast_to([128, 16, nb]),
                            A.is_equal)
            toff = TOFF2[l]
            ch(v, v.tensor_copy(Scol[:, :, toff:toff + nb], ut3))
            v.tensor_tensor(Scol[:, :, toff + 1:toff + nb],
                            Scol[:, :, toff + 1:toff + nb],
                            ut3[:, :, 0:nb - 1], A.add).then_inc(sV.h, 1)
        vl.append(_cv)

    def rk_crep(hi):
        s = st[hi]
        a_ct = rA()

        def _tr(te, vsc=s["v_scol"][2], vrk=s["v_rkm"][2]):
            te.wait_ge(sV.h, vsc)
            te.wait_ge(sV.h, vrk)
            te.matmul(PS_C[0:48, 0:128], cvb[:, 0:48], k.ident[:],
                      start=True, stop=True, skip_group_check=True).then_inc(sA.h, 1)
        tel.append(_tr)
        s["v_ctr"] = rV()

        def _trv(v, a=a_ct):
            v.wait_ge(sA.h, a)
            v.tensor_copy(ctr[0:48, 0:128], PS_C[0:48, 0:128]).then_inc(sV.h, 1)
        vl.append(_trv)
        y_cr = sY.n + 4 * DMAW[0]
        sY.n = y_cr

        def _cb(sy, vc=s["v_ctr"]):
            sy.wait_ge(sV.h, vc)
            sy.dma_start(cscr, ctr[0:48, 0:128]).then_inc(sY.h, 16)
            sy.dma_start(crep0[:], cview[0:1, :].broadcast_to([128, 2048])).then_inc(sY.h, 16)
            sy.dma_start(crep12[0:32, :], cview[2:3, :].broadcast_to([32, 2048])).then_inc(sY.h, 16)
            sy.dma_start(crep12[32:40, :], cview[1:2, :].broadcast_to([8, 2048])).then_inc(sY.h, 16)
        syl.append(_cb)
        s["v_us"] = rV()

        def _us(v, ycr=y_cr):
            v.wait_ge(sY.h, ycr)
            v.tensor_scalar(U0T[:], crep0[:], iotac[:, 0:1], None, A.is_equal)
            v.tensor_scalar(S2T[:], crep0[:], iotacm1[:, 0:1], None, A.is_equal)
            v.tensor_tensor(S2T[:], S2T[:], U0T[:], A.add)
            v.tensor_scalar(U01T[0:32, :], crep12[0:32, :], iotac[0:32, 0:1], None, A.is_equal)
            v.tensor_scalar(U01T[32:40, :], crep12[32:40, :], iotacm32[32:40, 0:1], None, A.is_equal)
            v.tensor_scalar(S01T[0:32, :], crep12[0:32, :], iotacm1[0:32, 0:1], None, A.is_equal)
            v.tensor_scalar(S01T[32:40, :], crep12[32:40, :], iotacm33[32:40, 0:1], None, A.is_equal)
            v.tensor_tensor(S01T[0:40, :], S01T[0:40, :], U01T[0:40, :], A.add).then_inc(sV.h, 1)
        vl.append(_us)

    v_vh_box = [0]

    def vproj():
        a_vlast = 0
        for tc in range(NC):
            a_vlast = rA()

            def _vp(te, tc=tc):
                te.wait_ge(s_wl.h, wl_base + 6 * DMAW[0])
                te.wait_ge(sC.h, c_ev[1][tc // 4])
                for ic in range(FC):
                    te.matmul(PS_A[:, tc * 128:(tc + 1) * 128],
                              xf[:, ic, tc * 128:(tc + 1) * 128],
                              wqkv[gi][:, 2, ic, 0:128],
                              start=(ic == 0), stop=False, skip_group_check=True)
                te.matmul(PS_A[:, tc * 128:(tc + 1) * 128], k.ones_rowf[:, 0:128],
                          bvr[:, g * 128:(g + 1) * 128],
                          start=False, stop=True, skip_group_check=True).then_inc(sA.h, 1)
            tel.append(_vp)
        v_vh_box[0] = rV()

        def _vev(v, a=a_vlast):
            v.wait_ge(sA.h, a)
            v.tensor_copy(vh2[:].rearrange("p c e -> p (c e)"), PS_A[:, :]).then_inc(sV.h, 1)
        vl.append(_vev)
        hist_v[g] = a_vlast

    def scores(hi, pumps):
        hb = hi * 64
        c_exp = [0] * 32
        gate0 = v_vh_box[0] if hi == 0 else st_pm_last[0]
        for idx in range(32):
            jc, qh = idx // 2, idx % 2
            slot = idx % 2
            a = rA()

            def _sm(te, hi=hi, jc=jc, qh=qh, slot=slot, idx=idx, hb=hb, a=a,
                    gate0=gate0, cprev=c_exp[idx - 2] if idx >= 2 else 0):
                if idx < 2:
                    te.wait_ge(sV.h, gate0)
                else:
                    te.wait_ge(sC.h, cprev)
                for i in range(2):
                    mm = te.matmul(PS_A[:, slot * 1024 + i * 512:slot * 1024 + (i + 1) * 512],
                                   kfh2[hb:hb + 64, jc * 128:(jc + 1) * 128],
                                   qfh2[hb:hb + 64, qh * 1024 + i * 512:qh * 1024 + (i + 1) * 512],
                                   start=True, stop=True, skip_group_check=True)
                mm.then_inc(sA.h, 1)
            tel.append(_sm)
            c_exp[idx] = rC()

            def _ex(sc, jc=jc, qh=qh, slot=slot, a=a):
                sc.wait_ge(sA.h, a)
                sc.activation(E[:, jc, qh * 1024:(qh + 1) * 1024],
                              PS_A[:, slot * 1024:(slot + 1) * 1024], AF.Exp,
                              scale=0.125).then_inc(sC.h, 1)
            scl.append(_ex)
            for p in pumps.get(idx, []):
                p()
        return c_exp[31]

    def t_phase(hi, pumps):
        s = st[hi]
        c_all = st_cexp[0]
        a_t2 = [0] * 4
        v_tut0 = [0] * 4
        a_rs0 = [0] * 4
        v_rs0e = [0] * 4
        a_t01 = [0] * 4
        v_tut12 = [0] * 4
        a_rs12 = [0] * 4
        v_rs12e = [0] * 4
        a_b0 = [0] * 4
        a_b12 = [0] * 4
        for nt in range(4):
            a_t2[nt] = rA()

            def _t2(te, nt=nt, c_all=c_all, vs0=s["v_scol"][0]):
                te.wait_ge(sC.h, c_all)
                te.wait_ge(sV.h, vs0)
                for kc in range(NC):
                    mm = te.matmul(PS_A[:, nt * 512:(nt + 1) * 512],
                                   Scol[:, kc, 0:128], E[:, kc, nt * 512:(nt + 1) * 512],
                                   start=(kc == 0), stop=(kc == NC - 1),
                                   skip_group_check=True)
                mm.then_inc(sA.h, 1)
            tel.append(_t2)
            v_tut0[nt] = rV()

            def _tu0(v, nt=nt, a=a_t2[nt], vus=s["v_us"]):
                v.wait_ge(sA.h, a)
                v.wait_ge(sV.h, vus)
                v.tensor_tensor(TUT[:, nt * 512:(nt + 1) * 512],
                                U0T[:, nt * 512:(nt + 1) * 512],
                                PS_A[:, nt * 512:(nt + 1) * 512], A.mult).then_inc(sV.h, 1)
            vl.append(_tu0)
        for p in pumps.get("t2", []):
            p()
        for nt in range(4):
            a_rs0[nt] = rA()

            def _r0(te, nt=nt, vt=v_tut0[nt]):
                te.wait_ge(sV.h, vt)
                te.matmul(PS_A[0:1, nt * 512:(nt + 1) * 512], k.ones_col[:, :],
                          TUT[:, nt * 512:(nt + 1) * 512], start=True, stop=True,
                          skip_group_check=True).then_inc(sA.h, 1)
            tel.append(_r0)
            v_rs0e[nt] = rV()

            def _r0v(v, nt=nt, a=a_rs0[nt]):
                v.wait_ge(sA.h, a)
                ch(v, v.reciprocal(PS_A[0:1, nt * 512:(nt + 1) * 512],
                                   PS_A[0:1, nt * 512:(nt + 1) * 512]))
                v.tensor_scalar(brow[0:1, nt * 512:(nt + 1) * 512],
                                PS_A[0:1, nt * 512:(nt + 1) * 512], 1.0 / 3.0, None,
                                A.mult).then_inc(sV.h, 1)
            vl.append(_r0v)
        for nt in range(4):
            a_t01[nt] = rA()

            def _t01(te, nt=nt, vr=v_rs0e[nt], vs1=s["v_scol"][1], vs2=s["v_scol"][2]):
                te.wait_ge(sV.h, vr)
                te.wait_ge(sV.h, vs1)
                te.wait_ge(sV.h, vs2)
                for kc in range(NC):
                    mm = te.matmul(PS_A[0:40, nt * 512:(nt + 1) * 512],
                                   Scol[:, kc, 128:168], E[:, kc, nt * 512:(nt + 1) * 512],
                                   start=(kc == 0), stop=(kc == NC - 1),
                                   skip_group_check=True)
                mm.then_inc(sA.h, 1)
            tel.append(_t01)
            v_tut12[nt] = rV()

            def _tu12(v, nt=nt, a=a_t01[nt]):
                v.wait_ge(sA.h, a)
                v.tensor_tensor(TUT[0:40, nt * 512:(nt + 1) * 512],
                                U01T[0:40, nt * 512:(nt + 1) * 512],
                                PS_A[0:40, nt * 512:(nt + 1) * 512], A.mult).then_inc(sV.h, 1)
            vl.append(_tu12)
        for p in pumps.get("t01", []):
            p()
        for nt in range(4):
            a_rs12[nt] = rA()

            def _r12(te, nt=nt, vt=v_tut12[nt]):
                te.wait_ge(sV.h, vt)
                te.matmul(PS_A[0:33, nt * 512:(nt + 1) * 512], e1col[0:40, :],
                          TUT[0:40, nt * 512:(nt + 1) * 512], start=True, stop=True,
                          skip_group_check=True).then_inc(sA.h, 1)
            tel.append(_r12)
            v_rs12e[nt] = rV()

            def _r12v(v, nt=nt, a=a_rs12[nt]):
                v.wait_ge(sA.h, a)
                ch(v, v.reciprocal(PS_A[0:1, nt * 512:(nt + 1) * 512],
                                   PS_A[0:1, nt * 512:(nt + 1) * 512]))
                ch(v, v.tensor_scalar(brow[0:1, 2048 + nt * 512:2048 + (nt + 1) * 512],
                                      PS_A[0:1, nt * 512:(nt + 1) * 512], 1.0 / 3.0, None,
                                      A.mult))
                ch(v, v.reciprocal(PS_A[32:33, nt * 512:(nt + 1) * 512],
                                   PS_A[32:33, nt * 512:(nt + 1) * 512]))
                v.tensor_scalar(brow[0:1, 4096 + nt * 512:4096 + (nt + 1) * 512],
                                PS_A[32:33, nt * 512:(nt + 1) * 512], 1.0 / 3.0, None,
                                A.mult).then_inc(sV.h, 1)
            vl.append(_r12v)
        for nt in range(4):
            a_b0[nt] = rA()

            def _b0(te, nt=nt, vr=v_rs12e[nt]):
                te.wait_ge(sV.h, vr)
                te.matmul(PS_A[:, nt * 512:(nt + 1) * 512], k.ones_row[0:1, :],
                          brow[0:1, nt * 512:(nt + 1) * 512], start=True, stop=True,
                          skip_group_check=True).then_inc(sA.h, 1)
            tel.append(_b0)
        v_b0e = rV()

        def _b0v(v, a=a_b0[3]):
            v.wait_ge(sA.h, a)
            v.tensor_copy(b0rep[:], PS_A[:, :]).then_inc(sV.h, 1)
        vl.append(_b0v)
        for nt in range(4):
            a_b12[nt] = rA()

            def _b12(te, nt=nt, vb=v_b0e):
                te.wait_ge(sV.h, vb)
                te.matmul(PS_A[0:32, nt * 512:(nt + 1) * 512], k.ones_row[0:1, 0:32],
                          brow[0:1, 2048 + nt * 512:2048 + (nt + 1) * 512],
                          start=True, stop=True, skip_group_check=True)
                te.matmul(PS_A[32:40, nt * 512:(nt + 1) * 512], k.ones_row[0:1, 0:8],
                          brow[0:1, 4096 + nt * 512:4096 + (nt + 1) * 512],
                          start=True, stop=True, skip_group_check=True).then_inc(sA.h, 1)
            tel.append(_b12)
        v_at = rV()

        def _atv(v, a=a_b12[3]):
            v.wait_ge(sA.h, a)
            v.tensor_copy(b12rep[0:40, :], PS_A[0:40, :])
            v.tensor_tensor(U0T[:], U0T[:], b0rep[:], A.mult)
            v.tensor_tensor(U01T[0:40, :], U01T[0:40, :], b12rep[0:40, :], A.mult) \
                .then_inc(sV.h, 1)
        vl.append(_atv)
        for p in pumps.get("b", []):
            p()
        return v_at

    def fav(hi, v_at, pumps):
        hb = hi * 64
        v_pm = [[0] * 2 for _ in range(16)]
        v_avep = [0, 0]
        for qh in range(2):
            for jc in range(16):
                slot = jc % 2
                a_f = rA()
                if jc >= 2:
                    wpm_f = v_pm[jc - 2][qh]
                elif qh == 1:
                    wpm_f = v_pm[14 + jc][0]
                else:
                    wpm_f = 0

                def _f(te, hi=hi, jc=jc, qh=qh, slot=slot, a=a_f, v_at=v_at,
                       wpm=wpm_f):
                    te.wait_ge(sV.h, v_at)
                    if wpm:
                        te.wait_ge(sV.h, wpm)
                    for i in range(2):
                        te.matmul(PS_A[:, slot * 1024 + i * 512:slot * 1024 + (i + 1) * 512],
                                  S2T[:, jc * 128:(jc + 1) * 128],
                                  U0T[:, qh * 1024 + i * 512:qh * 1024 + (i + 1) * 512],
                                  start=True, stop=False, skip_group_check=True)
                        mm = te.matmul(PS_A[:, slot * 1024 + i * 512:slot * 1024 + (i + 1) * 512],
                                       S01T[0:40, jc * 128:(jc + 1) * 128],
                                       U01T[0:40, qh * 1024 + i * 512:qh * 1024 + (i + 1) * 512],
                                       start=False, stop=True, skip_group_check=True)
                    mm.then_inc(sA.h, 1)
                tel.append(_f)
                v_pm[jc][qh] = rV()

                def _pm(v, jc=jc, qh=qh, slot=slot, a=a_f):
                    v.wait_ge(sA.h, a)
                    v.tensor_tensor(E[:, jc, qh * 1024:(qh + 1) * 1024],
                                    E[:, jc, qh * 1024:(qh + 1) * 1024],
                                    PS_A[:, slot * 1024:(slot + 1) * 1024],
                                    A.mult).then_inc(sV.h, 1)
                vl.append(_pm)
                a_av = rA()

                def _av(te, hi=hi, jc=jc, qh=qh, hb=hb, wpm=v_pm[jc][qh],
                        wep=(v_avep[0] if qh else prev["avep"]) if jc == 0 else 0):
                    te.wait_ge(sV.h, wpm)
                    if jc == 0 and wep:
                        te.wait_ge(sV.h, wep)
                    for i in range(2):
                        mm = te.matmul(PS_C[0:64, 1024 + i * 512:1024 + (i + 1) * 512],
                                       vh2[:, jc, hb:hb + 64],
                                       E[:, jc, qh * 1024 + i * 512:qh * 1024 + (i + 1) * 512],
                                       start=(jc == 0), stop=(jc == 15),
                                       skip_group_check=True)
                    mm.then_inc(sA.h, 1)
                tel.append(_av)
                a_av_last = a_av
                for p in pumps.get((qh, jc), []):
                    p()
            v_avep[qh] = rV()
            oDw = prev["oD"] if (hi == 0 and qh == 0) else 0

            def _ep(v, hi=hi, qh=qh, a=a_av_last, oDw=oDw):
                v.wait_ge(sA.h, a)
                if oDw:
                    v.wait_ge(sY.h, oDw)
                if hi == 0:
                    v.tensor_copy(outs[0:64, qh * 1024:(qh + 1) * 1024],
                                  PS_C[0:64, 1024:2048]).then_inc(sV.h, 1)
                else:
                    v.tensor_copy(kfh2[0:64, qh * 1024:(qh + 1) * 1024],
                                  PS_C[0:64, 1024:2048]).then_inc(sV.h, 1)
            vl.append(_ep)
        return v_pm[15][1], v_avep[1]

    st_pm_last = [0]
    st_cexp = [0]
    rk_probe(0, 0)
    vproj()
    sc_pumps0 = {
        2: [lambda: rk_counts(0, 0)],
        6: [lambda: rk_prefix(0, 0)],
        8: [lambda: rk_rp(0, 0)],
        10: [lambda: rk_probe(0, 1)],
        12: [lambda: rk_counts(0, 1)],
        14: [lambda: rk_prefix(0, 1)],
        16: [lambda: rk_rp(0, 1)],
        18: [lambda: rk_probe(0, 2)],
        20: [lambda: rk_counts(0, 2)],
        22: [lambda: rk_prefix(0, 2)],
        24: [lambda: rk_rp(0, 2)],
        26: [lambda: rk_crep(0)],
    }
    st_cexp[0] = scores(0, sc_pumps0)
    t_pumps0 = {
        "t2": [lambda: rk_probe(1, 0)],
        "t01": [lambda: rk_counts(1, 0)],
        "b": [lambda: rk_prefix(1, 0)],
    }
    v_at0 = t_phase(0, t_pumps0)
    f_pumps0 = {
        (0, 1): [lambda: rk_rp(1, 0)],
        (0, 4): [lambda: rk_probe(1, 1)],
        (0, 6): [lambda: rk_counts(1, 1)],
        (0, 8): [lambda: rk_prefix(1, 1)],
        (0, 10): [lambda: rk_rp(1, 1)],
        (0, 12): [lambda: rk_probe(1, 2)],
        (0, 14): [lambda: rk_counts(1, 2)],
        (1, 1): [lambda: rk_prefix(1, 2)],
        (1, 3): [lambda: rk_rp(1, 2)],
    }
    st_pm_last[0], avep0 = fav(0, v_at0, f_pumps0)
    prev["avep"] = avep0
    st_cexp[0] = scores(1, {0: [lambda: rk_crep(1)]})
    v_at1 = t_phase(1, {})
    pm1, avep1 = fav(1, v_at1, {})

    a_sh = [0] * 4
    for nt in range(4):
        a_sh[nt] = rA()

        def _sh(te, nt=nt, wpm=pm1, wav=avep1):
            te.wait_ge(sV.h, wpm)
            te.wait_ge(sV.h, wav)
            te.matmul(PS_A[:, nt * 512:(nt + 1) * 512], shid[:],
                      kfh2[0:64, nt * 512:(nt + 1) * 512], start=True, stop=True,
                      skip_group_check=True).then_inc(sA.h, 1)
        tel.append(_sh)
    v_sh = rV()

    def _shv(v, a=a_sh[3]):
        v.wait_ge(sA.h, a)
        for nt in range(4):
            cp = v.tensor_copy(outs[64:128, nt * 512:(nt + 1) * 512],
                               PS_A[64:128, nt * 512:(nt + 1) * 512])
        cp.then_inc(sV.h, 1)
    vl.append(_shv)
    y_oD = rY()

    def _oD(sy, vsh=v_sh):
        sy.wait_ge(sV.h, vsh)
        sy.dma_start(outTd[:, g, :], outs[:]).then_inc(sY.h, 16)
    syl.append(_oD)

    prev["pm"] = pm1
    prev["sh"] = v_sh
    prev["avep"] = avep1
    prev["rk"] = st[1]["v_scol"][2]
    prev["ctr"] = st[1]["v_ctr"]
    prev["oD"] = y_oD
    hist_pa[g] = st[1]["a_pr_last"]
    B["final_out_v"][0] = y_oD

    with nc.Block() as blk:
        @blk.gpsimd
        def _(gg):
            for f in gl:
                f(gg)

        @blk.tensor
        def _(te):
            sid = nc.enter_named_scope(f"P2_{g}", False)[0]
            for f in tel:
                f(te)
            nc.leave_named_scope(f"P2_{g}", sid, False)

        @blk.vector
        def _(v):
            for f in vl:
                f(v)

        @blk.scalar
        def _(sc):
            for f in scl:
                f(sc)

        @blk.sync
        def _(sy):
            for f in syl:
                f(sy)


TOFF2 = [0, 160, 128]
T16OFF = [0, 128, 136]
OB_OFF = [0, 2048, 2176]


def emit_l0attn2(k, xf, x_in_dram, xr_out_dram, tag="a0", npairs=8):
    nc = k.nc
    xrv_in = x_in_dram.rearrange("(c p) d -> p c d", p=128)
    xrv_out = xr_out_dram.rearrange("(c p) d -> p c d", p=128)
    scr3 = nc.dram_tensor(tag + "_scr3", [1, 2688], BF16, kind="Internal").ap()
    cscr = nc.dram_tensor(tag + "_cscr", [48, 128], BF16, kind="Internal").ap()
    outTd = nc.dram_tensor(tag + "_outTd", [128, FC, T], BF16, kind="Internal").ap()
    cview = cscr.rearrange("(l c) p -> l (c p)", l=3)
    xfev = k.sem("xfc_ev")
    xf_tgt = xfev.n

    with ExitStack() as es:
        L128 = es.enter_context(nc.sbuf_tensor(tag + "_L128", [128, 128], BF16))
        L16 = es.enter_context(nc.sbuf_tensor(tag + "_L16", [16, 16], BF16))
        SELf = es.enter_context(nc.sbuf_tensor(tag + "_SELf", [16, T], F32))
        Unb = [es.enter_context(nc.sbuf_tensor(f"{tag}_Unb{l}", [NBL[l], NBL[l]], F32))
               for l in range(3)]
        iotan = es.enter_context(nc.sbuf_tensor(tag + "_iotan", [128, 128], F32))
        iota64r = es.enter_context(nc.sbuf_tensor(tag + "_iota64r", [128, 64], F32))
        shid = es.enter_context(nc.sbuf_tensor(tag + "_shid", [64, 128], BF16))
        iotac = es.enter_context(nc.sbuf_tensor(tag + "_iotac", [128, 1], F32))
        iotacm1 = es.enter_context(nc.sbuf_tensor(tag + "_iotacm1", [128, 1], F32))
        iotacm32 = es.enter_context(nc.sbuf_tensor(tag + "_iotacm32", [128, 1], F32))
        iotacm33 = es.enter_context(nc.sbuf_tensor(tag + "_iotacm33", [128, 1], F32))
        e1f = es.enter_context(nc.sbuf_tensor(tag + "_e1f", [40, 33], F32))
        e1col = es.enter_context(nc.sbuf_tensor(tag + "_e1col", [40, 33], BF16))
        bqc = es.enter_context(nc.sbuf_tensor(tag + "_bqcs", [128, 8], F32))
        bkc = es.enter_context(nc.sbuf_tensor(tag + "_bkcs", [128, 8], F32))
        bvr = es.enter_context(nc.sbuf_tensor(tag + "_bvrs", [1, D], F32))

        s_ct = k.sem(tag + "_ct2")
        chc = k.chainer(tag + "_chc2")
        with nc.Block() as blk:
            @blk.sync
            def _(sy):
                for _dst, _src in ((bqc, "a0_bqc"), (bkc, "a0_bkc"), (bvr, "a0_bvr")):
                    sy.dma_start(_dst[:], k.inp[_src]).then_inc(s_ct.h, 16)
                    s_ct.n += DMAW[0]
            @blk.gpsimd
            def _(g):
                g.iota(iotan[:], pattern=[[1, 128]], base=0, channel_multiplier=-1,
                       allow_small_or_imprecise_dtypes=True)
                g.iota(iota64r[:], pattern=[[1, 64]], base=0, channel_multiplier=0,
                       allow_small_or_imprecise_dtypes=True)
                g.iota(iotac[:], pattern=[[0, 1]], base=0, channel_multiplier=1,
                       allow_small_or_imprecise_dtypes=True)
                g.iota(iotacm1[:], pattern=[[0, 1]], base=-1, channel_multiplier=1,
                       allow_small_or_imprecise_dtypes=True)
                g.iota(iotacm32[:], pattern=[[0, 1]], base=-32, channel_multiplier=1,
                       allow_small_or_imprecise_dtypes=True)
                g.iota(iotacm33[:], pattern=[[0, 1]], base=-33, channel_multiplier=1,
                       allow_small_or_imprecise_dtypes=True)
                s_ct.inc(g.memset(e1f[0:40, :], 0.0))
                s_ct.inc(g.iota(SELf[:], pattern=[[1, 16], [0, 128]], base=0,
                                channel_multiplier=-1, allow_small_or_imprecise_dtypes=True))
            @blk.vector
            def _(v):
                s_ct.wait(v)
                s_ct.inc(v.tensor_scalar(SELf[:], SELf[:], 0.0, None, A.is_equal))
                chc(v, v.tensor_scalar(L128[:], iotan[:], 0.0, None, A.is_gt))
                chc(v, v.tensor_scalar(L16[:], iotan[0:16, 0:16], 0.0, None, A.is_gt))
                for l in range(3):
                    nb = NBL[l]
                    chc(v, v.tensor_scalar(Unb[l][:], iotan[0:nb, 0:nb], 0.0, None, A.is_gt))
                chc(v, v.tensor_scalar(iotan[0:64, :], iotan[0:64, :], 64.0, None, A.subtract))
                s_ct.inc(v.tensor_scalar(shid[:], iotan[0:64, :], 0.0, None, A.is_equal))
                chc(v, v.tensor_scalar(e1f[0:40, 32:33], iotac[0:40, 0:1], 31.5, None,
                                       A.is_gt))
                chc(v, v.tensor_scalar(e1f[0:40, 0:1], e1f[0:40, 32:33], -1.0, 1.0,
                                       A.mult, A.add))
                s_ct.inc(v.tensor_copy(e1col[0:40, :], e1f[0:40, :]))
        with nc.Block() as blk:
            @blk.gpsimd
            def _(g):
                g.wait_ge(s_ct.h, s_ct.n)
                s_ct.inc(g.iota(iotan[:], pattern=[[1, 128]], base=0, channel_multiplier=0,
                                allow_small_or_imprecise_dtypes=True))

        with ExitStack() as hs:
            E = hs.enter_context(nc.sbuf_tensor(tag + "_E", [128, NC, T], BF16))
            qfh2 = hs.enter_context(nc.sbuf_tensor(tag + "_qfh2", [128, T], BF16))
            kfh2 = hs.enter_context(nc.sbuf_tensor(tag + "_kfh2", [128, T], BF16))
            vh2 = hs.enter_context(nc.sbuf_tensor(tag + "_vh2", [128, NC, 128], BF16))
            outs = hs.enter_context(nc.sbuf_tensor(tag + "_outs", [128, T], BF16))
            wqkv = [hs.enter_context(nc.sbuf_tensor(f"{tag}_wqkv{i}", [128, 3, 8, 128], BF16))
                    for i in range(2)]
            probeb = [hs.enter_context(nc.sbuf_tensor(tag + "_prb0", [128, 3, 128], BF16))] * 2
            Obuf = hs.enter_context(nc.sbuf_tensor(tag + "_Obuf", [128, 2688], BF16))
            ohb = hs.enter_context(nc.sbuf_tensor(tag + "_ohb", [128, 1024], BF16))
            pmax = hs.enter_context(nc.sbuf_tensor(tag + "_pmax", [128, 16], F32))
            regf = hs.enter_context(nc.sbuf_tensor(tag + "_regf", [128, 16], F32))
            regi = hs.enter_context(nc.sbuf_tensor(tag + "_regi", [128, 16], I32))
            bkt = hs.enter_context(nc.sbuf_tensor(tag + "_bkt", [128, 16], F32))
            ranks = hs.enter_context(nc.sbuf_tensor(tag + "_ranks", [128, 16], F32))
            ranki = hs.enter_context(nc.sbuf_tensor(tag + "_ranki", [128, 16], I32))
            cvals = hs.enter_context(nc.sbuf_tensor(tag + "_cvals", [128, 3, 16], F32))
            cvb = hs.enter_context(nc.sbuf_tensor(tag + "_cvb", [128, 48], BF16))
            ctr = hs.enter_context(nc.sbuf_tensor(tag + "_ctr", [48, 128], BF16))
            tstag = hs.enter_context(nc.sbuf_tensor(tag + "_tstag", [1, 2688], BF16))
            T163 = hs.enter_context(nc.sbuf_tensor(tag + "_T163", [16, 168], BF16))
            ccol3 = hs.enter_context(nc.sbuf_tensor(tag + "_ccol3", [128, 3], F32))
            strow3 = hs.enter_context(nc.sbuf_tensor(tag + "_strow3", [1, 384], F32))
            carr3 = hs.enter_context(nc.sbuf_tensor(tag + "_carr3", [16, 384], F32))
            rtmp = hs.enter_context(nc.sbuf_tensor(tag + "_rtmp", [128, 1024], F32))
            Scol = hs.enter_context(nc.sbuf_tensor(tag + "_Scol", [128, NC, NT168], BF16))
            crep0 = hs.enter_context(nc.sbuf_tensor(tag + "_crep0", [128, T], BF16))
            crep12 = hs.enter_context(nc.sbuf_tensor(tag + "_crep12", [40, T], BF16))
            U0T = hs.enter_context(nc.sbuf_tensor(tag + "_U0T", [128, T], BF16))
            S2T = hs.enter_context(nc.sbuf_tensor(tag + "_S2T", [128, T], BF16))
            U01T = hs.enter_context(nc.sbuf_tensor(tag + "_U01T", [40, T], BF16))
            S01T = hs.enter_context(nc.sbuf_tensor(tag + "_S01T", [40, T], BF16))
            TUT = hs.enter_context(nc.sbuf_tensor(tag + "_TUT", [128, T], BF16))
            brow = hs.enter_context(nc.sbuf_tensor(tag + "_brow", [1, 3 * T], BF16))
            PS_A = hs.enter_context(nc.psum_tensor(tag + "_psA", [128, T], F32))
            PS_C = hs.enter_context(nc.psum_tensor(tag + "_psC", [128, T], F32))
            b0rep = crep0
            b12rep = crep12
            Utmp = TUT

            sA = k.sem(tag + "_2A")
            sV = k.sem(tag + "_2V")
            sC = k.sem(tag + "_2C")
            sY = k.sem(tag + "_2Y")
            s_wl = k.sem(tag + "_2wl")
            s_pb = k.sem(tag + "_2pb")
            ch = k.chainer(tag + "_2ch")
            Ofl = Obuf

            prev = {"pm": 0, "sh": 0, "avep": 0, "rk": 0, "ctr": 0, "oD": 0}
            hist_pa = {}
            hist_v = {}
            final_out_v = [0]

            for g in range(npairs):
                _emit_l0a2_pair(k, tag, g, locals())

        with ExitStack() as ts:
            xb = ts.enter_context(nc.sbuf_tensor(tag + "_xb", [128, NC, D], BF16))
            wob = ts.enter_context(nc.sbuf_tensor(tag + "_wob", [128, 2, 8, 512], BF16))
            outTl = ts.enter_context(nc.sbuf_tensor(tag + "_outTl", [128, 2, FC, 128], BF16))
            bor = ts.enter_context(nc.sbuf_tensor(tag + "_bors", [1, D], F32))
            vch = ts.enter_context(nc.sbuf_tensor(tag + "_vch", [128, 2, 1024], F32))
            xrsl = ts.enter_context(nc.sbuf_tensor(tag + "_xrsl", [128, 2, 1024], F32))
            junkv = ts.enter_context(nc.sbuf_tensor(tag + "_junkv", [128, 1024], F32))
            junkc = ts.enter_context(nc.sbuf_tensor(tag + "_junkc", [128, 1], F32))
            lst = ts.enter_context(nc.sbuf_tensor(tag + "_lst", [128, 2, 8], F32))
            ps_w = ts.enter_context(nc.psum_tensor(tag + "_psw", [128, T], F32))
            s_wlo = k.sem(tag + "_wlo")
            s_otl = k.sem(tag + "_otl")
            s_mm = k.sem(tag + "_mmo")
            s_v = k.sem(tag + "_vo")
            sYt = k.sem(tag + "_2Y")
            out_tgt = sYt.n
            with nc.Block() as blk:
                @blk.gpsimd
                def _(g):
                    wvw = k.inp["a0_wo"].rearrange("(ic p) o -> p ic o", p=128)
                    for half in range(2):
                        s_wlo.inc(g.dma_start(wob[:, half, :, :],
                                              wvw[:, :, half * 512:(half + 1) * 512]), 16)
                    g.wait_ge(sYt.h, out_tgt)
                    for c in range(NC):
                        if c >= 2:
                            g.wait_ge(s_mm.h, 2 * (c - 1))
                        _d = g.dma_start(outTl[:, c % 2, :, :],
                                         outTd[:, :, c * 128:(c + 1) * 128])
                        _d.then_inc(s_otl.h, 16)
                        s_otl.n += DMAW[0]
                @blk.sync
                def _(sy):
                    s_bor = k.sem(tag + "_bor")
                    s_bor.inc(sy.dma_start(bor[:], k.inp["a0_bor"]), 16)
                @blk.tensor
                def _(te):
                    te.wait_ge(s_wlo.h, 2 * DMAW[0])
                    te.wait_ge(k.sem(tag + "_bor").h, DMAW[0])
                    for c in range(NC):
                        if c >= 2:
                            te.wait_ge(s_v.h, 2 * (c - 1))
                        te.wait_ge(s_otl.h, DMAW[0] * (c + 1))
                        for et in range(2):
                            bank = ps_w[:, (2 * (c % 2) + et) * 512:(2 * (c % 2) + et + 1) * 512]
                            for fc in range(FC):
                                te.matmul(bank, outTl[:, c % 2, fc, :],
                                          wob[:, et, fc, :],
                                          start=(fc == 0), stop=False,
                                          skip_group_check=True)
                            mm = te.matmul(bank, k.ones_rowf[:, 0:128],
                                           bor[:, et * 512:(et + 1) * 512],
                                           start=False, stop=True, skip_group_check=True)
                            s_mm.inc(mm)
                emit_tail_bodies(k, tag + "_t", blk, vch=vch, xrsl=xrsl, junkv=junkv,
                                 junkc=junkc, lst=lst, xb=xb, xrv_in=xrv_in,
                                 xrv_out=xrv_out,
                                 ps_slice=lambda c, et: ps_w[:, (2 * (c % 2) + et) * 512:(2 * (c % 2) + et + 1) * 512],
                                 s_mm=s_mm, b_mm=0, mm_target=lambda c, et: 2 * c + et + 1,
                                 s_v=s_v)
            emit_xf(k, xb, xf, tag + "_xf")


def host_prep_l0attn(inputs):
    pr = np.ascontiguousarray(inputs["L0_attn_probes"].transpose(2, 0, 1, 3).reshape(64, 3 * 16 * 64))
    return {
        "a0_wq": inputs["L0_attn_Wq"], "a0_wk": inputs["L0_attn_Wk"],
        "a0_wv": inputs["L0_attn_Wv"], "a0_wo": inputs["L0_attn_Wo"],
        "a0_bqc": np.ascontiguousarray(inputs["L0_attn_bq"].reshape(8, 128).T),
        "a0_bkc": np.ascontiguousarray(inputs["L0_attn_bk"].reshape(8, 128).T),
        "a0_bvr": inputs["L0_attn_bv"].reshape(1, 1024),
        "a0_bor": inputs["L0_attn_bo"].reshape(1, 1024),
        "a0_probes": np.ascontiguousarray(pr),
    }


def build_l0attn_standalone(race=False, npairs=8):
    nc = bass.Bass("TRN2", target_bir_lowering=False, debug=False,
                   detect_race_conditions=race)
    k = K(nc)
    x = nc.dram_tensor("x", [T, D], F32, kind="ExternalInput").ap()
    y = nc.dram_tensor("y", [T, D], F32, kind="ExternalOutput").ap()
    declare_inputs(k, L0ATTN_SPECS)
    make_consts(k)
    xf = nc.alloc_sbuf_tensor("xf", [128, FC, T], BF16)
    s_ld = k.sem("xld")
    s_cv = k.sem("xcv")
    with ExitStack() as es:
        xr = es.enter_context(nc.sbuf_tensor("xr", [128, NC, D], F32))
        xb = es.enter_context(nc.sbuf_tensor("xb", [128, NC, D], BF16))
        with nc.Block() as blk:
            @blk.sync
            def _(sy):
                s_ld.inc(sy.dma_start(xr[:], x.rearrange("(c p) d -> p c d", p=128)), 16)
            @blk.vector
            def _(v):
                s_ld.wait(v)
                for c in range(NC):
                    s_cv.inc(v.tensor_copy(xb[:, c, :], xr[:, c, :]))
        emit_xf(k, xb, xf, "xf0")
    emit_l0attn2(k, xf, x, y, tag="a0", npairs=npairs)
    return nc


def build_l0attn_debug():
    nc = bass.Bass("TRN2", target_bir_lowering=False, debug=False,
                   detect_race_conditions=False)
    k = K(nc)
    x = nc.dram_tensor("x", [T, D], F32, kind="ExternalInput").ap()
    y = nc.dram_tensor("y", [T, D], F32, kind="ExternalOutput").ap()
    dbg = nc.dram_tensor("dbg", [128, 4096], F32, kind="ExternalOutput").ap()
    declare_inputs(k, L0ATTN_SPECS)
    make_consts(k)
    xf = nc.alloc_sbuf_tensor("xf", [128, FC, T], BF16)
    s_ld = k.sem("xld")
    s_cv = k.sem("xcv")
    with ExitStack() as es:
        xr = es.enter_context(nc.sbuf_tensor("xr", [128, NC, D], F32))
        xb = es.enter_context(nc.sbuf_tensor("xb", [128, NC, D], BF16))
        with nc.Block() as blk:
            @blk.sync
            def _(sy):
                s_ld.inc(sy.dma_start(xr[:], x.rearrange("(c p) d -> p c d", p=128)), 16)
            @blk.vector
            def _(v):
                s_ld.wait(v)
                for c in range(NC):
                    s_cv.inc(v.tensor_copy(xb[:, c, :], xr[:, c, :]))
        emit_xf(k, xb, xf, "xf0")
    emit_l0attn(k, xf, x, y, tag="a0", dbg=dbg)
    return nc


def build_fused():
    nc = bass.Bass("TRN2", target_bir_lowering=False, debug=False,
                   detect_race_conditions=False)
    k = K(nc)
    x = nc.dram_tensor("x", [T, D], F32, kind="ExternalInput").ap()
    y = nc.dram_tensor("y", [T, D], F32, kind="ExternalOutput").ap()
    xr1 = nc.dram_tensor("fz_xr1", [T, D], F32, kind="Internal").ap()
    xr2 = nc.dram_tensor("fz_xr2", [T, D], F32, kind="Internal").ap()
    xr3 = nc.dram_tensor("fz_xr3", [T, D], F32, kind="Internal").ap()
    declare_inputs(k, L0ATTN_SPECS)
    declare_inputs(k, L0FFN_SPECS)
    declare_inputs(k, L1ATTN_SPECS)
    declare_inputs(k, L1FFN_SPECS)
    make_consts(k)
    xf = nc.alloc_sbuf_tensor("xf", [128, FC, T], BF16)
    s_ld = k.sem("xld")
    s_cv = k.sem("xcv")
    with ExitStack() as es:
        xr = es.enter_context(nc.sbuf_tensor("xr", [128, NC, D], F32))
        xb0 = es.enter_context(nc.sbuf_tensor("xb0", [128, NC, D], BF16))
        with nc.Block() as blk:
            @blk.sync
            def _(sy):
                s_ld.inc(sy.dma_start(xr[:], x.rearrange("(c p) d -> p c d", p=128)), 16)
            @blk.vector
            def _(v):
                s_ld.wait(v)
                for c in range(NC):
                    s_cv.inc(v.tensor_copy(xb0[:, c, :], xr[:, c, :]))
        emit_xf(k, xb0, xf, "xf0")
    with nc.named_scope("P_l0attn"):
        emit_l0attn2(k, xf, x, xr1, tag="a0")
    with nc.named_scope("P_l0ffn"):
        emit_l0ffn(k, xf, xr1, xr2, tag="f0")
    with nc.named_scope("P_l1attn"):
        emit_l1attn(k, xf, xr2, xr3, tag="a1")
    with nc.named_scope("P_l1ffn"):
        emit_l1ffn(k, xr3, xf, y)
    return nc


def kernel(**inputs):
    x = np.ascontiguousarray(inputs["L0_attn_x"])
    pp = {}
    pp.update(host_prep_l0attn(inputs))
    pp.update(host_prep_l0ffn(inputs))
    pp.update(host_prep_l1attn(inputs))
    pp.update(host_prep_l1ffn(inputs))
    nc = _get_prog("fused", build_fused)
    in_maps = []
    for c in range(8):
        m = {"x": np.ascontiguousarray(x[c])}
        m.update(pp)
        in_maps.append(m)
    res = run_bass_kernel_spmd(nc, in_maps, core_ids=list(range(8)))
    return np.stack([res.results[c]["y"] for c in range(8)], axis=0)
